# version 12
# baseline (speedup 1.0000x reference)
import contextlib
import numpy as np
import ml_dtypes
import concourse.bass as bass
import concourse.mybir as mybir
from concourse.bass_utils import run_bass_kernel_spmd

F32 = mybir.dt.float32
BF16 = mybir.dt.bfloat16
AF = mybir.ActivationFunctionType
ALU = mybir.AluOpType
AX = mybir.AxisListType

S, D, H, HD, WB, NB, FFN, NI = 2048, 1024, 16, 64, 256, 8, 4096, 2816
NT = S // 128
MID = [0.25, 0.75, 1.25, 1.75, 2.5, 3.5, 5.0]
ENGS = ("sync", "gpsimd", "tensor", "vector", "scalar")
SAFE_DVE = True


def build(dump_scratch=False):
    nc = bass.Bass("TRN2", target_bir_lowering=False)

    def din(name, shape):
        return nc.dram_tensor(name, shape, F32, kind="ExternalInput").ap()

    x_in = din("x", [S, D])
    Wq_d, Wk_d, Wv_d, Wo_d = (din(n, [D, D]) for n in ("Wq", "Wk", "Wv", "Wo"))
    bq_d, bk_d, bv_d, bo_d = (din(n, [D]) for n in ("bq", "bk", "bv", "bo"))
    g1_d, b1_d, g2_d, b2_d = (din(n, [D]) for n in ("g1", "b1", "g2", "b2"))
    W1_d = din("W1", [D, FFN]); c1_d = din("c1", [FFN])
    W2_d = din("W2", [FFN, D]); c2_d = din("c2", [D])
    wggu_sl = din("wggu_sl", [704, D])
    wd_sl = din("wd_sl", [128, NI])
    masks_d = [nc.dram_tensor(f"mask{j}", [128, 256], BF16, kind="ExternalInput").ap()
               for j in range(4)]
    ones_d = nc.dram_tensor("ones_row", [1, 128], BF16, kind="ExternalInput").ap()
    out_d = nc.dram_tensor("out", [S, D], F32, kind="ExternalOutput").ap()

    sk = "ExternalOutput" if dump_scratch else "Internal"
    u_dr = nc.dram_tensor("u_dr", [S, D], BF16, kind=sk).ap()
    hf_dr = nc.dram_tensor("hf_dr", [S, D], F32, kind=sk).ap()
    hb_dr = nc.dram_tensor("hb_dr", [S, D], BF16, kind=sk).ap()
    x1_dr = nc.dram_tensor("x1_dr", [S, D], F32, kind=sk).ap()
    xq_dr = nc.dram_tensor("xq_dr", [S, D], BF16, kind=sk).ap()
    hq_dr = nc.dram_tensor("hq_dr", [S, NI], BF16, kind=sk).ap()
    ho_dbg = nc.dram_tensor("ho_dbg", [D, S], BF16, kind=sk).ap()
    q_dbg = nc.dram_tensor("q_dbg", [D, 1024], BF16, kind=sk).ap()
    k_dbg = nc.dram_tensor("k_dbg", [D, 1280], BF16, kind=sk).ap()
    v_dbg = nc.dram_tensor("v_dbg", [1280, 1040], BF16, kind=sk).ap()
    ex_dbg = nc.dram_tensor("ex_dbg", [128, 1536], BF16, kind=sk).ap()
    pv_dbg = nc.dram_tensor("pv_dbg", [64, 256], F32, kind=sk).ap()
    rr_dbg = nc.dram_tensor("rr_dbg", [1, 256], F32, kind=sk).ap()
    rep_dbg = nc.dram_tensor("rep_dbg", [64, 256], F32, kind=sk).ap()
    w2b_dr = nc.dram_tensor("w2b_dr", [FFN, D], BF16, kind="Internal").ap()
    cb1_dr = nc.dram_tensor("cb1_dr", [704, D // 2], F32, kind="Internal").ap()
    cb2_dr = nc.dram_tensor("cb2_dr", [128, NI // 2], F32, kind="Internal").ap()
    gg1_dr = nc.dram_tensor("gg1_dr", [2 * NI, D // 2], F32, kind="Internal").ap()
    gg2_dr = nc.dram_tensor("gg2_dr", [D, NI // 2], F32, kind="Internal").ap()
    gg1_dbg = nc.dram_tensor("gg1_dbg", [2 * NI, D // 2], F32, kind=sk).ap()
    gg2_dbg = nc.dram_tensor("gg2_dbg", [D, NI // 2], F32, kind=sk).ap()
    wgq_dr = gg1_dr.bitcast(BF16)[0:NI, :]
    wuq_dr = gg1_dr.bitcast(BF16)[NI:2 * NI, :]
    wdq_dr = gg2_dr.bitcast(BF16)

    es = contextlib.ExitStack()
    with es:
        T = lambda name, shape, dt=F32: es.enter_context(nc.sbuf_tensor(name, shape, dt))

        pool = [T(f"pool{i}", [128, 2048], BF16) for i in range(20)]
        wp = [T(f"wp{i}", [128, 1024], BF16) for i in range(16)]
        vt = [T(f"vt{i}", [128, 16 * 65], BF16) for i in range(10)]
        exb = [T(f"exb{i}", [128, 1536], BF16) for i in range(2)]
        xr = [T(f"xr{i}", [128, 1024], F32) for i in range(4)]
        uS = T("uS", [128, 1024], F32)
        tmpS = T("tmpS", [128, 1024], F32)
        xnS = T("xnS", [128, 512], F32)
        absS = T("absS", [128, 512], F32)
        sgS = T("sgS", [128, 512], F32)
        pvS = T("pvS", [64, 256], F32)
        Ib = [T(f"Ib{k}", [128, 512], BF16) for k in range(2)]
        sgnB = T("sgnB", [128, 512], BF16)
        bmS = T("bmS", [128, 64], F32)
        sclS = T("sclS", [128, 64], F32)
        rsS = T("rsS", [128, 64], F32)
        sclB = T("sclB", [128, 64], BF16)
        stg = [T(f"stg{i}", [128, 1024], BF16) for i in range(2)]
        hqstg = [T(f"hqstg{i}", [128, 512], BF16) for i in range(2)]
        repA = T("repA", [128, 1024], F32)
        repB = T("repB", [128, 1024], F32)
        lnS = T("lnS", [128, 8], F32)
        mu1 = T("mu1", [128, NT], F32)
        rstd1 = T("rstd1", [128, NT], F32)
        nmr1 = T("nmr1", [128, NT], F32)
        bq8 = T("bq8", [128, 8], F32)
        bkp = T("bkp", [128, 8], F32)
        c1p = T("c1p", [128, 32], F32)
        brow = T("brow", [1, 1024], BF16)
        onesb = T("onesb", [1, 128], BF16)
        rrow = T("rrow", [1, 256], F32)
        rrowb = T("rrowb", [1, 256], BF16)
        maskT = [T(f"maskT{j}", [128, 256], BF16) for j in range(4)]
        wfp = [T(f"wfp{i}", [128, 1024], F32) for i in range(2)]
        wqs = [T(f"wqs{i}", [128, 1024], BF16) for i in range(2)]

        PA = [es.enter_context(nc.psum_tensor(f"PA{i}", [128, 1536], F32)) for i in range(2)]
        PD = es.enter_context(nc.psum_tensor("PD", [128, 512], F32))
        PE = es.enter_context(nc.psum_tensor("PE", [128, 512], F32))

        class Ctr:
            def __init__(self, name):
                self.h = es.enter_context(nc.semaphore(name))
                self.n = 0
        cSL = Ctr("cSL"); cSS = Ctr("cSS"); cG = Ctr("cG")
        cT = Ctr("cT"); cV = Ctr("cV"); cC = Ctr("cC")
        cH = Ctr("cH")
        cCC = Ctr("cCC")
        cW2 = Ctr("cW2")

        PROG = []

        def em(eng, fn):
            PROG.append((eng, fn))

        def wait(eng, ctr, val=None):
            v = ctr.n if val is None else val
            if v > 0:
                em(eng, lambda e, h=ctr.h, v=v: e.wait_ge(h, v))

        def snap(ctr):
            return (ctr.h, ctr.n)

        def wait_s(eng, s):
            if s is None:
                return
            h, v = s
            if v > 0:
                em(eng, lambda e, h=h, v=v: e.wait_ge(h, v))

        def dma(qeng, ctr, out, in_, transpose=False, slow=False):
            ctr.n += 16
            def f(e, out=out, in_=in_, h=ctr.h, transpose=transpose, slow=slow):
                kw = {}
                if transpose:
                    kw["transpose"] = True
                if slow:
                    kw["allow_slow_non_contiguous"] = True
                e.dma_start(out=out, in_=in_, **kw).then_inc(h, 16)
            em(qeng, f)

        def op(eng, ctr, builder):
            ctr.n += 1
            def f(e, b=builder, h=ctr.h):
                b(e).then_inc(h, 1)
            em(eng, f)

        TE = lambda b: op("tensor", cT, b)
        SE = lambda b: op("scalar", cC, b)

        def VE(b, sp=True):
            if sp or SAFE_DVE:
                wait("vector", cV)
            op("vector", cV, b)

        def barrier():
            for eng in ENGS:
                for c in (cSL, cSS, cG, cT, cV, cC, cH, cW2, cWQ, cBR, *cWF, *cW1, *cGW, *cHW):
                    wait(eng, c)

        def qd_chain(src_ap, dst_ap, width, p=128):
            nb = width // 16
            big = width >= 384
            src3 = src_ap.rearrange("p (c k) -> p c k", k=16)
            VE(lambda v, a=src3, w=nb: v.tensor_reduce(
                bmS[0:p, 0:w].unsqueeze(2), a, AX.X, ALU.max, apply_absolute_value=True))
            VE(lambda v, w=nb: v.tensor_scalar(
                sclS[0:p, 0:w], bmS[0:p, 0:w], 1.0 / 6, 1e-12, ALU.mult, ALU.max))
            wait("vector", cV)
            VE(lambda v, w=nb: v.reciprocal(rsS[0:p, 0:w], sclS[0:p, 0:w]), sp=False)
            wait("vector", cV)
            xn3 = xnS[0:p, 0:width].rearrange("p (c k) -> p c k", k=16)
            VE(lambda v, a=src3, x=xn3, w=nb: v.tensor_tensor(
                x, a, rsS[0:p, 0:w].unsqueeze(2).broadcast_to([p, w, 16]), ALU.mult), sp=False)
            wait("scalar", cV)
            SE(lambda s, w=width: s.activation(absS[0:p, 0:w], xnS[0:p, 0:w], AF.Abs))
            SE(lambda s, w=width: s.activation(sgnB[0:p, 0:w], xnS[0:p, 0:w], AF.Sign))
            wait("vector", cC)
            A, B = Ib[0], Ib[1]
            VE(lambda v, w=width: v.tensor_scalar(A[0:p, 0:w], absS[0:p, 0:w], MID[0], None, ALU.is_gt))
            for k in (1, 2, 3):
                VE(lambda v, k=k, w=width: v.scalar_tensor_tensor(
                    A[0:p, 0:w], absS[0:p, 0:w], MID[k], A[0:p, 0:w], ALU.is_gt, ALU.add), sp=not big)
            VE(lambda v, w=width: v.tensor_scalar(B[0:p, 0:w], absS[0:p, 0:w], MID[4], None, ALU.is_gt), sp=False)
            VE(lambda v, w=width: v.scalar_tensor_tensor(
                B[0:p, 0:w], absS[0:p, 0:w], MID[5], B[0:p, 0:w], ALU.is_gt, ALU.add), sp=not big)
            VE(lambda v, w=width: v.scalar_tensor_tensor(
                A[0:p, 0:w], B[0:p, 0:w], 2.0, A[0:p, 0:w], ALU.mult, ALU.add), sp=not big)
            VE(lambda v, w=width: v.tensor_scalar(B[0:p, 0:w], absS[0:p, 0:w], MID[6], 4.0, ALU.is_gt, ALU.mult), sp=False)
            VE(lambda v, w=width: v.tensor_tensor(A[0:p, 0:w], A[0:p, 0:w], B[0:p, 0:w], ALU.add), sp=not big)
            VE(lambda v, w=width: v.tensor_tensor(A[0:p, 0:w], A[0:p, 0:w], sgnB[0:p, 0:w], ALU.mult), sp=not big)
            VE(lambda v, w=nb: v.tensor_scalar(sclB[0:p, 0:w], sclS[0:p, 0:w], 0.5, None, ALU.mult), sp=False)
            wait("vector", cV)
            q3 = Ib[0][0:p, 0:width].rearrange("p (c k) -> p c k", k=16)
            d3 = dst_ap.rearrange("p (c k) -> p c k", k=16)
            VE(lambda v, q=q3, d=d3, w=nb: v.tensor_tensor(
                d, q, sclB[0:p, 0:w].unsqueeze(2).broadcast_to([p, w, 16]), ALU.mult), sp=False)

        wqd_jobs = []
        for (r0, rr) in ((0, 128), (128, 128), (256, 128), (384, 128), (512, 128), (640, 64)):
            wqd_jobs.append((wggu_sl, cb1_dr, r0, rr, 0, 1024))
        for (c0, cw) in ((0, 1024), (1024, 1024), (2048, 768)):
            wqd_jobs.append((wd_sl, cb2_dr, 0, 128, c0, cw))
        wqd_state = {"i": 0, "ring": 0, "wfp_free": [None, None], "wqs_free": [None, None]}
        cWQ = Ctr("cWQ")
        cWF = [Ctr(f"cWF{i}") for i in range(2)]
        cW1 = [Ctr(f"cW1_{i}") for i in range(4)]
        cGW = [Ctr(f"cGW{i}") for i in range(4)]
        cHW = [Ctr(f"cHW{i}") for i in range(8)]
        cBR = Ctr("cBR")

        def drain_wqd(k):
            for _ in range(k):
                i = wqd_state["i"]
                if i >= len(wqd_jobs):
                    return
                wqd_state["i"] = i + 1
                src_d, dst_d, r0, rr, c0, cw = wqd_jobs[i]
                rg = wqd_state["ring"]; wqd_state["ring"] ^= 1
                wait_s("gpsimd", wqd_state["wfp_free"][rg])
                dma("gpsimd", cWF[rg], wfp[rg][0:rr, 0:cw], src_d[r0:r0 + rr, c0:c0 + cw])
                wait("vector", cWF[rg])
                wait("vector", cWQ)
                for s0 in range(0, cw, 512):
                    sw = min(512, cw - s0)
                    qd_chain(wfp[rg][0:rr, s0:s0 + sw], wqs[rg][0:rr, s0:s0 + sw], sw, p=rr)
                wqd_state["wfp_free"][rg] = snap(cV)
                wait("sync", cV)
                dma("sync", cWQ, dst_d[r0:r0 + rr, c0 // 2:(c0 + cw) // 2],
                    wqs[rg][0:rr, 0:cw].bitcast(F32))


        for j in range(4):
            dma("sync", cSL, maskT[j][:], masks_d[j])
        dma("sync", cSL, onesb[:], ones_d)
        dma("sync", cSL, bq8[:], bq_d.rearrange("(c p) -> p c", p=128), slow=True)
        dma("sync", cSL, bkp[:], bk_d.rearrange("(c p) -> p c", p=128), slow=True)
        dma("sync", cSL, c1p[:], c1_d.rearrange("(c p) -> p c", p=128), slow=True)
        dma("sync", cSL, repA[:], g1_d.unsqueeze(0).broadcast_to([128, 1024]))
        dma("sync", cSL, repB[:], b1_d.unsqueeze(0).broadcast_to([128, 1024]))
        dma("gpsimd", cG, brow[:], bv_d.unsqueeze(0))
        wait("vector", cSL)
        VE(lambda v: v.tensor_scalar(bq8[:], bq8[:], 0.125, None, ALU.mult))

        def ln_stats(src, sq=None):
            if sq is None:
                sq = uS
            VE(lambda v, a=src: v.reduce_sum(lnS[:, 0:1], a, AX.X))
            wait("scalar", cV)
            SE(lambda s, a=src, sq=sq: s.activation(sq[:], a, AF.Square))
            wait("vector", cC)
            VE(lambda v, sq=sq: v.reduce_sum(lnS[:, 1:2], sq[:], AX.X))
            VE(lambda v: v.tensor_scalar(lnS[:, 2:3], lnS[:, 0:1], 1.0 / D, None, ALU.mult))
            wait("vector", cV)
            VE(lambda v: v.tensor_tensor(lnS[:, 3:4], lnS[:, 2:3], lnS[:, 2:3], ALU.mult))
            VE(lambda v: v.scalar_tensor_tensor(
                lnS[:, 4:5], lnS[:, 1:2], 1.0 / D, lnS[:, 3:4], ALU.mult, ALU.subtract))
            VE(lambda v: v.tensor_scalar(lnS[:, 4:5], lnS[:, 4:5], 1e-5, None, ALU.add))
            wait("scalar", cV)
            SE(lambda s: s.activation(lnS[:, 5:6], lnS[:, 4:5], AF.Sqrt))
            wait("vector", cC)
            VE(lambda v: v.reciprocal(lnS[:, 6:7], lnS[:, 5:6]))
            wait("vector", cV)

        for t in range(NT):
            rg = t % 2
            wait("sync", cC)
            wait("sync", cV)
            dma("sync", cSL, xr[rg][:], x_in[t * 128:(t + 1) * 128, :])
            wait("vector", cSL)
            wait("scalar", cSL)
            ln_stats(xr[rg][:])
            VE(lambda v, t=t: v.tensor_copy(mu1[:, t:t + 1], lnS[:, 2:3]))
            VE(lambda v, t=t: v.tensor_copy(rstd1[:, t:t + 1], lnS[:, 6:7]))
            wait("vector", cV)
            VE(lambda v, t=t: v.scalar_tensor_tensor(
                nmr1[:, t:t + 1], mu1[:, t:t + 1], -1.0, rstd1[:, t:t + 1], ALU.mult, ALU.mult))
            wait("scalar", cV)
            wait("scalar", cSS)
            SE(lambda s, t=t, rg=rg: s.activation(
                stg[rg][:], xr[rg][:], AF.Identity,
                bias=nmr1[:, t:t + 1], scale=rstd1[:, t:t + 1]))
            wait("sync", cC)
            dma("sync", cSS, u_dr[t * 128:(t + 1) * 128, :], stg[rg][:])
        drain_wqd(4)
        for i in range(64):
            rt_, ch_ = i // 2, i % 2
            hb = hqstg[i % 2]
            wait("gpsimd", cSS)
            dma("gpsimd", cG, hb[:],
                W2_d[rt_ * 128:(rt_ + 1) * 128, ch_ * 512:ch_ * 512 + 512])
            wait("sync", cG)
            dma("sync", cSS, w2b_dr[rt_ * 128:(rt_ + 1) * 128, ch_ * 512:ch_ * 512 + 512], hb[:])
        barrier()

        u_fm = lambda c: pool[c][:, 0:1280]
        k_fm = lambda c: pool[8 + c][:, 0:1280]
        q_fm = lambda c: pool[16 + c // 2][:, (c % 2) * 1024:(c % 2) * 1024 + 1024]
        ho_fm = lambda c: pool[c][:, 0:1024]

        for half in range(2):
            e0 = 0 if half == 0 else 6
            q0t = half * 8
            qex = 0 if half == 0 else 2
            for c in range(8):
                dma("sync", cSL, u_fm(c),
                    u_dr[e0 * 128:e0 * 128 + 1280, c * 128:(c + 1) * 128], transpose=True)
            wait("gpsimd", cT)
            for kc in range(8):
                dma("gpsimd", cGW[0], wp[kc][:], Wq_d[kc * 128:(kc + 1) * 128, :])
            wait("tensor", cSL)
            wait("tensor", cGW[0])

            pp = 0
            ps_free = {0: None, 1: None}
            for c in range(8):
                for tt in range(2):
                    pi = pp; ps = (PD if pp == 0 else PE); pp ^= 1
                    wait_s("tensor", ps_free[pi])
                    def qmm(t_, c=c, tt=tt, ps=ps, qex=qex):
                        for kc in range(8):
                            ins = t_.matmul(
                                ps[:], wp[kc][:, c * 128:(c + 1) * 128],
                                u_fm(kc)[:, qex * 128 + tt * 512: qex * 128 + tt * 512 + 512],
                                start=(kc == 0), stop=(kc == 7))
                        return ins
                    TE(qmm)
                    mm_done = snap(cT)
                    wait_s("scalar", mm_done)
                    wait("scalar", cV)
                    SE(lambda s, c=c, tt=tt, ps=ps: s.activation(
                        q_fm(c)[:, tt * 512:tt * 512 + 512], ps[:], AF.Identity,
                        bias=bq8[:, c:c + 1], scale=0.125))
                    ps_free[pi] = snap(cC)
            for kc in range(8):
                dma("gpsimd", cGW[1], wp[8 + kc][:], Wk_d[kc * 128:(kc + 1) * 128, :])
            wait("tensor", cGW[1])
            for c in range(8):
                for (t0, tw) in ((0, 512), (512, 512), (1024, 256)):
                    pi = pp; ps = (PD if pp == 0 else PE); pp ^= 1
                    wait_s("tensor", ps_free[pi])
                    def kmm(t_, c=c, t0=t0, tw=tw, ps=ps):
                        for kc in range(8):
                            ins = t_.matmul(
                                ps[:, 0:tw], wp[8 + kc][:, c * 128:(c + 1) * 128],
                                u_fm(kc)[:, t0:t0 + tw], start=(kc == 0), stop=(kc == 7))
                        return ins
                    TE(kmm)
                    mm_done = snap(cT)
                    wait_s("scalar", mm_done)
                    SE(lambda s, c=c, t0=t0, tw=tw, ps=ps: s.activation(
                        k_fm(c)[:, t0:t0 + tw], ps[:, 0:tw], AF.Identity,
                        bias=bkp[:, c:c + 1]))
                    ps_free[pi] = snap(cC)
            wait("gpsimd", cT)
            dma("gpsimd", cGW[2], brow[:], bv_d.unsqueeze(0))
            for kc in range(8):
                dma("gpsimd", cGW[2], wp[kc][:], Wv_d[kc * 128:(kc + 1) * 128, :])
            for bt in range(10):
                VE(lambda v, bt=bt: v.memset(
                    vt[bt][:].rearrange("p (g c) -> p g c", c=65)[:, :, 64:65], 1.0))
            wait("tensor", cGW[2])
            wait("tensor", cV)
            for bt in range(10):
                for dh in range(2):
                    pi = pp; ps = (PD if pp == 0 else PE); pp ^= 1
                    wait_s("tensor", ps_free[pi])
                    def vmm(t_, bt=bt, dh=dh, ps=ps):
                        for kc in range(8):
                            t_.matmul(ps[:], u_fm(kc)[:, bt * 128:(bt + 1) * 128],
                                      wp[kc][:, dh * 512:dh * 512 + 512],
                                      start=(kc == 0), stop=False)
                        return t_.matmul(ps[:], onesb[:],
                                         brow[0:1, dh * 512:dh * 512 + 512],
                                         start=False, stop=True)
                    TE(vmm)
                    mm_done = snap(cT)
                    wait_s("vector", mm_done)
                    VE(lambda v, bt=bt, dh=dh, ps=ps: v.tensor_copy(
                        vt[bt][:].rearrange("p (g c) -> p g c", c=65)[:, dh * 8:dh * 8 + 8, 0:64],
                        ps[:].rearrange("p (g c) -> p g c", c=64)), sp=False)
                    ps_free[pi] = snap(cV)
            drain_wqd(12)
            if half == 0:
                wait("gpsimd", cWQ)
                em("gpsimd", lambda g: g.collective_compute(
                    "AllGather", ALU.bypass, replica_groups=[list(range(8))],
                    ins=[cb1_dr.opt()], outs=[gg1_dr.opt()]).then_inc(cCC.h, 1))
                cCC.n += 1
                em("gpsimd", lambda g: g.collective_compute(
                    "AllGather", ALU.bypass, replica_groups=[list(range(8))],
                    ins=[cb2_dr.opt()], outs=[gg2_dr.opt()]).then_inc(cCC.h, 1))
                cCC.n += 1
            barrier()
            if dump_scratch and half == 1:
                for c in range(8):
                    dma("sync", cSS, q_dbg[c * 128:(c + 1) * 128, :], q_fm(c))
                    dma("sync", cSS, k_dbg[c * 128:(c + 1) * 128, :], k_fm(c))
                for bt in range(10):
                    dma("sync", cSS, v_dbg[bt * 128:(bt + 1) * 128, :], vt[bt][:])
                barrier()

            groups = [(nloc, h) for nloc in range(4) for h in range(H)]

            def g_meta(gi):
                nloc, h = groups[gi]
                n = half * 4 + nloc
                jpres = [j for j in range(6) if 0 <= (2 * (n - 1) + j) < 16]
                return nloc, h, n, jpres

            pa_rd = [None, None]
            ex_rd = [None, None]
            pd_rd_v = None
            pd_rd_c = None
            pe_rd = None
            rrowb_rd = None
            pvs_rd = None
            sc_done = {}; exp_done = {}; msk_done = {}; pv_done = {}
            rep_done = {}; evac_done = {}

            def emit_front(gi):
                nloc, h, n, jpres = g_meta(gi)
                jlo, jhi = jpres[0], jpres[-1] + 1
                pa = PA[gi % 2]
                ex = exb[gi % 2]
                c = h // 2
                rb = (h % 2) * 64
                wait_s("tensor", pa_rd[gi % 2])
                def smm(t_, n=n, c=c, rb=rb, pa=pa, jpres=jpres, e0=e0, nloc=nloc):
                    for j in jpres:
                        bt = 2 * (n - 1) + j - e0
                        ins = t_.matmul(
                            pa[:, j * 256:(j + 1) * 256],
                            k_fm(c)[rb:rb + 64, bt * 128:(bt + 1) * 128],
                            q_fm(c)[rb:rb + 64, nloc * 256:(nloc + 1) * 256],
                            start=True, stop=True)
                    return ins
                TE(smm)
                sc_done[gi] = snap(cT)
                wait_s("scalar", sc_done[gi])
                wait_s("scalar", ex_rd[gi % 2])
                SE(lambda s, pa=pa, ex=ex, jlo=jlo, jhi=jhi: s.activation(
                    ex[:, jlo * 256:jhi * 256], pa[:, jlo * 256:jhi * 256], AF.Exp))
                exp_done[gi] = snap(cC)
                pa_rd[gi % 2] = snap(cC)
                wait_s("vector", exp_done[gi])
                first = True
                for j in jpres:
                    if j in (2, 3):
                        continue
                    mi = {0: 0, 1: 1, 4: 2, 5: 3}[j]
                    VE(lambda v, ex=ex, j=j, mi=mi: v.tensor_tensor(
                        ex[:, j * 256:(j + 1) * 256], ex[:, j * 256:(j + 1) * 256],
                        maskT[mi][:], ALU.mult), sp=first)
                    first = False
                msk_done[gi] = snap(cV)

            def emit_back(gi):
                nonlocal pd_rd_v, pd_rd_c, pe_rd, rrowb_rd, pvs_rd
                nloc, h, n, jpres = g_meta(gi)
                ex = exb[gi % 2]
                c = h // 2
                rb = (h % 2) * 64
                wait_s("tensor", msk_done[gi])
                wait_s("tensor", exp_done[gi])
                wait_s("tensor", pd_rd_v)
                wait_s("tensor", pd_rd_c)
                def pvmm(t_, n=n, h=h, ex=ex, jpres=jpres, e0=e0):
                    for ji, j in enumerate(jpres):
                        bt = 2 * (n - 1) + j - e0
                        ins = t_.matmul(
                            PD[0:65, 0:256], vt[bt][:, h * 65:h * 65 + 65],
                            ex[:, j * 256:(j + 1) * 256],
                            start=(ji == 0), stop=(ji == len(jpres) - 1))
                    return ins
                TE(pvmm)
                pv_done[gi] = snap(cT)
                ex_rd[gi % 2] = snap(cT)
                wait_s("vector", pv_done[gi])
                VE(lambda v: v.reciprocal(rrow[:], PD[64:65, 0:256]))
                pd_rd_v = snap(cV)
                wait_s("vector", rrowb_rd)
                VE(lambda v: v.tensor_scalar(rrowb[:], rrow[:], 1.0, None, ALU.mult))
                rrowb_done = snap(cV)
                wait_s("tensor", rrowb_done)
                wait_s("tensor", pe_rd)
                TE(lambda t_: t_.matmul(PE[0:64, 0:256], onesb[0:1, 0:64], rrowb[:],
                                        start=True, stop=True))
                rep_done[gi] = snap(cT)
                rrowb_rd = snap(cT)
                wait_s("scalar", pv_done[gi])
                wait_s("scalar", pvs_rd)
                SE(lambda s: s.activation(pvS[:], PD[0:64, 0:256], AF.Identity))
                evac_done[gi] = snap(cC)
                pd_rd_c = snap(cC)
                wait_s("vector", evac_done[gi])
                wait_s("vector", rep_done[gi])
                VE(lambda v, c=c, rb=rb, nloc=nloc: v.tensor_tensor(
                    ho_fm(c)[rb:rb + 64, nloc * 256:(nloc + 1) * 256],
                    pvS[:], PE[0:64, 0:256], ALU.mult), sp=False)
                pe_rd = snap(cV)
                pvs_rd = snap(cV)

            emit_front(0)
            for gi in range(len(groups)):
                if gi + 1 < len(groups):
                    emit_front(gi + 1)
                emit_back(gi)
                if gi % 8 == 7:
                    drain_wqd(1)
            barrier()

            if dump_scratch:
                for c in range(8):
                    dma("sync", cSS, ho_dbg[c * 128:(c + 1) * 128, half * 1024:half * 1024 + 1024],
                        ho_fm(c))
            wait("gpsimd", cT)
            dma("gpsimd", cGW[3], brow[:], bo_d.unsqueeze(0))
            for kc in range(8):
                dma("gpsimd", cGW[3], wp[8 + kc][:], Wo_d[kc * 128:(kc + 1) * 128, :])
            wait("tensor", cGW[3])
            pa3_free = [None, None]
            for tl in range(8):
                gt = q0t + tl
                pa = PA[tl % 2]
                rg = tl % 2
                wait("sync", cC)
                dma("sync", cSL, xr[rg][:], x_in[gt * 128:(gt + 1) * 128, :])
                wait_s("tensor", pa3_free[tl % 2])
                def womm(t_, tl=tl, pa=pa):
                    for dh in range(2):
                        for kc in range(8):
                            t_.matmul(pa[:, dh * 512:dh * 512 + 512],
                                      ho_fm(kc)[:, tl * 128:(tl + 1) * 128],
                                      wp[8 + kc][:, dh * 512:dh * 512 + 512],
                                      start=(kc == 0), stop=False)
                        ins = t_.matmul(pa[:, dh * 512:dh * 512 + 512], onesb[:],
                                        brow[0:1, dh * 512:dh * 512 + 512],
                                        start=False, stop=True)
                    return ins
                TE(womm)
                womm_done = snap(cT)
                wait("scalar", cSL)
                wait("scalar", cV)
                SE(lambda s, gt=gt, rg=rg: s.activation(
                    uS[:], xr[rg][:], AF.Identity,
                    bias=nmr1[:, gt:gt + 1], scale=rstd1[:, gt:gt + 1]))
                wait_s("vector", womm_done)
                wait("vector", cC)
                wait("vector", cSS)
                VE(lambda v, pa=pa: v.tensor_tensor(tmpS[:, 0:512], pa[:, 0:512], uS[:, 0:512], ALU.add))
                VE(lambda v, pa=pa: v.tensor_tensor(tmpS[:, 512:1024], pa[:, 512:1024], uS[:, 512:1024], ALU.add), sp=False)
                pa3_free[tl % 2] = snap(cV)
                ln_stats(tmpS[:])
                VE(lambda v: v.tensor_scalar(tmpS[:], tmpS[:], lnS[:, 2:3], lnS[:, 6:7], ALU.subtract, ALU.mult))
                VE(lambda v: v.tensor_tensor(tmpS[:], tmpS[:], repA[:], ALU.mult), sp=False)
                VE(lambda v: v.tensor_tensor(tmpS[:], tmpS[:], repB[:], ALU.add), sp=False)
                wait("scalar", cV)
                wait("scalar", cSS)
                SE(lambda s, rg=rg: s.activation(stg[rg][:], tmpS[:], AF.Identity))
                wait("sync", cV)
                dma("sync", cSS, hf_dr[gt * 128:(gt + 1) * 128, :], tmpS[:])
                wait("sync", cC)
                dma("sync", cSS, hb_dr[gt * 128:(gt + 1) * 128, :], stg[rg][:])
            drain_wqd(6)
            barrier()

        wait("sync", cV)
        dma("sync", cSL, repA[:], g2_d.unsqueeze(0).broadcast_to([128, 1024]))
        dma("sync", cSL, repB[:], b2_d.unsqueeze(0).broadcast_to([128, 1024]))
        wait("gpsimd", cT)
        dma("gpsimd", cBR, brow[:], c2_d.unsqueeze(0))
        for c in range(8):
            dma("sync", cSL, pool[c][:, 0:2048],
                hb_dr[:, c * 128:(c + 1) * 128], transpose=True)
        wait("tensor", cSL)
        wait("tensor", cBR)

        g_strip = lambda m: pool[8 + m // 4][:, (m % 4) * 512:(m % 4) * 512 + 512]
        f2slots = [(PA[0][:, 0:512], PA[0][:, 512:1024]),
                   (PA[0][:, 1024:1536], PA[1][:, 0:512]),
                   (PA[1][:, 512:1024], PA[1][:, 1024:1536]),
                   (PD[:], PE[:])]

        f2_evac_done = None
        ev1_prev = None
        gu_free = [None, None]
        w1ring_free = [None, None, None, None]
        w2slot_free = [None] * 8
        slots4 = [PA[0][:, 0:512], PA[0][:, 512:1024], PA[0][:, 1024:1536], PA[1][:, 0:512]]
        for tokg in range(4):
            wait("sync", cV)
            wait("sync", cC)
            for ts4 in range(4):
                gt = tokg * 4 + ts4
                dma("sync", cH, xr[ts4][:], hf_dr[gt * 128:(gt + 1) * 128, :])
            h_loaded = cH.n
            for Mcg in range(16):
                w1t = pool[16 + Mcg % 4]
                wait_s("gpsimd", w1ring_free[Mcg % 4])
                for kc in range(8):
                    dma("gpsimd", cW1[Mcg % 4], w1t[:, kc * 256:(kc + 1) * 256],
                        W1_d[kc * 128:(kc + 1) * 128, Mcg * 256:(Mcg + 1) * 256])
                for Mh in range(2):
                    Mc = Mcg * 2 + Mh
                    ps = (PD if Mc % 2 == 0 else PE)
                    if Mh == 0:
                        wait("tensor", cW1[Mcg % 4])
                    wait_s("tensor", gu_free[Mc % 2])
                    def w1mm(t_, Mh=Mh, tokg=tokg, ps=ps, w1t=w1t):
                        for kc in range(8):
                            ins = t_.matmul(
                                ps[:], w1t[:, kc * 256 + Mh * 128: kc * 256 + Mh * 128 + 128],
                                pool[kc][:, tokg * 512:tokg * 512 + 512],
                                start=(kc == 0), stop=(kc == 7))
                        return ins
                    TE(w1mm)
                    w1_done = snap(cT)
                    wait_s("scalar", w1_done)
                    SE(lambda s, Mc=Mc, ps=ps: s.activation(
                        g_strip(Mc), ps[:], AF.Gelu_apprx_tanh, bias=c1p[:, Mc:Mc + 1]))
                    gu_free[Mc % 2] = snap(cC)
                w1ring_free[Mcg % 4] = snap(cT)
            for dh in range(2):
                for Mc in range(32):
                    w2t = wp[8 + Mc % 8]
                    wait_s("sync", w2slot_free[Mc % 8])
                    dma("sync", cHW[Mc % 8], w2t[:, 0:512],
                        w2b_dr[Mc * 128:(Mc + 1) * 128, dh * 512:dh * 512 + 512])
                    wait("tensor", cHW[Mc % 8])
                    if Mc == 0:
                        wait_s("tensor", gu_free[0])
                        wait_s("tensor", gu_free[1])
                        if dh == 0:
                            wait_s("tensor", ev1_prev)
                        else:
                            wait_s("tensor", ev0)
                    def w2mm(t_, Mc=Mc, w2t=w2t):
                        for ts4 in range(4):
                            ins = t_.matmul(slots4[ts4],
                                            g_strip(Mc)[:, ts4 * 128:(ts4 + 1) * 128],
                                            w2t[:, 0:512],
                                            start=(Mc == 0), stop=False)
                        return ins
                    TE(w2mm)
                    w2slot_free[Mc % 8] = snap(cT)
                def c2mm(t_, dh=dh):
                    for ts4 in range(4):
                        ins = t_.matmul(slots4[ts4], onesb[:],
                                        brow[0:1, dh * 512:dh * 512 + 512],
                                        start=False, stop=True)
                    return ins
                TE(c2mm)
                te_dh = snap(cT)
                wait_s("vector", te_dh)
                wait("vector", cH, h_loaded)
                for ts4 in range(4):
                    VE(lambda v, ts4=ts4, dh=dh: v.tensor_tensor(
                        xr[ts4][:, dh * 512:dh * 512 + 512], slots4[ts4],
                        xr[ts4][:, dh * 512:dh * 512 + 512], ALU.add), sp=(ts4 == 0))
                if dh == 0:
                    ev0 = snap(cV)
                else:
                    ev1_prev = snap(cV)
            for ts4 in range(4):
                gt = tokg * 4 + ts4
                rg = ts4 % 2
                wait("scalar", cSS)
                wait("vector", cSS)
                ln_stats(xr[ts4][:], sq=uS)
                wait("vector", cSS)
                VE(lambda v, ts4=ts4: v.tensor_scalar(
                    tmpS[:], xr[ts4][:], lnS[:, 2:3], lnS[:, 6:7], ALU.subtract, ALU.mult))
                VE(lambda v: v.tensor_tensor(tmpS[:], tmpS[:], repA[:], ALU.mult), sp=False)
                VE(lambda v: v.tensor_tensor(tmpS[:], tmpS[:], repB[:], ALU.add), sp=False)
                t_done = snap(cV)
                wait("sync", cV)
                dma("sync", cH, xr[ts4][:], x_in[gt * 128:(gt + 1) * 128, :])
                wait("vector", cH)
                VE(lambda v, ts4=ts4: v.tensor_tensor(uS[:], tmpS[:], xr[ts4][:], ALU.add), sp=False)
                wait("sync", cV)
                dma("sync", cSS, x1_dr[gt * 128:(gt + 1) * 128, :], uS[:])
                ln_stats(uS[:], sq=xr[ts4])
                wait("vector", cSS)
                VE(lambda v: v.tensor_scalar(
                    tmpS[:], uS[:], lnS[:, 2:3], lnS[:, 6:7], ALU.subtract, ALU.mult))
                for ch in range(2):
                    qd_chain(tmpS[:, ch * 512:ch * 512 + 512], stg[rg][:, ch * 512:ch * 512 + 512], 512)
                wait("sync", cV)
                dma("sync", cSS, xq_dr[gt * 128:(gt + 1) * 128, :], stg[rg][:])
            drain_wqd(6)
        barrier()

        drain_wqd(len(wqd_jobs))
        barrier()
        wait("sync", cCC)
        if dump_scratch:
            dma("sync", cSS, gg1_dbg, gg1_dr)
            dma("sync", cSS, gg2_dbg, gg2_dr)
        for c in range(8):
            dma("sync", cSL, pool[c][:, 0:2048],
                xq_dr[:, c * 128:(c + 1) * 128], transpose=True)
        wait("tensor", cSL)

        NCHW = [(0, 512), (512, 512), (1024, 512), (1536, 512), (2048, 512), (2560, 256)]
        gu_slots = [(PD[:], PE[:]), (PA[0][:, 0:512], PA[0][:, 512:1024])]
        gu5_free = [(None, None), (None, None)]
        sg_rd = None
        slab_mm_done = [None, None]
        slab_i = 0
        for tokg in range(4):
            for (n0, nw) in NCHW:
                p5 = slab_i % 2
                slab_i += 1
                wait_s("sync", slab_mm_done[p5])
                for kc in range(8):
                    dma("sync", cHW[p5], wp[kc][:, p5 * 512:p5 * 512 + nw],
                        wgq_dr[n0:n0 + nw, kc * 128:(kc + 1) * 128], transpose=True)
                    dma("sync", cHW[p5], wp[8 + kc][:, p5 * 512:p5 * 512 + nw],
                        wuq_dr[n0:n0 + nw, kc * 128:(kc + 1) * 128], transpose=True)
                wait("tensor", cHW[p5])
                for ti in range(4):
                    tokc = tokg * 4 + ti
                    pg, pu = gu_slots[ti % 2]
                    wait_s("tensor", gu5_free[ti % 2][0])
                    wait_s("tensor", gu5_free[ti % 2][1])
                    def gumm(t_, tokc=tokc, nw=nw, pg=pg, pu=pu, p5=p5):
                        for kc in range(8):
                            t_.matmul(pg[:, 0:nw],
                                      pool[kc][:, tokc * 128:(tokc + 1) * 128],
                                      wp[kc][:, p5 * 512:p5 * 512 + nw], start=(kc == 0), stop=(kc == 7))
                        for kc in range(8):
                            ins = t_.matmul(pu[:, 0:nw],
                                            pool[kc][:, tokc * 128:(tokc + 1) * 128],
                                            wp[8 + kc][:, p5 * 512:p5 * 512 + nw], start=(kc == 0), stop=(kc == 7))
                        return ins
                    TE(gumm)
                    gu_done = snap(cT)
                    wait_s("scalar", gu_done)
                    wait_s("scalar", sg_rd)
                    SE(lambda s, pg=pg, nw=nw: s.activation(sgS[:, 0:nw], pg[:, 0:nw], AF.Silu))
                    silu_done = snap(cC)
                    wait_s("vector", silu_done)
                    wait_s("vector", gu_done)
                    wait("vector", cSS)
                    VE(lambda v, pu=pu, nw=nw: v.tensor_tensor(
                        tmpS[:, 0:nw], sgS[:, 0:nw], pu[:, 0:nw], ALU.mult), sp=False)
                    sg_rd = snap(cV)
                    gu5_free[ti % 2] = (silu_done, sg_rd)
                    hrg = ti % 2
                    qd_chain(tmpS[:, 0:nw], hqstg[hrg][:, 0:nw], nw)
                    wait("sync", cV)
                    dma("sync", cSS, hq_dr[tokc * 128:(tokc + 1) * 128, n0:n0 + nw],
                        hqstg[hrg][:, 0:nw])
                slab_mm_done[p5] = snap(cT)
        barrier()

        def wdt_tile(kc):
            if kc < 16:
                return wp[kc][:, 0:1024]
            u = pool[(kc - 16) // 2]
            return u[:, ((kc - 16) % 2) * 1024:((kc - 16) % 2) * 1024 + 1024]

        for kc in range(22):
            dma("sync", cHW[4], wdt_tile(kc), wdq_dr[:, kc * 128:(kc + 1) * 128],
                transpose=True)
        wait("tensor", cHW[4])

        def hq_ring(st, kc):
            u = pool[8 + st * 6 + kc // 4]
            return u[:, (kc % 4) * 512:(kc % 4) * 512 + 512]

        dslots = [(PA[0][:, 0:512], PA[0][:, 512:1024]),
                  (PA[0][:, 1024:1536], PA[1][:, 0:512]),
                  (PA[1][:, 512:1024], PA[1][:, 1024:1536]),
                  (PD[:], PE[:])]
        dw_done = [None, None]
        out_ev = [None] * 4
        for tokg in range(4):
            st = tokg % 2
            wait_s("sync", dw_done[st])
            for kc in range(22):
                dma("sync", cHW[2 + st], hq_ring(st, kc),
                    hq_dr[tokg * 512:(tokg + 1) * 512, kc * 128:(kc + 1) * 128], transpose=True)
            wait("tensor", cHW[2 + st])
            for ti in range(4):
                wait_s("tensor", out_ev[ti])
                def dwmm(t_, st=st, ti=ti):
                    for kc in range(22):
                        for dh in range(2):
                            ins = t_.matmul(
                                dslots[ti][dh],
                                hq_ring(st, kc)[:, ti * 128:(ti + 1) * 128],
                                wdt_tile(kc)[:, dh * 512:dh * 512 + 512],
                                start=(kc == 0), stop=(kc == 21))
                    return ins
                TE(dwmm)
                dmm_done = snap(cT)
                gt = tokg * 4 + ti
                rg = ti % 2
                wait("sync", cV)
                dma("sync", cH, xr[rg][:], x1_dr[gt * 128:(gt + 1) * 128, :])
                wait_s("vector", dmm_done)
                wait("vector", cH)
                wait("vector", cSS)
                VE(lambda v, ti=ti, rg=rg: v.tensor_tensor(
                    uS[:, 0:512], dslots[ti][0], xr[rg][:, 0:512], ALU.add))
                VE(lambda v, ti=ti, rg=rg: v.tensor_tensor(
                    uS[:, 512:1024], dslots[ti][1], xr[rg][:, 512:1024], ALU.add), sp=False)
                out_ev[ti] = snap(cV)
                wait("sync", cV)
                dma("sync", cSS, out_d[gt * 128:(gt + 1) * 128, :], uS[:])
            dw_done[st] = snap(cT)
        wait("sync", cSS)

        with nc.Block() as block:
            @block.sync
            def _(e):
                for eng, fn in PROG:
                    if eng == "sync":
                        fn(e)

            @block.gpsimd
            def _(e):
                for eng, fn in PROG:
                    if eng == "gpsimd":
                        fn(e)

            @block.tensor
            def _(e):
                for eng, fn in PROG:
                    if eng == "tensor":
                        fn(e)

            @block.vector
            def _(e):
                for eng, fn in PROG:
                    if eng == "vector":
                        fn(e)

            @block.scalar
            def _(e):
                for eng, fn in PROG:
                    if eng == "scalar":
                        fn(e)
    return nc


def _masks():
    r = np.arange(128)[:, None]
    q = np.arange(256)[None, :]
    m0 = (q <= r).astype(ml_dtypes.bfloat16)
    m1 = (q <= r + 128).astype(ml_dtypes.bfloat16)
    m4 = (q >= r).astype(ml_dtypes.bfloat16)
    m5 = (q >= r + 128).astype(ml_dtypes.bfloat16)
    return [m0, m1, m4, m5]


def _in_maps(inputs):
    masks = _masks()
    ones = np.ones((1, 128), dtype=ml_dtypes.bfloat16)
    base = {
        "Wq": inputs["L0_attn_Wq"], "bq": inputs["L0_attn_bq"],
        "Wk": inputs["L0_attn_Wk"], "bk": inputs["L0_attn_bk"],
        "Wv": inputs["L0_attn_Wv"], "bv": inputs["L0_attn_bv"],
        "Wo": inputs["L0_attn_Wo"], "bo": inputs["L0_attn_bo"],
        "g1": inputs["L0_attn_g1"], "b1": inputs["L0_attn_b1"],
        "W1": inputs["L0_attn_W1"], "c1": inputs["L0_attn_c1"],
        "W2": inputs["L0_attn_W2"], "c2": inputs["L0_attn_c2"],
        "g2": inputs["L0_attn_g2"], "b2": inputs["L0_attn_b2"],
        "ones_row": ones,
    }
    base = {k: np.ascontiguousarray(np.asarray(v, dtype=np.float32))
            if k not in ("ones_row",) else v for k, v in base.items()}
    for j in range(4):
        base[f"mask{j}"] = masks[j]
    x = np.ascontiguousarray(np.asarray(inputs["L0_attn_x"], dtype=np.float32))
    wggu = np.concatenate([np.asarray(inputs["L0_ffn_w_gate"], dtype=np.float32),
                           np.asarray(inputs["L0_ffn_w_up"], dtype=np.float32)], axis=0)
    wd = np.asarray(inputs["L0_ffn_w_down"], dtype=np.float32)
    in_maps = []
    for c in range(8):
        m = dict(base)
        m["x"] = np.ascontiguousarray(x[c])
        m["wggu_sl"] = np.ascontiguousarray(wggu[c * 704:(c + 1) * 704])
        m["wd_sl"] = np.ascontiguousarray(wd[c * 128:(c + 1) * 128])
        in_maps.append(m)
    return in_maps


def kernel(**inputs):
    nc = build()
    res = run_bass_kernel_spmd(nc, _in_maps(inputs), core_ids=list(range(8)))
    return np.stack([np.asarray(r["out"]) for r in res.results], axis=0)


# version 14
# speedup vs baseline: 1.1502x; 1.1502x over previous
import contextlib
import numpy as np
import ml_dtypes
import concourse.bass as bass
import concourse.mybir as mybir
from concourse.bass_utils import run_bass_kernel_spmd

F32 = mybir.dt.float32
BF16 = mybir.dt.bfloat16
AF = mybir.ActivationFunctionType
ALU = mybir.AluOpType
AX = mybir.AxisListType

S, D, H, HD, WB, NB, FFN, NI = 2048, 1024, 16, 64, 256, 8, 4096, 2816
NT = S // 128
MID = [0.25, 0.75, 1.25, 1.75, 2.5, 3.5, 5.0]
ENGS = ("sync", "gpsimd", "tensor", "vector", "scalar")
SAFE_DVE = False


def build(dump_scratch=False):
    nc = bass.Bass("TRN2", target_bir_lowering=False)

    def din(name, shape):
        return nc.dram_tensor(name, shape, F32, kind="ExternalInput").ap()

    x_in = din("x", [S, D])
    Wq_d, Wk_d, Wv_d, Wo_d = (din(n, [D, D]) for n in ("Wq", "Wk", "Wv", "Wo"))
    bq_d, bk_d, bv_d, bo_d = (din(n, [D]) for n in ("bq", "bk", "bv", "bo"))
    g1_d, b1_d, g2_d, b2_d = (din(n, [D]) for n in ("g1", "b1", "g2", "b2"))
    W1_d = din("W1", [D, FFN]); c1_d = din("c1", [FFN])
    W2_d = din("W2", [FFN, D]); c2_d = din("c2", [D])
    wggu_sl = din("wggu_sl", [704, D])
    wd_sl = din("wd_sl", [128, NI])
    masks_d = [nc.dram_tensor(f"mask{j}", [128, 256], BF16, kind="ExternalInput").ap()
               for j in range(4)]
    ones_d = nc.dram_tensor("ones_row", [1, 128], BF16, kind="ExternalInput").ap()
    out_d = nc.dram_tensor("out", [S, D], F32, kind="ExternalOutput").ap()

    sk = "ExternalOutput" if dump_scratch else "Internal"
    u_dr = nc.dram_tensor("u_dr", [S, D], BF16, kind=sk).ap()
    hf_dr = nc.dram_tensor("hf_dr", [S, D], F32, kind=sk).ap()
    hb_dr = nc.dram_tensor("hb_dr", [S, D], BF16, kind=sk).ap()
    x1_dr = nc.dram_tensor("x1_dr", [S, D], F32, kind=sk).ap()
    xq_dr = nc.dram_tensor("xq_dr", [S, D], BF16, kind=sk).ap()
    hq_dr = nc.dram_tensor("hq_dr", [S, NI], BF16, kind=sk).ap()
    ho_dbg = nc.dram_tensor("ho_dbg", [D, S], BF16, kind=sk).ap()
    q_dbg = nc.dram_tensor("q_dbg", [D, 1024], BF16, kind=sk).ap()
    k_dbg = nc.dram_tensor("k_dbg", [D, 1280], BF16, kind=sk).ap()
    v_dbg = nc.dram_tensor("v_dbg", [1280, 1040], BF16, kind=sk).ap()
    ex_dbg = nc.dram_tensor("ex_dbg", [128, 1536], BF16, kind=sk).ap()
    pv_dbg = nc.dram_tensor("pv_dbg", [64, 256], F32, kind=sk).ap()
    rr_dbg = nc.dram_tensor("rr_dbg", [1, 256], F32, kind=sk).ap()
    rep_dbg = nc.dram_tensor("rep_dbg", [64, 256], F32, kind=sk).ap()
    w2b_dr = nc.dram_tensor("w2b_dr", [FFN, D], BF16, kind="Internal").ap()
    cb1_dr = nc.dram_tensor("cb1_dr", [704, D // 2], F32, kind="Internal").ap()
    cb2_dr = nc.dram_tensor("cb2_dr", [128, NI // 2], F32, kind="Internal").ap()
    gg1_dr = nc.dram_tensor("gg1_dr", [2 * NI, D // 2], F32, kind="Internal").ap()
    gg2_dr = nc.dram_tensor("gg2_dr", [D, NI // 2], F32, kind="Internal").ap()
    gg1_dbg = nc.dram_tensor("gg1_dbg", [2 * NI, D // 2], F32, kind=sk).ap()
    gg2_dbg = nc.dram_tensor("gg2_dbg", [D, NI // 2], F32, kind=sk).ap()
    wgq_dr = gg1_dr.bitcast(BF16)[0:NI, :]
    wuq_dr = gg1_dr.bitcast(BF16)[NI:2 * NI, :]
    wdq_dr = gg2_dr.bitcast(BF16)

    es = contextlib.ExitStack()
    with es:
        T = lambda name, shape, dt=F32: es.enter_context(nc.sbuf_tensor(name, shape, dt))

        pool = [T(f"pool{i}", [128, 2048], BF16) for i in range(20)]
        wp = [T(f"wp{i}", [128, 1024], BF16) for i in range(16)]
        vt = [T(f"vt{i}", [128, 16 * 65], BF16) for i in range(10)]
        exb = [T(f"exb{i}", [128, 1536], BF16) for i in range(2)]
        xr = [T(f"xr{i}", [128, 1024], F32) for i in range(4)]
        uS = T("uS", [128, 1024], F32)
        tmpS = T("tmpS", [128, 1024], F32)
        xnS = T("xnS", [128, 512], F32)
        absS = T("absS", [128, 512], F32)
        sgS = T("sgS", [128, 512], F32)
        pvS = T("pvS", [64, 256], F32)
        Ib = [T(f"Ib{k}", [128, 512], BF16) for k in range(2)]
        sgnB = T("sgnB", [128, 512], BF16)
        bmS = T("bmS", [128, 64], F32)
        sclS = T("sclS", [128, 64], F32)
        rsS = T("rsS", [128, 64], F32)
        sclB = T("sclB", [128, 64], BF16)
        stg = [T(f"stg{i}", [128, 1024], BF16) for i in range(2)]
        hqstg = [T(f"hqstg{i}", [128, 512], BF16) for i in range(2)]
        repA = T("repA", [128, 1024], F32)
        repB = T("repB", [128, 1024], F32)
        lnS = T("lnS", [128, 8], F32)
        mu1 = T("mu1", [128, NT], F32)
        rstd1 = T("rstd1", [128, NT], F32)
        nmr1 = T("nmr1", [128, NT], F32)
        bq8 = T("bq8", [128, 8], F32)
        bkp = T("bkp", [128, 8], F32)
        c1p = T("c1p", [128, 32], F32)
        brow = T("brow", [1, 1024], BF16)
        onesb = T("onesb", [1, 128], BF16)
        rrow = T("rrow", [1, 256], F32)
        rrowb = T("rrowb", [1, 256], BF16)
        maskT = [T(f"maskT{j}", [128, 256], BF16) for j in range(4)]
        wfp = [T(f"wfp{i}", [128, 1024], F32) for i in range(2)]
        wqs = [T(f"wqs{i}", [128, 1024], BF16) for i in range(2)]

        PA = [es.enter_context(nc.psum_tensor(f"PA{i}", [128, 1536], F32)) for i in range(2)]
        PD = es.enter_context(nc.psum_tensor("PD", [128, 512], F32))
        PE = es.enter_context(nc.psum_tensor("PE", [128, 512], F32))

        class Ctr:
            def __init__(self, name):
                self.h = es.enter_context(nc.semaphore(name))
                self.n = 0
        cSL = Ctr("cSL"); cSS = Ctr("cSS"); cG = Ctr("cG")
        cT = Ctr("cT"); cV = Ctr("cV"); cC = Ctr("cC")
        cH = Ctr("cH")
        cCC = Ctr("cCC")
        cW2 = Ctr("cW2")

        PROG = []

        def em(eng, fn):
            PROG.append((eng, fn))

        def wait(eng, ctr, val=None):
            v = ctr.n if val is None else val
            if v > 0:
                em(eng, lambda e, h=ctr.h, v=v: e.wait_ge(h, v))

        def snap(ctr):
            return (ctr.h, ctr.n)

        def wait_s(eng, s):
            if s is None:
                return
            h, v = s
            if v > 0:
                em(eng, lambda e, h=h, v=v: e.wait_ge(h, v))

        def dma(qeng, ctr, out, in_, transpose=False, slow=False):
            ctr.n += 16
            def f(e, out=out, in_=in_, h=ctr.h, transpose=transpose, slow=slow):
                kw = {}
                if transpose:
                    kw["transpose"] = True
                if slow:
                    kw["allow_slow_non_contiguous"] = True
                e.dma_start(out=out, in_=in_, **kw).then_inc(h, 16)
            em(qeng, f)

        def op(eng, ctr, builder):
            ctr.n += 1
            def f(e, b=builder, h=ctr.h):
                b(e).then_inc(h, 1)
            em(eng, f)

        TE = lambda b: op("tensor", cT, b)
        SE = lambda b: op("scalar", cC, b)

        def VE(b, sp=True):
            if sp or SAFE_DVE:
                wait("vector", cV)
            op("vector", cV, b)

        def barrier():
            for eng in ENGS:
                for c in (cSL, cSS, cG, cT, cV, cC, cH, cW2, cWQ, cBR, *cWF, *cW1, *cGW, *cHW, *cPL, *cPS):
                    wait(eng, c)

        def qd_chain(src_ap, dst_ap, width, p=128):
            nb = width // 16
            big = width >= 384
            src3 = src_ap.rearrange("p (c k) -> p c k", k=16)
            VE(lambda v, a=src3, w=nb: v.tensor_reduce(
                bmS[0:p, 0:w].unsqueeze(2), a, AX.X, ALU.max, apply_absolute_value=True))
            VE(lambda v, w=nb: v.tensor_scalar(
                sclS[0:p, 0:w], bmS[0:p, 0:w], 1.0 / 6, 1e-12, ALU.mult, ALU.max))
            wait("vector", cV)
            VE(lambda v, w=nb: v.reciprocal(rsS[0:p, 0:w], sclS[0:p, 0:w]), sp=False)
            wait("vector", cV)
            xn3 = xnS[0:p, 0:width].rearrange("p (c k) -> p c k", k=16)
            VE(lambda v, a=src3, x=xn3, w=nb: v.tensor_tensor(
                x, a, rsS[0:p, 0:w].unsqueeze(2).broadcast_to([p, w, 16]), ALU.mult), sp=False)
            wait("scalar", cV)
            SE(lambda s, w=width: s.activation(absS[0:p, 0:w], xnS[0:p, 0:w], AF.Abs))
            SE(lambda s, w=width: s.activation(sgnB[0:p, 0:w], xnS[0:p, 0:w], AF.Sign))
            wait("vector", cC)
            A, B = Ib[0], Ib[1]
            VE(lambda v, w=width: v.tensor_scalar(A[0:p, 0:w], absS[0:p, 0:w], MID[0], None, ALU.is_gt))
            for k in (1, 2, 3):
                VE(lambda v, k=k, w=width: v.scalar_tensor_tensor(
                    A[0:p, 0:w], absS[0:p, 0:w], MID[k], A[0:p, 0:w], ALU.is_gt, ALU.add), sp=not big)
            VE(lambda v, w=width: v.tensor_scalar(B[0:p, 0:w], absS[0:p, 0:w], MID[4], None, ALU.is_gt), sp=False)
            VE(lambda v, w=width: v.scalar_tensor_tensor(
                B[0:p, 0:w], absS[0:p, 0:w], MID[5], B[0:p, 0:w], ALU.is_gt, ALU.add), sp=not big)
            VE(lambda v, w=width: v.scalar_tensor_tensor(
                A[0:p, 0:w], B[0:p, 0:w], 2.0, A[0:p, 0:w], ALU.mult, ALU.add), sp=not big)
            VE(lambda v, w=width: v.tensor_scalar(B[0:p, 0:w], absS[0:p, 0:w], MID[6], 4.0, ALU.is_gt, ALU.mult), sp=False)
            VE(lambda v, w=width: v.tensor_tensor(A[0:p, 0:w], A[0:p, 0:w], B[0:p, 0:w], ALU.add), sp=not big)
            VE(lambda v, w=width: v.tensor_tensor(A[0:p, 0:w], A[0:p, 0:w], sgnB[0:p, 0:w], ALU.mult), sp=not big)
            VE(lambda v, w=nb: v.tensor_scalar(sclB[0:p, 0:w], sclS[0:p, 0:w], 0.5, None, ALU.mult), sp=False)
            wait("vector", cV)
            q3 = Ib[0][0:p, 0:width].rearrange("p (c k) -> p c k", k=16)
            d3 = dst_ap.rearrange("p (c k) -> p c k", k=16)
            VE(lambda v, q=q3, d=d3, w=nb: v.tensor_tensor(
                d, q, sclB[0:p, 0:w].unsqueeze(2).broadcast_to([p, w, 16]), ALU.mult), sp=False)

        wqd_jobs = []
        for (r0, rr) in ((0, 128), (128, 128), (256, 128), (384, 128), (512, 128), (640, 64)):
            wqd_jobs.append((wggu_sl, cb1_dr, r0, rr, 0, 1024))
        for (c0, cw) in ((0, 1024), (1024, 1024), (2048, 768)):
            wqd_jobs.append((wd_sl, cb2_dr, 0, 128, c0, cw))
        wqd_state = {"i": 0, "ring": 0, "wfp_free": [None, None], "wqs_free": [None, None]}
        cWQ = Ctr("cWQ")
        cWF = [Ctr(f"cWF{i}") for i in range(2)]
        cW1 = [Ctr(f"cW1_{i}") for i in range(4)]
        cGW = [Ctr(f"cGW{i}") for i in range(4)]
        cHW = [Ctr(f"cHW{i}") for i in range(8)]
        cBR = Ctr("cBR")
        cPL = [Ctr(f"cPL{i}") for i in range(2)]
        cPS = [Ctr(f"cPS{i}") for i in range(2)]

        w2pc_state = {"i": 0}

        def drain_w2pc(k):
            for _ in range(k):
                i = w2pc_state["i"]
                if i >= 64:
                    return
                w2pc_state["i"] = i + 1
                rt_, ch_ = i // 2, i % 2
                s = i % 2
                hb = hqstg[s]
                wait("gpsimd", cPS[s])
                dma("gpsimd", cPL[s], hb[:],
                    W2_d[rt_ * 128:(rt_ + 1) * 128, ch_ * 512:ch_ * 512 + 512])
                wait("sync", cPL[s])
                dma("sync", cPS[s], w2b_dr[rt_ * 128:(rt_ + 1) * 128, ch_ * 512:ch_ * 512 + 512],
                    hb[:])

        def drain_wqd(k):
            for _ in range(k):
                i = wqd_state["i"]
                if i >= len(wqd_jobs):
                    return
                wqd_state["i"] = i + 1
                src_d, dst_d, r0, rr, c0, cw = wqd_jobs[i]
                rg = wqd_state["ring"]; wqd_state["ring"] ^= 1
                wait_s("gpsimd", wqd_state["wfp_free"][rg])
                dma("gpsimd", cWF[rg], wfp[rg][0:rr, 0:cw], src_d[r0:r0 + rr, c0:c0 + cw])
                wait("vector", cWF[rg])
                wait("vector", cWQ)
                for s0 in range(0, cw, 512):
                    sw = min(512, cw - s0)
                    qd_chain(wfp[rg][0:rr, s0:s0 + sw], wqs[rg][0:rr, s0:s0 + sw], sw, p=rr)
                wqd_state["wfp_free"][rg] = snap(cV)
                wait("sync", cV)
                dma("sync", cWQ, dst_d[r0:r0 + rr, c0 // 2:(c0 + cw) // 2],
                    wqs[rg][0:rr, 0:cw].bitcast(F32))


        for j in range(4):
            dma("sync", cSL, maskT[j][:], masks_d[j])
        dma("sync", cSL, onesb[:], ones_d)
        dma("sync", cSL, bq8[:], bq_d.rearrange("(c p) -> p c", p=128), slow=True)
        dma("sync", cSL, bkp[:], bk_d.rearrange("(c p) -> p c", p=128), slow=True)
        dma("sync", cSL, c1p[:], c1_d.rearrange("(c p) -> p c", p=128), slow=True)
        dma("sync", cSL, repA[:], g1_d.unsqueeze(0).broadcast_to([128, 1024]))
        dma("sync", cSL, repB[:], b1_d.unsqueeze(0).broadcast_to([128, 1024]))
        dma("gpsimd", cG, brow[:], bv_d.unsqueeze(0))
        wait("vector", cSL)
        VE(lambda v: v.tensor_scalar(bq8[:], bq8[:], 0.125, None, ALU.mult))

        def ln_stats(src, sq=None):
            if sq is None:
                sq = uS
            VE(lambda v, a=src: v.reduce_sum(lnS[:, 0:1], a, AX.X))
            wait("scalar", cV)
            SE(lambda s, a=src, sq=sq: s.activation(sq[:], a, AF.Square))
            wait("vector", cC)
            VE(lambda v, sq=sq: v.reduce_sum(lnS[:, 1:2], sq[:], AX.X))
            VE(lambda v: v.tensor_scalar(lnS[:, 2:3], lnS[:, 0:1], 1.0 / D, None, ALU.mult))
            wait("vector", cV)
            VE(lambda v: v.tensor_tensor(lnS[:, 3:4], lnS[:, 2:3], lnS[:, 2:3], ALU.mult))
            VE(lambda v: v.scalar_tensor_tensor(
                lnS[:, 4:5], lnS[:, 1:2], 1.0 / D, lnS[:, 3:4], ALU.mult, ALU.subtract))
            VE(lambda v: v.tensor_scalar(lnS[:, 4:5], lnS[:, 4:5], 1e-5, None, ALU.add))
            wait("scalar", cV)
            SE(lambda s: s.activation(lnS[:, 5:6], lnS[:, 4:5], AF.Sqrt))
            wait("vector", cC)
            VE(lambda v: v.reciprocal(lnS[:, 6:7], lnS[:, 5:6]))
            wait("vector", cV)

        for t in range(NT):
            rg = t % 2
            drain_w2pc(1)
            wait("sync", cC)
            wait("sync", cV)
            dma("sync", cSL, xr[rg][:], x_in[t * 128:(t + 1) * 128, :])
            wait("vector", cSL)
            wait("scalar", cSL)
            ln_stats(xr[rg][:])
            VE(lambda v, t=t: v.tensor_copy(mu1[:, t:t + 1], lnS[:, 2:3]))
            VE(lambda v, t=t: v.tensor_copy(rstd1[:, t:t + 1], lnS[:, 6:7]))
            wait("vector", cV)
            VE(lambda v, t=t: v.scalar_tensor_tensor(
                nmr1[:, t:t + 1], mu1[:, t:t + 1], -1.0, rstd1[:, t:t + 1], ALU.mult, ALU.mult))
            wait("scalar", cV)
            wait("scalar", cSS)
            SE(lambda s, t=t, rg=rg: s.activation(
                stg[rg][:], xr[rg][:], AF.Identity,
                bias=nmr1[:, t:t + 1], scale=rstd1[:, t:t + 1]))
            wait("sync", cC)
            dma("sync", cSS, u_dr[t * 128:(t + 1) * 128, :], stg[rg][:])
        drain_wqd(4)
        barrier()

        u_fm = lambda c: pool[c][:, 0:1280]
        k_fm = lambda c: pool[8 + c][:, 0:1280]
        q_fm = lambda c: pool[16 + c // 2][:, (c % 2) * 1024:(c % 2) * 1024 + 1024]
        ho_fm = lambda c: pool[c][:, 0:1024]

        for half in range(2):
            e0 = 0 if half == 0 else 6
            q0t = half * 8
            qex = 0 if half == 0 else 2
            for c in range(8):
                dma("sync", cSL, u_fm(c),
                    u_dr[e0 * 128:e0 * 128 + 1280, c * 128:(c + 1) * 128], transpose=True)
            wait("gpsimd", cT)
            for kc in range(8):
                dma("gpsimd", cGW[0], wp[kc][:], Wq_d[kc * 128:(kc + 1) * 128, :])
            wait("tensor", cSL)
            wait("tensor", cGW[0])

            pp = 0
            ps_free = {0: None, 1: None}
            for c in range(8):
                for tt in range(2):
                    pi = pp; ps = (PD if pp == 0 else PE); pp ^= 1
                    wait_s("tensor", ps_free[pi])
                    def qmm(t_, c=c, tt=tt, ps=ps, qex=qex):
                        for kc in range(8):
                            ins = t_.matmul(
                                ps[:], wp[kc][:, c * 128:(c + 1) * 128],
                                u_fm(kc)[:, qex * 128 + tt * 512: qex * 128 + tt * 512 + 512],
                                start=(kc == 0), stop=(kc == 7))
                        return ins
                    TE(qmm)
                    mm_done = snap(cT)
                    wait_s("scalar", mm_done)
                    wait("scalar", cV)
                    SE(lambda s, c=c, tt=tt, ps=ps: s.activation(
                        q_fm(c)[:, tt * 512:tt * 512 + 512], ps[:], AF.Identity,
                        bias=bq8[:, c:c + 1], scale=0.125))
                    ps_free[pi] = snap(cC)
            for kc in range(8):
                dma("gpsimd", cGW[1], wp[8 + kc][:], Wk_d[kc * 128:(kc + 1) * 128, :])
            wait("tensor", cGW[1])
            for c in range(8):
                for (t0, tw) in ((0, 512), (512, 512), (1024, 256)):
                    pi = pp; ps = (PD if pp == 0 else PE); pp ^= 1
                    wait_s("tensor", ps_free[pi])
                    def kmm(t_, c=c, t0=t0, tw=tw, ps=ps):
                        for kc in range(8):
                            ins = t_.matmul(
                                ps[:, 0:tw], wp[8 + kc][:, c * 128:(c + 1) * 128],
                                u_fm(kc)[:, t0:t0 + tw], start=(kc == 0), stop=(kc == 7))
                        return ins
                    TE(kmm)
                    mm_done = snap(cT)
                    wait_s("scalar", mm_done)
                    SE(lambda s, c=c, t0=t0, tw=tw, ps=ps: s.activation(
                        k_fm(c)[:, t0:t0 + tw], ps[:, 0:tw], AF.Identity,
                        bias=bkp[:, c:c + 1]))
                    ps_free[pi] = snap(cC)
            wait("gpsimd", cT)
            dma("gpsimd", cGW[2], brow[:], bv_d.unsqueeze(0))
            for kc in range(8):
                dma("gpsimd", cGW[2], wp[kc][:], Wv_d[kc * 128:(kc + 1) * 128, :])
            for bt in range(10):
                VE(lambda v, bt=bt: v.memset(
                    vt[bt][:].rearrange("p (g c) -> p g c", c=65)[:, :, 64:65], 1.0))
            wait("tensor", cGW[2])
            wait("tensor", cV)
            for bt in range(10):
                for dh in range(2):
                    pi = pp; ps = (PD if pp == 0 else PE); pp ^= 1
                    wait_s("tensor", ps_free[pi])
                    def vmm(t_, bt=bt, dh=dh, ps=ps):
                        for kc in range(8):
                            t_.matmul(ps[:], u_fm(kc)[:, bt * 128:(bt + 1) * 128],
                                      wp[kc][:, dh * 512:dh * 512 + 512],
                                      start=(kc == 0), stop=False)
                        return t_.matmul(ps[:], onesb[:],
                                         brow[0:1, dh * 512:dh * 512 + 512],
                                         start=False, stop=True)
                    TE(vmm)
                    mm_done = snap(cT)
                    wait_s("vector", mm_done)
                    VE(lambda v, bt=bt, dh=dh, ps=ps: v.tensor_copy(
                        vt[bt][:].rearrange("p (g c) -> p g c", c=65)[:, dh * 8:dh * 8 + 8, 0:64],
                        ps[:].rearrange("p (g c) -> p g c", c=64)), sp=False)
                    ps_free[pi] = snap(cV)
            drain_wqd(12)
            if half == 0:
                wait("gpsimd", cWQ)
                em("gpsimd", lambda g: g.collective_compute(
                    "AllGather", ALU.bypass, replica_groups=[list(range(8))],
                    ins=[cb1_dr.opt()], outs=[gg1_dr.opt()]).then_inc(cCC.h, 1))
                cCC.n += 1
                em("gpsimd", lambda g: g.collective_compute(
                    "AllGather", ALU.bypass, replica_groups=[list(range(8))],
                    ins=[cb2_dr.opt()], outs=[gg2_dr.opt()]).then_inc(cCC.h, 1))
                cCC.n += 1
            barrier()
            if dump_scratch and half == 1:
                for c in range(8):
                    dma("sync", cSS, q_dbg[c * 128:(c + 1) * 128, :], q_fm(c))
                    dma("sync", cSS, k_dbg[c * 128:(c + 1) * 128, :], k_fm(c))
                for bt in range(10):
                    dma("sync", cSS, v_dbg[bt * 128:(bt + 1) * 128, :], vt[bt][:])
                barrier()

            groups = [(nloc, h) for nloc in range(4) for h in range(H)]

            def g_meta(gi):
                nloc, h = groups[gi]
                n = half * 4 + nloc
                jpres = [j for j in range(6) if 0 <= (2 * (n - 1) + j) < 16]
                return nloc, h, n, jpres

            pa_rd = [None, None]
            ex_rd = [None, None]
            pd_rd_v = None
            pd_rd_c = None
            pe_rd = None
            rrowb_rd = None
            pvs_rd = None
            sc_done = {}; exp_done = {}; msk_done = {}; pv_done = {}
            rep_done = {}; evac_done = {}

            def emit_front(gi):
                nloc, h, n, jpres = g_meta(gi)
                jlo, jhi = jpres[0], jpres[-1] + 1
                pa = PA[gi % 2]
                ex = exb[gi % 2]
                c = h // 2
                rb = (h % 2) * 64
                wait_s("tensor", pa_rd[gi % 2])
                def smm(t_, n=n, c=c, rb=rb, pa=pa, jpres=jpres, e0=e0, nloc=nloc):
                    for j in jpres:
                        bt = 2 * (n - 1) + j - e0
                        ins = t_.matmul(
                            pa[:, j * 256:(j + 1) * 256],
                            k_fm(c)[rb:rb + 64, bt * 128:(bt + 1) * 128],
                            q_fm(c)[rb:rb + 64, nloc * 256:(nloc + 1) * 256],
                            start=True, stop=True)
                    return ins
                TE(smm)
                sc_done[gi] = snap(cT)
                wait_s("scalar", sc_done[gi])
                wait_s("scalar", ex_rd[gi % 2])
                SE(lambda s, pa=pa, ex=ex, jlo=jlo, jhi=jhi: s.activation(
                    ex[:, jlo * 256:jhi * 256], pa[:, jlo * 256:jhi * 256], AF.Exp))
                exp_done[gi] = snap(cC)
                pa_rd[gi % 2] = snap(cC)
                wait_s("vector", exp_done[gi])
                first = True
                for j in jpres:
                    if j in (2, 3):
                        continue
                    mi = {0: 0, 1: 1, 4: 2, 5: 3}[j]
                    VE(lambda v, ex=ex, j=j, mi=mi: v.tensor_tensor(
                        ex[:, j * 256:(j + 1) * 256], ex[:, j * 256:(j + 1) * 256],
                        maskT[mi][:], ALU.mult), sp=first)
                    first = False
                msk_done[gi] = snap(cV)

            def emit_back(gi):
                nonlocal pd_rd_v, pd_rd_c, pe_rd, rrowb_rd, pvs_rd
                nloc, h, n, jpres = g_meta(gi)
                ex = exb[gi % 2]
                c = h // 2
                rb = (h % 2) * 64
                wait_s("tensor", msk_done[gi])
                wait_s("tensor", exp_done[gi])
                wait_s("tensor", pd_rd_v)
                wait_s("tensor", pd_rd_c)
                def pvmm(t_, n=n, h=h, ex=ex, jpres=jpres, e0=e0):
                    for ji, j in enumerate(jpres):
                        bt = 2 * (n - 1) + j - e0
                        ins = t_.matmul(
                            PD[0:65, 0:256], vt[bt][:, h * 65:h * 65 + 65],
                            ex[:, j * 256:(j + 1) * 256],
                            start=(ji == 0), stop=(ji == len(jpres) - 1))
                    return ins
                TE(pvmm)
                pv_done[gi] = snap(cT)
                ex_rd[gi % 2] = snap(cT)
                wait_s("vector", pv_done[gi])
                VE(lambda v: v.reciprocal(rrow[:], PD[64:65, 0:256]))
                pd_rd_v = snap(cV)
                wait_s("vector", rrowb_rd)
                VE(lambda v: v.tensor_scalar(rrowb[:], rrow[:], 1.0, None, ALU.mult))
                rrowb_done = snap(cV)
                wait_s("tensor", rrowb_done)
                wait_s("tensor", pe_rd)
                TE(lambda t_: t_.matmul(PE[0:64, 0:256], onesb[0:1, 0:64], rrowb[:],
                                        start=True, stop=True))
                rep_done[gi] = snap(cT)
                rrowb_rd = snap(cT)
                wait_s("scalar", pv_done[gi])
                wait_s("scalar", pvs_rd)
                SE(lambda s: s.activation(pvS[:], PD[0:64, 0:256], AF.Identity))
                evac_done[gi] = snap(cC)
                pd_rd_c = snap(cC)
                wait_s("vector", evac_done[gi])
                wait_s("vector", rep_done[gi])
                VE(lambda v, c=c, rb=rb, nloc=nloc: v.tensor_tensor(
                    ho_fm(c)[rb:rb + 64, nloc * 256:(nloc + 1) * 256],
                    pvS[:], PE[0:64, 0:256], ALU.mult), sp=False)
                pe_rd = snap(cV)
                pvs_rd = snap(cV)

            emit_front(0)
            for gi in range(len(groups)):
                if gi + 1 < len(groups):
                    emit_front(gi + 1)
                emit_back(gi)
                drain_w2pc(1)
                if gi % 8 == 7:
                    drain_wqd(1)
            barrier()

            if dump_scratch:
                for c in range(8):
                    dma("sync", cSS, ho_dbg[c * 128:(c + 1) * 128, half * 1024:half * 1024 + 1024],
                        ho_fm(c))
            wait("gpsimd", cT)
            dma("gpsimd", cGW[3], brow[:], bo_d.unsqueeze(0))
            for kc in range(8):
                dma("gpsimd", cGW[3], wp[8 + kc][:], Wo_d[kc * 128:(kc + 1) * 128, :])
            wait("tensor", cGW[3])
            pa3_free = [None, None]
            for tl in range(8):
                gt = q0t + tl
                pa = PA[tl % 2]
                rg = tl % 2
                wait("sync", cC)
                dma("sync", cSL, xr[rg][:], x_in[gt * 128:(gt + 1) * 128, :])
                wait_s("tensor", pa3_free[tl % 2])
                def womm(t_, tl=tl, pa=pa):
                    for dh in range(2):
                        for kc in range(8):
                            t_.matmul(pa[:, dh * 512:dh * 512 + 512],
                                      ho_fm(kc)[:, tl * 128:(tl + 1) * 128],
                                      wp[8 + kc][:, dh * 512:dh * 512 + 512],
                                      start=(kc == 0), stop=False)
                        ins = t_.matmul(pa[:, dh * 512:dh * 512 + 512], onesb[:],
                                        brow[0:1, dh * 512:dh * 512 + 512],
                                        start=False, stop=True)
                    return ins
                TE(womm)
                womm_done = snap(cT)
                wait("scalar", cSL)
                wait("scalar", cV)
                SE(lambda s, gt=gt, rg=rg: s.activation(
                    uS[:], xr[rg][:], AF.Identity,
                    bias=nmr1[:, gt:gt + 1], scale=rstd1[:, gt:gt + 1]))
                wait_s("vector", womm_done)
                wait("vector", cC)
                wait("vector", cSS)
                VE(lambda v, pa=pa: v.tensor_tensor(tmpS[:, 0:512], pa[:, 0:512], uS[:, 0:512], ALU.add))
                VE(lambda v, pa=pa: v.tensor_tensor(tmpS[:, 512:1024], pa[:, 512:1024], uS[:, 512:1024], ALU.add), sp=False)
                pa3_free[tl % 2] = snap(cV)
                ln_stats(tmpS[:])
                VE(lambda v: v.tensor_scalar(tmpS[:], tmpS[:], lnS[:, 2:3], lnS[:, 6:7], ALU.subtract, ALU.mult))
                VE(lambda v: v.tensor_tensor(tmpS[:], tmpS[:], repA[:], ALU.mult), sp=False)
                VE(lambda v: v.tensor_tensor(tmpS[:], tmpS[:], repB[:], ALU.add), sp=False)
                wait("scalar", cV)
                wait("scalar", cSS)
                SE(lambda s, rg=rg: s.activation(stg[rg][:], tmpS[:], AF.Identity))
                wait("sync", cV)
                dma("sync", cSS, hf_dr[gt * 128:(gt + 1) * 128, :], tmpS[:])
                wait("sync", cC)
                dma("sync", cSS, hb_dr[gt * 128:(gt + 1) * 128, :], stg[rg][:])
            drain_wqd(6)
            barrier()

        wait("sync", cV)
        dma("sync", cSL, repA[:], g2_d.unsqueeze(0).broadcast_to([128, 1024]))
        dma("sync", cSL, repB[:], b2_d.unsqueeze(0).broadcast_to([128, 1024]))
        wait("gpsimd", cT)
        dma("gpsimd", cBR, brow[:], c2_d.unsqueeze(0))
        drain_w2pc(64)
        wait("sync", cPS[0])
        wait("sync", cPS[1])
        for c in range(8):
            dma("sync", cSL, pool[c][:, 0:2048],
                hb_dr[:, c * 128:(c + 1) * 128], transpose=True)
        wait("tensor", cSL)
        wait("tensor", cBR)

        g_strip = lambda m: pool[8 + m // 4][:, (m % 4) * 512:(m % 4) * 512 + 512]
        f2slots = [(PA[0][:, 0:512], PA[0][:, 512:1024]),
                   (PA[0][:, 1024:1536], PA[1][:, 0:512]),
                   (PA[1][:, 512:1024], PA[1][:, 1024:1536]),
                   (PD[:], PE[:])]

        f2_evac_done = None
        ev1_prev = None
        gu_free = [None, None]
        w1ring_free = [None, None, None, None]
        w2slot_free = [None] * 8
        slots4 = [PA[0][:, 0:512], PA[0][:, 512:1024], PA[0][:, 1024:1536], PA[1][:, 0:512]]
        for tokg in range(4):
            wait("sync", cV)
            wait("sync", cC)
            for ts4 in range(4):
                gt = tokg * 4 + ts4
                dma("sync", cH, xr[ts4][:], hf_dr[gt * 128:(gt + 1) * 128, :])
            h_loaded = cH.n
            for Mcg in range(16):
                w1t = pool[16 + Mcg % 4]
                wait_s("gpsimd", w1ring_free[Mcg % 4])
                for kc in range(8):
                    dma("gpsimd", cW1[Mcg % 4], w1t[:, kc * 256:(kc + 1) * 256],
                        W1_d[kc * 128:(kc + 1) * 128, Mcg * 256:(Mcg + 1) * 256])
                for Mh in range(2):
                    Mc = Mcg * 2 + Mh
                    ps = (PD if Mc % 2 == 0 else PE)
                    if Mh == 0:
                        wait("tensor", cW1[Mcg % 4])
                    wait_s("tensor", gu_free[Mc % 2])
                    def w1mm(t_, Mh=Mh, tokg=tokg, ps=ps, w1t=w1t):
                        for kc in range(8):
                            ins = t_.matmul(
                                ps[:], w1t[:, kc * 256 + Mh * 128: kc * 256 + Mh * 128 + 128],
                                pool[kc][:, tokg * 512:tokg * 512 + 512],
                                start=(kc == 0), stop=(kc == 7))
                        return ins
                    TE(w1mm)
                    w1_done = snap(cT)
                    wait_s("scalar", w1_done)
                    SE(lambda s, Mc=Mc, ps=ps: s.activation(
                        g_strip(Mc), ps[:], AF.Gelu_apprx_tanh, bias=c1p[:, Mc:Mc + 1]))
                    gu_free[Mc % 2] = snap(cC)
                w1ring_free[Mcg % 4] = snap(cT)
            for dh in range(2):
                for Mc in range(32):
                    w2t = wp[8 + Mc % 8]
                    wait_s("sync", w2slot_free[Mc % 8])
                    dma("sync", cHW[Mc % 8], w2t[:, 0:512],
                        w2b_dr[Mc * 128:(Mc + 1) * 128, dh * 512:dh * 512 + 512])
                    wait("tensor", cHW[Mc % 8])
                    if Mc == 0:
                        wait_s("tensor", gu_free[0])
                        wait_s("tensor", gu_free[1])
                        if dh == 0:
                            wait_s("tensor", ev1_prev)
                        else:
                            wait_s("tensor", ev0)
                    def w2mm(t_, Mc=Mc, w2t=w2t):
                        for ts4 in range(4):
                            ins = t_.matmul(slots4[ts4],
                                            g_strip(Mc)[:, ts4 * 128:(ts4 + 1) * 128],
                                            w2t[:, 0:512],
                                            start=(Mc == 0), stop=False)
                        return ins
                    TE(w2mm)
                    w2slot_free[Mc % 8] = snap(cT)
                def c2mm(t_, dh=dh):
                    for ts4 in range(4):
                        ins = t_.matmul(slots4[ts4], onesb[:],
                                        brow[0:1, dh * 512:dh * 512 + 512],
                                        start=False, stop=True)
                    return ins
                TE(c2mm)
                te_dh = snap(cT)
                wait_s("vector", te_dh)
                wait("vector", cH, h_loaded)
                for ts4 in range(4):
                    VE(lambda v, ts4=ts4, dh=dh: v.tensor_tensor(
                        xr[ts4][:, dh * 512:dh * 512 + 512], slots4[ts4],
                        xr[ts4][:, dh * 512:dh * 512 + 512], ALU.add), sp=(ts4 == 0))
                if dh == 0:
                    ev0 = snap(cV)
                else:
                    ev1_prev = snap(cV)
            for ts4 in range(4):
                gt = tokg * 4 + ts4
                rg = ts4 % 2
                wait("scalar", cSS)
                wait("vector", cSS)
                ln_stats(xr[ts4][:], sq=uS)
                wait("vector", cSS)
                VE(lambda v, ts4=ts4: v.tensor_scalar(
                    tmpS[:], xr[ts4][:], lnS[:, 2:3], lnS[:, 6:7], ALU.subtract, ALU.mult))
                VE(lambda v: v.tensor_tensor(tmpS[:], tmpS[:], repA[:], ALU.mult), sp=False)
                VE(lambda v: v.tensor_tensor(tmpS[:], tmpS[:], repB[:], ALU.add), sp=False)
                t_done = snap(cV)
                wait("sync", cV)
                dma("sync", cH, xr[ts4][:], x_in[gt * 128:(gt + 1) * 128, :])
                wait("vector", cH)
                VE(lambda v, ts4=ts4: v.tensor_tensor(uS[:], tmpS[:], xr[ts4][:], ALU.add), sp=False)
                wait("sync", cV)
                dma("sync", cSS, x1_dr[gt * 128:(gt + 1) * 128, :], uS[:])
                ln_stats(uS[:], sq=xr[ts4])
                wait("vector", cSS)
                VE(lambda v: v.tensor_scalar(
                    tmpS[:], uS[:], lnS[:, 2:3], lnS[:, 6:7], ALU.subtract, ALU.mult))
                for ch in range(2):
                    qd_chain(tmpS[:, ch * 512:ch * 512 + 512], stg[rg][:, ch * 512:ch * 512 + 512], 512)
                wait("sync", cV)
                dma("sync", cSS, xq_dr[gt * 128:(gt + 1) * 128, :], stg[rg][:])
            drain_wqd(6)
        barrier()

        drain_wqd(len(wqd_jobs))
        barrier()
        wait("sync", cCC)
        if dump_scratch:
            dma("sync", cSS, gg1_dbg, gg1_dr)
            dma("sync", cSS, gg2_dbg, gg2_dr)
        for c in range(8):
            dma("sync", cSL, pool[c][:, 0:2048],
                xq_dr[:, c * 128:(c + 1) * 128], transpose=True)
        wait("tensor", cSL)

        NCHW = [(0, 512), (512, 512), (1024, 512), (1536, 512), (2048, 512), (2560, 256)]
        gu_slots = [(PD[:], PE[:]), (PA[0][:, 0:512], PA[0][:, 512:1024])]
        gu5_free = [(None, None), (None, None)]
        sg_rd = None
        slab_mm_done = [None, None]
        slab_i = 0
        for tokg in range(4):
            for (n0, nw) in NCHW:
                p5 = slab_i % 2
                slab_i += 1
                wait_s("sync", slab_mm_done[p5])
                for kc in range(8):
                    dma("sync", cHW[p5], wp[kc][:, p5 * 512:p5 * 512 + nw],
                        wgq_dr[n0:n0 + nw, kc * 128:(kc + 1) * 128], transpose=True)
                    dma("sync", cHW[p5], wp[8 + kc][:, p5 * 512:p5 * 512 + nw],
                        wuq_dr[n0:n0 + nw, kc * 128:(kc + 1) * 128], transpose=True)
                wait("tensor", cHW[p5])
                for ti in range(4):
                    tokc = tokg * 4 + ti
                    pg, pu = gu_slots[ti % 2]
                    wait_s("tensor", gu5_free[ti % 2][0])
                    wait_s("tensor", gu5_free[ti % 2][1])
                    def gumm(t_, tokc=tokc, nw=nw, pg=pg, pu=pu, p5=p5):
                        for kc in range(8):
                            t_.matmul(pg[:, 0:nw],
                                      pool[kc][:, tokc * 128:(tokc + 1) * 128],
                                      wp[kc][:, p5 * 512:p5 * 512 + nw], start=(kc == 0), stop=(kc == 7))
                        for kc in range(8):
                            ins = t_.matmul(pu[:, 0:nw],
                                            pool[kc][:, tokc * 128:(tokc + 1) * 128],
                                            wp[8 + kc][:, p5 * 512:p5 * 512 + nw], start=(kc == 0), stop=(kc == 7))
                        return ins
                    TE(gumm)
                    gu_done = snap(cT)
                    wait_s("scalar", gu_done)
                    wait_s("scalar", sg_rd)
                    SE(lambda s, pg=pg, nw=nw: s.activation(sgS[:, 0:nw], pg[:, 0:nw], AF.Silu))
                    silu_done = snap(cC)
                    wait_s("vector", silu_done)
                    wait_s("vector", gu_done)
                    wait("vector", cSS)
                    VE(lambda v, pu=pu, nw=nw: v.tensor_tensor(
                        tmpS[:, 0:nw], sgS[:, 0:nw], pu[:, 0:nw], ALU.mult), sp=False)
                    sg_rd = snap(cV)
                    gu5_free[ti % 2] = (silu_done, sg_rd)
                    hrg = ti % 2
                    qd_chain(tmpS[:, 0:nw], hqstg[hrg][:, 0:nw], nw)
                    wait("sync", cV)
                    dma("sync", cSS, hq_dr[tokc * 128:(tokc + 1) * 128, n0:n0 + nw],
                        hqstg[hrg][:, 0:nw])
                slab_mm_done[p5] = snap(cT)
        barrier()

        def wdt_tile(kc):
            if kc < 16:
                return wp[kc][:, 0:1024]
            u = pool[(kc - 16) // 2]
            return u[:, ((kc - 16) % 2) * 1024:((kc - 16) % 2) * 1024 + 1024]

        for kc in range(22):
            dma("sync", cHW[4], wdt_tile(kc), wdq_dr[:, kc * 128:(kc + 1) * 128],
                transpose=True)
        wait("tensor", cHW[4])

        def hq_ring(st, kc):
            u = pool[8 + st * 6 + kc // 4]
            return u[:, (kc % 4) * 512:(kc % 4) * 512 + 512]

        dslots = [(PA[0][:, 0:512], PA[0][:, 512:1024]),
                  (PA[0][:, 1024:1536], PA[1][:, 0:512]),
                  (PA[1][:, 512:1024], PA[1][:, 1024:1536]),
                  (PD[:], PE[:])]
        dw_done = [None, None]
        out_ev = [None] * 4
        for tokg in range(4):
            st = tokg % 2
            wait_s("sync", dw_done[st])
            for kc in range(22):
                dma("sync", cHW[2 + st], hq_ring(st, kc),
                    hq_dr[tokg * 512:(tokg + 1) * 512, kc * 128:(kc + 1) * 128], transpose=True)
            wait("tensor", cHW[2 + st])
            for ti in range(4):
                wait_s("tensor", out_ev[ti])
                def dwmm(t_, st=st, ti=ti):
                    for kc in range(22):
                        for dh in range(2):
                            ins = t_.matmul(
                                dslots[ti][dh],
                                hq_ring(st, kc)[:, ti * 128:(ti + 1) * 128],
                                wdt_tile(kc)[:, dh * 512:dh * 512 + 512],
                                start=(kc == 0), stop=(kc == 21))
                    return ins
                TE(dwmm)
                dmm_done = snap(cT)
                gt = tokg * 4 + ti
                rg = ti % 2
                wait("sync", cV)
                dma("sync", cH, xr[rg][:], x1_dr[gt * 128:(gt + 1) * 128, :])
                wait_s("vector", dmm_done)
                wait("vector", cH)
                wait("vector", cSS)
                VE(lambda v, ti=ti, rg=rg: v.tensor_tensor(
                    uS[:, 0:512], dslots[ti][0], xr[rg][:, 0:512], ALU.add))
                VE(lambda v, ti=ti, rg=rg: v.tensor_tensor(
                    uS[:, 512:1024], dslots[ti][1], xr[rg][:, 512:1024], ALU.add), sp=False)
                out_ev[ti] = snap(cV)
                wait("sync", cV)
                dma("sync", cSS, out_d[gt * 128:(gt + 1) * 128, :], uS[:])
            dw_done[st] = snap(cT)
        wait("sync", cSS)

        with nc.Block() as block:
            @block.sync
            def _(e):
                for eng, fn in PROG:
                    if eng == "sync":
                        fn(e)

            @block.gpsimd
            def _(e):
                for eng, fn in PROG:
                    if eng == "gpsimd":
                        fn(e)

            @block.tensor
            def _(e):
                for eng, fn in PROG:
                    if eng == "tensor":
                        fn(e)

            @block.vector
            def _(e):
                for eng, fn in PROG:
                    if eng == "vector":
                        fn(e)

            @block.scalar
            def _(e):
                for eng, fn in PROG:
                    if eng == "scalar":
                        fn(e)
    return nc


def _masks():
    r = np.arange(128)[:, None]
    q = np.arange(256)[None, :]
    m0 = (q <= r).astype(ml_dtypes.bfloat16)
    m1 = (q <= r + 128).astype(ml_dtypes.bfloat16)
    m4 = (q >= r).astype(ml_dtypes.bfloat16)
    m5 = (q >= r + 128).astype(ml_dtypes.bfloat16)
    return [m0, m1, m4, m5]


def _in_maps(inputs):
    masks = _masks()
    ones = np.ones((1, 128), dtype=ml_dtypes.bfloat16)
    base = {
        "Wq": inputs["L0_attn_Wq"], "bq": inputs["L0_attn_bq"],
        "Wk": inputs["L0_attn_Wk"], "bk": inputs["L0_attn_bk"],
        "Wv": inputs["L0_attn_Wv"], "bv": inputs["L0_attn_bv"],
        "Wo": inputs["L0_attn_Wo"], "bo": inputs["L0_attn_bo"],
        "g1": inputs["L0_attn_g1"], "b1": inputs["L0_attn_b1"],
        "W1": inputs["L0_attn_W1"], "c1": inputs["L0_attn_c1"],
        "W2": inputs["L0_attn_W2"], "c2": inputs["L0_attn_c2"],
        "g2": inputs["L0_attn_g2"], "b2": inputs["L0_attn_b2"],
        "ones_row": ones,
    }
    base = {k: np.ascontiguousarray(np.asarray(v, dtype=np.float32))
            if k not in ("ones_row",) else v for k, v in base.items()}
    for j in range(4):
        base[f"mask{j}"] = masks[j]
    x = np.ascontiguousarray(np.asarray(inputs["L0_attn_x"], dtype=np.float32))
    wggu = np.concatenate([np.asarray(inputs["L0_ffn_w_gate"], dtype=np.float32),
                           np.asarray(inputs["L0_ffn_w_up"], dtype=np.float32)], axis=0)
    wd = np.asarray(inputs["L0_ffn_w_down"], dtype=np.float32)
    in_maps = []
    for c in range(8):
        m = dict(base)
        m["x"] = np.ascontiguousarray(x[c])
        m["wggu_sl"] = np.ascontiguousarray(wggu[c * 704:(c + 1) * 704])
        m["wd_sl"] = np.ascontiguousarray(wd[c * 128:(c + 1) * 128])
        in_maps.append(m)
    return in_maps


def kernel(**inputs):
    nc = build()
    res = run_bass_kernel_spmd(nc, _in_maps(inputs), core_ids=list(range(8)))
    return np.stack([np.asarray(r["out"]) for r in res.results], axis=0)


# version 15
# speedup vs baseline: 1.1790x; 1.0251x over previous
import contextlib
import numpy as np
import ml_dtypes
import concourse.bass as bass
import concourse.mybir as mybir
from concourse.bass_utils import run_bass_kernel_spmd

F32 = mybir.dt.float32
BF16 = mybir.dt.bfloat16
AF = mybir.ActivationFunctionType
ALU = mybir.AluOpType
AX = mybir.AxisListType

S, D, H, HD, WB, NB, FFN, NI = 2048, 1024, 16, 64, 256, 8, 4096, 2816
NT = S // 128
MID = [0.25, 0.75, 1.25, 1.75, 2.5, 3.5, 5.0]
ENGS = ("sync", "gpsimd", "tensor", "vector", "scalar")
SAFE_DVE = False


def build(dump_scratch=False):
    nc = bass.Bass("TRN2", target_bir_lowering=False)

    def din(name, shape):
        return nc.dram_tensor(name, shape, F32, kind="ExternalInput").ap()

    x_in = din("x", [S, D])
    Wq_d, Wk_d, Wv_d, Wo_d = (din(n, [D, D]) for n in ("Wq", "Wk", "Wv", "Wo"))
    bq_d, bk_d, bv_d, bo_d = (din(n, [D]) for n in ("bq", "bk", "bv", "bo"))
    g1_d, b1_d, g2_d, b2_d = (din(n, [D]) for n in ("g1", "b1", "g2", "b2"))
    W1_d = din("W1", [D, FFN]); c1_d = din("c1", [FFN])
    W2_d = din("W2", [FFN, D]); c2_d = din("c2", [D])
    wggu_sl = din("wggu_sl", [704, D])
    wd_sl = din("wd_sl", [128, NI])
    masks_d = [nc.dram_tensor(f"mask{j}", [128, 256], BF16, kind="ExternalInput").ap()
               for j in range(4)]
    ones_d = nc.dram_tensor("ones_row", [1, 128], BF16, kind="ExternalInput").ap()
    out_d = nc.dram_tensor("out", [S, D], F32, kind="ExternalOutput").ap()

    sk = "ExternalOutput" if dump_scratch else "Internal"
    u_dr = nc.dram_tensor("u_dr", [S, D], BF16, kind=sk).ap()
    hf_dr = nc.dram_tensor("hf_dr", [S, D], F32, kind=sk).ap()
    hb_dr = nc.dram_tensor("hb_dr", [S, D], BF16, kind=sk).ap()
    x1_dr = nc.dram_tensor("x1_dr", [S, D], F32, kind=sk).ap()
    xq_dr = nc.dram_tensor("xq_dr", [S, D], BF16, kind=sk).ap()
    hq_dr = nc.dram_tensor("hq_dr", [S, NI], BF16, kind=sk).ap()
    ho_dbg = nc.dram_tensor("ho_dbg", [D, S], BF16, kind=sk).ap()
    q_dbg = nc.dram_tensor("q_dbg", [D, 1024], BF16, kind=sk).ap()
    k_dbg = nc.dram_tensor("k_dbg", [D, 1280], BF16, kind=sk).ap()
    v_dbg = nc.dram_tensor("v_dbg", [1280, 1040], BF16, kind=sk).ap()
    ex_dbg = nc.dram_tensor("ex_dbg", [128, 1536], BF16, kind=sk).ap()
    pv_dbg = nc.dram_tensor("pv_dbg", [64, 256], F32, kind=sk).ap()
    rr_dbg = nc.dram_tensor("rr_dbg", [1, 256], F32, kind=sk).ap()
    rep_dbg = nc.dram_tensor("rep_dbg", [64, 256], F32, kind=sk).ap()
    w2b_dr = nc.dram_tensor("w2b_dr", [FFN, D], BF16, kind="Internal").ap()
    cb1_dr = nc.dram_tensor("cb1_dr", [704, D // 2], F32, kind="Internal").ap()
    cb2_dr = nc.dram_tensor("cb2_dr", [128, NI // 2], F32, kind="Internal").ap()
    gg1_dr = nc.dram_tensor("gg1_dr", [2 * NI, D // 2], F32, kind="Internal").ap()
    gg2_dr = nc.dram_tensor("gg2_dr", [D, NI // 2], F32, kind="Internal").ap()
    gg1_dbg = nc.dram_tensor("gg1_dbg", [2 * NI, D // 2], F32, kind=sk).ap()
    gg2_dbg = nc.dram_tensor("gg2_dbg", [D, NI // 2], F32, kind=sk).ap()
    wgq_dr = gg1_dr.bitcast(BF16)[0:NI, :]
    wuq_dr = gg1_dr.bitcast(BF16)[NI:2 * NI, :]
    wdq_dr = gg2_dr.bitcast(BF16)

    es = contextlib.ExitStack()
    with es:
        T = lambda name, shape, dt=F32: es.enter_context(nc.sbuf_tensor(name, shape, dt))

        pool = [T(f"pool{i}", [128, 2048], BF16) for i in range(20)]
        wp = [T(f"wp{i}", [128, 1024], BF16) for i in range(16)]
        vt = [T(f"vt{i}", [128, 16 * 65], BF16) for i in range(10)]
        exb = [T(f"exb{i}", [128, 1536], BF16) for i in range(2)]
        xr = [T(f"xr{i}", [128, 1024], F32) for i in range(4)]
        uS = T("uS", [128, 1024], F32)
        tmpS = T("tmpS", [128, 1024], F32)
        xnS = T("xnS", [128, 512], F32)
        absS = T("absS", [128, 512], F32)
        sgS = T("sgS", [128, 512], F32)
        pvS = T("pvS", [64, 256], F32)
        Ib = [T(f"Ib{k}", [128, 512], BF16) for k in range(2)]
        sgnB = T("sgnB", [128, 512], BF16)
        bmS = T("bmS", [128, 64], F32)
        sclS = T("sclS", [128, 64], F32)
        rsS = T("rsS", [128, 64], F32)
        sclB = T("sclB", [128, 64], BF16)
        stg = [T(f"stg{i}", [128, 1024], BF16) for i in range(2)]
        hqstg = [T(f"hqstg{i}", [128, 512], BF16) for i in range(2)]
        repA = T("repA", [128, 1024], F32)
        repB = T("repB", [128, 1024], F32)
        lnS = T("lnS", [128, 8], F32)
        mu1 = T("mu1", [128, NT], F32)
        rstd1 = T("rstd1", [128, NT], F32)
        nmr1 = T("nmr1", [128, NT], F32)
        bq8 = T("bq8", [128, 8], F32)
        bkp = T("bkp", [128, 8], F32)
        c1p = T("c1p", [128, 32], F32)
        brow = T("brow", [1, 1024], BF16)
        onesb = T("onesb", [1, 128], BF16)
        rrow = T("rrow", [1, 256], F32)
        rrowb = T("rrowb", [1, 256], BF16)
        maskT = [T(f"maskT{j}", [128, 256], BF16) for j in range(4)]
        wfp = [T(f"wfp{i}", [128, 1024], F32) for i in range(2)]
        wqs = [T(f"wqs{i}", [128, 1024], BF16) for i in range(2)]

        PA = [es.enter_context(nc.psum_tensor(f"PA{i}", [128, 1536], F32)) for i in range(2)]
        PD = es.enter_context(nc.psum_tensor("PD", [128, 512], F32))
        PE = es.enter_context(nc.psum_tensor("PE", [128, 512], F32))

        class Ctr:
            def __init__(self, name):
                self.h = es.enter_context(nc.semaphore(name))
                self.n = 0
        cSL = Ctr("cSL"); cSS = Ctr("cSS"); cG = Ctr("cG")
        cT = Ctr("cT"); cV = Ctr("cV"); cC = Ctr("cC")
        cH = Ctr("cH")
        cCC = Ctr("cCC")
        cW2 = Ctr("cW2")

        PROG = []

        def em(eng, fn):
            PROG.append((eng, fn))

        def wait(eng, ctr, val=None):
            v = ctr.n if val is None else val
            if v > 0:
                em(eng, lambda e, h=ctr.h, v=v: e.wait_ge(h, v))

        def snap(ctr):
            return (ctr.h, ctr.n)

        def wait_s(eng, s):
            if s is None:
                return
            h, v = s
            if v > 0:
                em(eng, lambda e, h=h, v=v: e.wait_ge(h, v))

        def dma(qeng, ctr, out, in_, transpose=False, slow=False):
            ctr.n += 16
            def f(e, out=out, in_=in_, h=ctr.h, transpose=transpose, slow=slow):
                kw = {}
                if transpose:
                    kw["transpose"] = True
                if slow:
                    kw["allow_slow_non_contiguous"] = True
                e.dma_start(out=out, in_=in_, **kw).then_inc(h, 16)
            em(qeng, f)

        def op(eng, ctr, builder):
            ctr.n += 1
            def f(e, b=builder, h=ctr.h):
                b(e).then_inc(h, 1)
            em(eng, f)

        TE = lambda b: op("tensor", cT, b)
        SE = lambda b: op("scalar", cC, b)

        def VE(b, sp=True):
            if sp or SAFE_DVE:
                wait("vector", cV)
            op("vector", cV, b)

        def barrier():
            for eng in ENGS:
                for c in (cSL, cSS, cG, cT, cV, cC, cH, cW2, cWQ, cBR, *cWF, *cW1, *cGW, *cHW, *cPL, *cPS, *cHQ, *cOS):
                    wait(eng, c)

        def qd_chain(src_ap, dst_ap, width, p=128):
            nb = width // 16
            big = width >= 384
            src3 = src_ap.rearrange("p (c k) -> p c k", k=16)
            VE(lambda v, a=src3, w=nb: v.tensor_reduce(
                bmS[0:p, 0:w].unsqueeze(2), a, AX.X, ALU.max, apply_absolute_value=True))
            VE(lambda v, w=nb: v.tensor_scalar(
                sclS[0:p, 0:w], bmS[0:p, 0:w], 1.0 / 6, 1e-12, ALU.mult, ALU.max))
            wait("vector", cV)
            VE(lambda v, w=nb: v.reciprocal(rsS[0:p, 0:w], sclS[0:p, 0:w]), sp=False)
            wait("vector", cV)
            xn3 = xnS[0:p, 0:width].rearrange("p (c k) -> p c k", k=16)
            VE(lambda v, a=src3, x=xn3, w=nb: v.tensor_tensor(
                x, a, rsS[0:p, 0:w].unsqueeze(2).broadcast_to([p, w, 16]), ALU.mult), sp=False)
            wait("scalar", cV)
            SE(lambda s, w=width: s.activation(absS[0:p, 0:w], xnS[0:p, 0:w], AF.Abs))
            SE(lambda s, w=width: s.activation(sgnB[0:p, 0:w], xnS[0:p, 0:w], AF.Sign))
            wait("vector", cC)
            A, B = Ib[0], Ib[1]
            VE(lambda v, w=width: v.tensor_scalar(A[0:p, 0:w], absS[0:p, 0:w], MID[0], None, ALU.is_gt))
            for k in (1, 2, 3):
                VE(lambda v, k=k, w=width: v.scalar_tensor_tensor(
                    A[0:p, 0:w], absS[0:p, 0:w], MID[k], A[0:p, 0:w], ALU.is_gt, ALU.add), sp=not big)
            VE(lambda v, w=width: v.tensor_scalar(B[0:p, 0:w], absS[0:p, 0:w], MID[4], None, ALU.is_gt), sp=False)
            VE(lambda v, w=width: v.scalar_tensor_tensor(
                B[0:p, 0:w], absS[0:p, 0:w], MID[5], B[0:p, 0:w], ALU.is_gt, ALU.add), sp=not big)
            VE(lambda v, w=width: v.scalar_tensor_tensor(
                A[0:p, 0:w], B[0:p, 0:w], 2.0, A[0:p, 0:w], ALU.mult, ALU.add), sp=not big)
            VE(lambda v, w=width: v.tensor_scalar(B[0:p, 0:w], absS[0:p, 0:w], MID[6], 4.0, ALU.is_gt, ALU.mult), sp=False)
            VE(lambda v, w=width: v.tensor_tensor(A[0:p, 0:w], A[0:p, 0:w], B[0:p, 0:w], ALU.add), sp=not big)
            VE(lambda v, w=width: v.tensor_tensor(A[0:p, 0:w], A[0:p, 0:w], sgnB[0:p, 0:w], ALU.mult), sp=not big)
            VE(lambda v, w=nb: v.tensor_scalar(sclB[0:p, 0:w], sclS[0:p, 0:w], 0.5, None, ALU.mult), sp=False)
            wait("vector", cV)
            q3 = Ib[0][0:p, 0:width].rearrange("p (c k) -> p c k", k=16)
            d3 = dst_ap.rearrange("p (c k) -> p c k", k=16)
            VE(lambda v, q=q3, d=d3, w=nb: v.tensor_tensor(
                d, q, sclB[0:p, 0:w].unsqueeze(2).broadcast_to([p, w, 16]), ALU.mult), sp=False)

        wqd_jobs = []
        for (r0, rr) in ((0, 128), (128, 128), (256, 128), (384, 128), (512, 128), (640, 64)):
            wqd_jobs.append((wggu_sl, cb1_dr, r0, rr, 0, 1024))
        for (c0, cw) in ((0, 1024), (1024, 1024), (2048, 768)):
            wqd_jobs.append((wd_sl, cb2_dr, 0, 128, c0, cw))
        wqd_state = {"i": 0, "ring": 0, "wfp_free": [None, None], "wqs_free": [None, None]}
        cWQ = Ctr("cWQ")
        cWF = [Ctr(f"cWF{i}") for i in range(2)]
        cW1 = [Ctr(f"cW1_{i}") for i in range(4)]
        cGW = [Ctr(f"cGW{i}") for i in range(4)]
        cHW = [Ctr(f"cHW{i}") for i in range(8)]
        cBR = Ctr("cBR")
        cPL = [Ctr(f"cPL{i}") for i in range(2)]
        cPS = [Ctr(f"cPS{i}") for i in range(2)]
        cHQ = [Ctr(f"cHQ{i}") for i in range(4)]
        cOS = [Ctr(f"cOS{i}") for i in range(2)]

        w2pc_state = {"i": 0}

        def drain_w2pc(k):
            for _ in range(k):
                i = w2pc_state["i"]
                if i >= 64:
                    return
                w2pc_state["i"] = i + 1
                rt_, ch_ = i // 2, i % 2
                s = i % 2
                hb = hqstg[s]
                wait("gpsimd", cPS[s])
                dma("gpsimd", cPL[s], hb[:],
                    W2_d[rt_ * 128:(rt_ + 1) * 128, ch_ * 512:ch_ * 512 + 512])
                wait("sync", cPL[s])
                dma("sync", cPS[s], w2b_dr[rt_ * 128:(rt_ + 1) * 128, ch_ * 512:ch_ * 512 + 512],
                    hb[:])

        def drain_wqd(k):
            for _ in range(k):
                i = wqd_state["i"]
                if i >= len(wqd_jobs):
                    return
                wqd_state["i"] = i + 1
                src_d, dst_d, r0, rr, c0, cw = wqd_jobs[i]
                rg = wqd_state["ring"]; wqd_state["ring"] ^= 1
                wait_s("gpsimd", wqd_state["wfp_free"][rg])
                dma("gpsimd", cWF[rg], wfp[rg][0:rr, 0:cw], src_d[r0:r0 + rr, c0:c0 + cw])
                wait("vector", cWF[rg])
                wait("vector", cWQ)
                for s0 in range(0, cw, 512):
                    sw = min(512, cw - s0)
                    qd_chain(wfp[rg][0:rr, s0:s0 + sw], wqs[rg][0:rr, s0:s0 + sw], sw, p=rr)
                wqd_state["wfp_free"][rg] = snap(cV)
                wait("sync", cV)
                dma("sync", cWQ, dst_d[r0:r0 + rr, c0 // 2:(c0 + cw) // 2],
                    wqs[rg][0:rr, 0:cw].bitcast(F32))


        for j in range(4):
            dma("sync", cSL, maskT[j][:], masks_d[j])
        dma("sync", cSL, onesb[:], ones_d)
        dma("sync", cSL, bq8[:], bq_d.rearrange("(c p) -> p c", p=128), slow=True)
        dma("sync", cSL, bkp[:], bk_d.rearrange("(c p) -> p c", p=128), slow=True)
        dma("sync", cSL, c1p[:], c1_d.rearrange("(c p) -> p c", p=128), slow=True)
        dma("sync", cSL, repA[:], g1_d.unsqueeze(0).broadcast_to([128, 1024]))
        dma("sync", cSL, repB[:], b1_d.unsqueeze(0).broadcast_to([128, 1024]))
        dma("gpsimd", cG, brow[:], bv_d.unsqueeze(0))
        wait("vector", cSL)
        VE(lambda v: v.tensor_scalar(bq8[:], bq8[:], 0.125, None, ALU.mult))

        def ln_stats(src, sq=None):
            if sq is None:
                sq = uS
            VE(lambda v, a=src: v.reduce_sum(lnS[:, 0:1], a, AX.X))
            wait("scalar", cV)
            SE(lambda s, a=src, sq=sq: s.activation(sq[:], a, AF.Square))
            wait("vector", cC)
            VE(lambda v, sq=sq: v.reduce_sum(lnS[:, 1:2], sq[:], AX.X))
            VE(lambda v: v.tensor_scalar(lnS[:, 2:3], lnS[:, 0:1], 1.0 / D, None, ALU.mult))
            wait("vector", cV)
            VE(lambda v: v.tensor_tensor(lnS[:, 3:4], lnS[:, 2:3], lnS[:, 2:3], ALU.mult))
            VE(lambda v: v.scalar_tensor_tensor(
                lnS[:, 4:5], lnS[:, 1:2], 1.0 / D, lnS[:, 3:4], ALU.mult, ALU.subtract))
            VE(lambda v: v.tensor_scalar(lnS[:, 4:5], lnS[:, 4:5], 1e-5, None, ALU.add))
            wait("scalar", cV)
            SE(lambda s: s.activation(lnS[:, 5:6], lnS[:, 4:5], AF.Sqrt))
            wait("vector", cC)
            VE(lambda v: v.reciprocal(lnS[:, 6:7], lnS[:, 5:6]))
            wait("vector", cV)

        for t in range(NT):
            rg = t % 2
            drain_w2pc(1)
            wait("sync", cC)
            wait("sync", cV)
            dma("sync", cSL, xr[rg][:], x_in[t * 128:(t + 1) * 128, :])
            wait("vector", cSL)
            wait("scalar", cSL)
            ln_stats(xr[rg][:])
            VE(lambda v, t=t: v.tensor_copy(mu1[:, t:t + 1], lnS[:, 2:3]))
            VE(lambda v, t=t: v.tensor_copy(rstd1[:, t:t + 1], lnS[:, 6:7]))
            wait("vector", cV)
            VE(lambda v, t=t: v.scalar_tensor_tensor(
                nmr1[:, t:t + 1], mu1[:, t:t + 1], -1.0, rstd1[:, t:t + 1], ALU.mult, ALU.mult))
            wait("scalar", cV)
            wait("scalar", cSS)
            SE(lambda s, t=t, rg=rg: s.activation(
                stg[rg][:], xr[rg][:], AF.Identity,
                bias=nmr1[:, t:t + 1], scale=rstd1[:, t:t + 1]))
            wait("sync", cC)
            dma("sync", cSS, u_dr[t * 128:(t + 1) * 128, :], stg[rg][:])
        drain_wqd(4)
        barrier()

        u_fm = lambda c: pool[c][:, 0:1280]
        k_fm = lambda c: pool[8 + c][:, 0:1280]
        q_fm = lambda c: pool[16 + c // 2][:, (c % 2) * 1024:(c % 2) * 1024 + 1024]
        ho_fm = lambda c: pool[c][:, 0:1024]

        for half in range(2):
            e0 = 0 if half == 0 else 6
            q0t = half * 8
            qex = 0 if half == 0 else 2
            for c in range(8):
                dma("sync", cSL, u_fm(c),
                    u_dr[e0 * 128:e0 * 128 + 1280, c * 128:(c + 1) * 128], transpose=True)
            wait("gpsimd", cT)
            for kc in range(8):
                dma("gpsimd", cGW[0], wp[kc][:], Wq_d[kc * 128:(kc + 1) * 128, :])
            wait("tensor", cSL)
            wait("tensor", cGW[0])

            pp = 0
            ps_free = {0: None, 1: None}
            for c in range(8):
                for tt in range(2):
                    pi = pp; ps = (PD if pp == 0 else PE); pp ^= 1
                    wait_s("tensor", ps_free[pi])
                    def qmm(t_, c=c, tt=tt, ps=ps, qex=qex):
                        for kc in range(8):
                            ins = t_.matmul(
                                ps[:], wp[kc][:, c * 128:(c + 1) * 128],
                                u_fm(kc)[:, qex * 128 + tt * 512: qex * 128 + tt * 512 + 512],
                                start=(kc == 0), stop=(kc == 7))
                        return ins
                    TE(qmm)
                    mm_done = snap(cT)
                    wait_s("scalar", mm_done)
                    wait("scalar", cV)
                    SE(lambda s, c=c, tt=tt, ps=ps: s.activation(
                        q_fm(c)[:, tt * 512:tt * 512 + 512], ps[:], AF.Identity,
                        bias=bq8[:, c:c + 1], scale=0.125))
                    ps_free[pi] = snap(cC)
            for kc in range(8):
                dma("gpsimd", cGW[1], wp[8 + kc][:], Wk_d[kc * 128:(kc + 1) * 128, :])
            wait("tensor", cGW[1])
            for c in range(8):
                for (t0, tw) in ((0, 512), (512, 512), (1024, 256)):
                    pi = pp; ps = (PD if pp == 0 else PE); pp ^= 1
                    wait_s("tensor", ps_free[pi])
                    def kmm(t_, c=c, t0=t0, tw=tw, ps=ps):
                        for kc in range(8):
                            ins = t_.matmul(
                                ps[:, 0:tw], wp[8 + kc][:, c * 128:(c + 1) * 128],
                                u_fm(kc)[:, t0:t0 + tw], start=(kc == 0), stop=(kc == 7))
                        return ins
                    TE(kmm)
                    mm_done = snap(cT)
                    wait_s("scalar", mm_done)
                    SE(lambda s, c=c, t0=t0, tw=tw, ps=ps: s.activation(
                        k_fm(c)[:, t0:t0 + tw], ps[:, 0:tw], AF.Identity,
                        bias=bkp[:, c:c + 1]))
                    ps_free[pi] = snap(cC)
            wait("gpsimd", cT)
            dma("gpsimd", cGW[2], brow[:], bv_d.unsqueeze(0))
            for kc in range(8):
                dma("gpsimd", cGW[2], wp[kc][:], Wv_d[kc * 128:(kc + 1) * 128, :])
            for bt in range(10):
                VE(lambda v, bt=bt: v.memset(
                    vt[bt][:].rearrange("p (g c) -> p g c", c=65)[:, :, 64:65], 1.0))
            wait("tensor", cGW[2])
            wait("tensor", cV)
            for bt in range(10):
                for dh in range(2):
                    pi = pp; ps = (PD if pp == 0 else PE); pp ^= 1
                    wait_s("tensor", ps_free[pi])
                    def vmm(t_, bt=bt, dh=dh, ps=ps):
                        for kc in range(8):
                            t_.matmul(ps[:], u_fm(kc)[:, bt * 128:(bt + 1) * 128],
                                      wp[kc][:, dh * 512:dh * 512 + 512],
                                      start=(kc == 0), stop=False)
                        return t_.matmul(ps[:], onesb[:],
                                         brow[0:1, dh * 512:dh * 512 + 512],
                                         start=False, stop=True)
                    TE(vmm)
                    mm_done = snap(cT)
                    wait_s("vector", mm_done)
                    VE(lambda v, bt=bt, dh=dh, ps=ps: v.tensor_copy(
                        vt[bt][:].rearrange("p (g c) -> p g c", c=65)[:, dh * 8:dh * 8 + 8, 0:64],
                        ps[:].rearrange("p (g c) -> p g c", c=64)), sp=False)
                    ps_free[pi] = snap(cV)
            drain_wqd(12)
            if half == 0:
                wait("gpsimd", cWQ)
                em("gpsimd", lambda g: g.collective_compute(
                    "AllGather", ALU.bypass, replica_groups=[list(range(8))],
                    ins=[cb1_dr.opt()], outs=[gg1_dr.opt()]).then_inc(cCC.h, 1))
                cCC.n += 1
                em("gpsimd", lambda g: g.collective_compute(
                    "AllGather", ALU.bypass, replica_groups=[list(range(8))],
                    ins=[cb2_dr.opt()], outs=[gg2_dr.opt()]).then_inc(cCC.h, 1))
                cCC.n += 1
            barrier()
            if dump_scratch and half == 1:
                for c in range(8):
                    dma("sync", cSS, q_dbg[c * 128:(c + 1) * 128, :], q_fm(c))
                    dma("sync", cSS, k_dbg[c * 128:(c + 1) * 128, :], k_fm(c))
                for bt in range(10):
                    dma("sync", cSS, v_dbg[bt * 128:(bt + 1) * 128, :], vt[bt][:])
                barrier()

            groups = [(nloc, h) for nloc in range(4) for h in range(H)]

            def g_meta(gi):
                nloc, h = groups[gi]
                n = half * 4 + nloc
                jpres = [j for j in range(6) if 0 <= (2 * (n - 1) + j) < 16]
                return nloc, h, n, jpres

            pa_rd = [None, None]
            ex_rd = [None, None]
            pd_rd_v = None
            pd_rd_c = None
            pe_rd = None
            rrowb_rd = None
            pvs_rd = None
            sc_done = {}; exp_done = {}; msk_done = {}; pv_done = {}
            rep_done = {}; evac_done = {}

            def emit_front(gi):
                nloc, h, n, jpres = g_meta(gi)
                jlo, jhi = jpres[0], jpres[-1] + 1
                pa = PA[gi % 2]
                ex = exb[gi % 2]
                c = h // 2
                rb = (h % 2) * 64
                wait_s("tensor", pa_rd[gi % 2])
                def smm(t_, n=n, c=c, rb=rb, pa=pa, jpres=jpres, e0=e0, nloc=nloc):
                    for j in jpres:
                        bt = 2 * (n - 1) + j - e0
                        ins = t_.matmul(
                            pa[:, j * 256:(j + 1) * 256],
                            k_fm(c)[rb:rb + 64, bt * 128:(bt + 1) * 128],
                            q_fm(c)[rb:rb + 64, nloc * 256:(nloc + 1) * 256],
                            start=True, stop=True)
                    return ins
                TE(smm)
                sc_done[gi] = snap(cT)
                wait_s("scalar", sc_done[gi])
                wait_s("scalar", ex_rd[gi % 2])
                SE(lambda s, pa=pa, ex=ex, jlo=jlo, jhi=jhi: s.activation(
                    ex[:, jlo * 256:jhi * 256], pa[:, jlo * 256:jhi * 256], AF.Exp))
                exp_done[gi] = snap(cC)
                pa_rd[gi % 2] = snap(cC)
                wait_s("vector", exp_done[gi])
                first = True
                for j in jpres:
                    if j in (2, 3):
                        continue
                    mi = {0: 0, 1: 1, 4: 2, 5: 3}[j]
                    VE(lambda v, ex=ex, j=j, mi=mi: v.tensor_tensor(
                        ex[:, j * 256:(j + 1) * 256], ex[:, j * 256:(j + 1) * 256],
                        maskT[mi][:], ALU.mult), sp=first)
                    first = False
                msk_done[gi] = snap(cV)

            def emit_back(gi):
                nonlocal pd_rd_v, pd_rd_c, pe_rd, rrowb_rd, pvs_rd
                nloc, h, n, jpres = g_meta(gi)
                ex = exb[gi % 2]
                c = h // 2
                rb = (h % 2) * 64
                wait_s("tensor", msk_done[gi])
                wait_s("tensor", exp_done[gi])
                wait_s("tensor", pd_rd_v)
                wait_s("tensor", pd_rd_c)
                def pvmm(t_, n=n, h=h, ex=ex, jpres=jpres, e0=e0):
                    for ji, j in enumerate(jpres):
                        bt = 2 * (n - 1) + j - e0
                        ins = t_.matmul(
                            PD[0:65, 0:256], vt[bt][:, h * 65:h * 65 + 65],
                            ex[:, j * 256:(j + 1) * 256],
                            start=(ji == 0), stop=(ji == len(jpres) - 1))
                    return ins
                TE(pvmm)
                pv_done[gi] = snap(cT)
                ex_rd[gi % 2] = snap(cT)
                wait_s("vector", pv_done[gi])
                VE(lambda v: v.reciprocal(rrow[:], PD[64:65, 0:256]))
                pd_rd_v = snap(cV)
                wait_s("vector", rrowb_rd)
                VE(lambda v: v.tensor_scalar(rrowb[:], rrow[:], 1.0, None, ALU.mult))
                rrowb_done = snap(cV)
                wait_s("tensor", rrowb_done)
                wait_s("tensor", pe_rd)
                TE(lambda t_: t_.matmul(PE[0:64, 0:256], onesb[0:1, 0:64], rrowb[:],
                                        start=True, stop=True))
                rep_done[gi] = snap(cT)
                rrowb_rd = snap(cT)
                wait_s("scalar", pv_done[gi])
                wait_s("scalar", pvs_rd)
                SE(lambda s: s.activation(pvS[:], PD[0:64, 0:256], AF.Identity))
                evac_done[gi] = snap(cC)
                pd_rd_c = snap(cC)
                wait_s("vector", evac_done[gi])
                wait_s("vector", rep_done[gi])
                VE(lambda v, c=c, rb=rb, nloc=nloc: v.tensor_tensor(
                    ho_fm(c)[rb:rb + 64, nloc * 256:(nloc + 1) * 256],
                    pvS[:], PE[0:64, 0:256], ALU.mult), sp=False)
                pe_rd = snap(cV)
                pvs_rd = snap(cV)

            emit_front(0)
            for gi in range(len(groups)):
                if gi + 1 < len(groups):
                    emit_front(gi + 1)
                emit_back(gi)
                drain_w2pc(1)
                if gi % 8 == 7:
                    drain_wqd(1)
            barrier()

            if dump_scratch:
                for c in range(8):
                    dma("sync", cSS, ho_dbg[c * 128:(c + 1) * 128, half * 1024:half * 1024 + 1024],
                        ho_fm(c))
            wait("gpsimd", cT)
            dma("gpsimd", cGW[3], brow[:], bo_d.unsqueeze(0))
            for kc in range(8):
                dma("gpsimd", cGW[3], wp[8 + kc][:], Wo_d[kc * 128:(kc + 1) * 128, :])
            wait("tensor", cGW[3])
            pa3_free = [None, None]
            for tl in range(8):
                gt = q0t + tl
                pa = PA[tl % 2]
                rg = tl % 2
                wait("sync", cC)
                dma("sync", cSL, xr[rg][:], x_in[gt * 128:(gt + 1) * 128, :])
                wait_s("tensor", pa3_free[tl % 2])
                def womm(t_, tl=tl, pa=pa):
                    for dh in range(2):
                        for kc in range(8):
                            t_.matmul(pa[:, dh * 512:dh * 512 + 512],
                                      ho_fm(kc)[:, tl * 128:(tl + 1) * 128],
                                      wp[8 + kc][:, dh * 512:dh * 512 + 512],
                                      start=(kc == 0), stop=False)
                        ins = t_.matmul(pa[:, dh * 512:dh * 512 + 512], onesb[:],
                                        brow[0:1, dh * 512:dh * 512 + 512],
                                        start=False, stop=True)
                    return ins
                TE(womm)
                womm_done = snap(cT)
                wait("scalar", cSL)
                wait("scalar", cV)
                SE(lambda s, gt=gt, rg=rg: s.activation(
                    uS[:], xr[rg][:], AF.Identity,
                    bias=nmr1[:, gt:gt + 1], scale=rstd1[:, gt:gt + 1]))
                wait_s("vector", womm_done)
                wait("vector", cC)
                wait("vector", cSS)
                VE(lambda v, pa=pa: v.tensor_tensor(tmpS[:, 0:512], pa[:, 0:512], uS[:, 0:512], ALU.add))
                VE(lambda v, pa=pa: v.tensor_tensor(tmpS[:, 512:1024], pa[:, 512:1024], uS[:, 512:1024], ALU.add), sp=False)
                pa3_free[tl % 2] = snap(cV)
                ln_stats(tmpS[:])
                VE(lambda v: v.tensor_scalar(tmpS[:], tmpS[:], lnS[:, 2:3], lnS[:, 6:7], ALU.subtract, ALU.mult))
                VE(lambda v: v.tensor_tensor(tmpS[:], tmpS[:], repA[:], ALU.mult), sp=False)
                VE(lambda v: v.tensor_tensor(tmpS[:], tmpS[:], repB[:], ALU.add), sp=False)
                wait("scalar", cV)
                wait("scalar", cSS)
                SE(lambda s, rg=rg: s.activation(stg[rg][:], tmpS[:], AF.Identity))
                wait("sync", cV)
                dma("sync", cSS, hf_dr[gt * 128:(gt + 1) * 128, :], tmpS[:])
                wait("sync", cC)
                dma("sync", cSS, hb_dr[gt * 128:(gt + 1) * 128, :], stg[rg][:])
            drain_wqd(6)
            barrier()

        wait("sync", cV)
        dma("sync", cSL, repA[:], g2_d.unsqueeze(0).broadcast_to([128, 1024]))
        dma("sync", cSL, repB[:], b2_d.unsqueeze(0).broadcast_to([128, 1024]))
        wait("gpsimd", cT)
        dma("gpsimd", cBR, brow[:], c2_d.unsqueeze(0))
        drain_w2pc(64)
        wait("sync", cPS[0])
        wait("sync", cPS[1])
        for c in range(8):
            dma("sync", cSL, pool[c][:, 0:2048],
                hb_dr[:, c * 128:(c + 1) * 128], transpose=True)
        wait("tensor", cSL)
        wait("tensor", cBR)

        g_strip = lambda m: pool[8 + m // 4][:, (m % 4) * 512:(m % 4) * 512 + 512]
        f2slots = [(PA[0][:, 0:512], PA[0][:, 512:1024]),
                   (PA[0][:, 1024:1536], PA[1][:, 0:512]),
                   (PA[1][:, 512:1024], PA[1][:, 1024:1536]),
                   (PD[:], PE[:])]

        f2_evac_done = None
        ev1_prev = None
        gu_free = [None, None]
        w1ring_free = [None, None, None, None]
        w2slot_free = [None] * 8
        slots4 = [PA[0][:, 0:512], PA[0][:, 512:1024], PA[0][:, 1024:1536], PA[1][:, 0:512]]
        for tokg in range(4):
            wait("sync", cV)
            wait("sync", cC)
            for ts4 in range(4):
                gt = tokg * 4 + ts4
                dma("sync", cH, xr[ts4][:], hf_dr[gt * 128:(gt + 1) * 128, :])
            h_loaded = cH.n
            for Mcg in range(16):
                w1t = pool[16 + Mcg % 4]
                wait_s("gpsimd", w1ring_free[Mcg % 4])
                for kc in range(8):
                    dma("gpsimd", cW1[Mcg % 4], w1t[:, kc * 256:(kc + 1) * 256],
                        W1_d[kc * 128:(kc + 1) * 128, Mcg * 256:(Mcg + 1) * 256])
                for Mh in range(2):
                    Mc = Mcg * 2 + Mh
                    ps = (PD if Mc % 2 == 0 else PE)
                    if Mh == 0:
                        wait("tensor", cW1[Mcg % 4])
                    wait_s("tensor", gu_free[Mc % 2])
                    def w1mm(t_, Mh=Mh, tokg=tokg, ps=ps, w1t=w1t):
                        for kc in range(8):
                            ins = t_.matmul(
                                ps[:], w1t[:, kc * 256 + Mh * 128: kc * 256 + Mh * 128 + 128],
                                pool[kc][:, tokg * 512:tokg * 512 + 512],
                                start=(kc == 0), stop=(kc == 7))
                        return ins
                    TE(w1mm)
                    w1_done = snap(cT)
                    wait_s("scalar", w1_done)
                    SE(lambda s, Mc=Mc, ps=ps: s.activation(
                        g_strip(Mc), ps[:], AF.Gelu_apprx_tanh, bias=c1p[:, Mc:Mc + 1]))
                    gu_free[Mc % 2] = snap(cC)
                w1ring_free[Mcg % 4] = snap(cT)
            for dh in range(2):
                for Mc in range(32):
                    w2t = wp[8 + Mc % 8]
                    wait_s("sync", w2slot_free[Mc % 8])
                    dma("sync", cHW[Mc % 8], w2t[:, 0:512],
                        w2b_dr[Mc * 128:(Mc + 1) * 128, dh * 512:dh * 512 + 512])
                    wait("tensor", cHW[Mc % 8])
                    if Mc == 0:
                        wait_s("tensor", gu_free[0])
                        wait_s("tensor", gu_free[1])
                        if dh == 0:
                            wait_s("tensor", ev1_prev)
                        else:
                            wait_s("tensor", ev0)
                    def w2mm(t_, Mc=Mc, w2t=w2t):
                        for ts4 in range(4):
                            ins = t_.matmul(slots4[ts4],
                                            g_strip(Mc)[:, ts4 * 128:(ts4 + 1) * 128],
                                            w2t[:, 0:512],
                                            start=(Mc == 0), stop=False)
                        return ins
                    TE(w2mm)
                    w2slot_free[Mc % 8] = snap(cT)
                def c2mm(t_, dh=dh):
                    for ts4 in range(4):
                        ins = t_.matmul(slots4[ts4], onesb[:],
                                        brow[0:1, dh * 512:dh * 512 + 512],
                                        start=False, stop=True)
                    return ins
                TE(c2mm)
                te_dh = snap(cT)
                wait_s("vector", te_dh)
                wait("vector", cH, h_loaded)
                for ts4 in range(4):
                    VE(lambda v, ts4=ts4, dh=dh: v.tensor_tensor(
                        xr[ts4][:, dh * 512:dh * 512 + 512], slots4[ts4],
                        xr[ts4][:, dh * 512:dh * 512 + 512], ALU.add), sp=(ts4 == 0))
                if dh == 0:
                    ev0 = snap(cV)
                else:
                    ev1_prev = snap(cV)
            for ts4 in range(4):
                gt = tokg * 4 + ts4
                rg = ts4 % 2
                wait("scalar", cSS)
                wait("vector", cSS)
                ln_stats(xr[ts4][:], sq=uS)
                wait("vector", cSS)
                VE(lambda v, ts4=ts4: v.tensor_scalar(
                    tmpS[:], xr[ts4][:], lnS[:, 2:3], lnS[:, 6:7], ALU.subtract, ALU.mult))
                VE(lambda v: v.tensor_tensor(tmpS[:], tmpS[:], repA[:], ALU.mult), sp=False)
                VE(lambda v: v.tensor_tensor(tmpS[:], tmpS[:], repB[:], ALU.add), sp=False)
                t_done = snap(cV)
                wait("sync", cV)
                dma("sync", cH, xr[ts4][:], x_in[gt * 128:(gt + 1) * 128, :])
                wait("vector", cH)
                VE(lambda v, ts4=ts4: v.tensor_tensor(uS[:], tmpS[:], xr[ts4][:], ALU.add), sp=False)
                wait("sync", cV)
                dma("sync", cSS, x1_dr[gt * 128:(gt + 1) * 128, :], uS[:])
                ln_stats(uS[:], sq=xr[ts4])
                wait("vector", cSS)
                VE(lambda v: v.tensor_scalar(
                    tmpS[:], uS[:], lnS[:, 2:3], lnS[:, 6:7], ALU.subtract, ALU.mult))
                for ch in range(2):
                    qd_chain(tmpS[:, ch * 512:ch * 512 + 512], stg[rg][:, ch * 512:ch * 512 + 512], 512)
                wait("sync", cV)
                dma("sync", cSS, xq_dr[gt * 128:(gt + 1) * 128, :], stg[rg][:])
            drain_wqd(6)
        barrier()

        drain_wqd(len(wqd_jobs))
        barrier()
        wait("sync", cCC)
        if dump_scratch:
            dma("sync", cSS, gg1_dbg, gg1_dr)
            dma("sync", cSS, gg2_dbg, gg2_dr)
        for c in range(8):
            dma("sync", cSL, pool[c][:, 0:2048],
                xq_dr[:, c * 128:(c + 1) * 128], transpose=True)
        wait("tensor", cSL)

        NCHW = [(0, 512), (512, 512), (1024, 512), (1536, 512), (2048, 512), (2560, 256)]
        gu_slots = [(PD[:], PE[:]), (PA[0][:, 0:512], PA[0][:, 512:1024])]
        gu5_free = [(None, None), (None, None)]
        sg_rd = None
        slab_mm_done = [None, None]
        slab_i = 0
        for tokg in range(4):
            for (n0, nw) in NCHW:
                p5 = slab_i % 2
                slab_i += 1
                wait_s("sync", slab_mm_done[p5])
                for kc in range(8):
                    dma("sync", cHW[p5], wp[kc][:, p5 * 512:p5 * 512 + nw],
                        wgq_dr[n0:n0 + nw, kc * 128:(kc + 1) * 128], transpose=True)
                    dma("sync", cHW[p5], wp[8 + kc][:, p5 * 512:p5 * 512 + nw],
                        wuq_dr[n0:n0 + nw, kc * 128:(kc + 1) * 128], transpose=True)
                wait("tensor", cHW[p5])
                for ti in range(4):
                    tokc = tokg * 4 + ti
                    pg, pu = gu_slots[ti % 2]
                    wait_s("tensor", gu5_free[ti % 2][0])
                    wait_s("tensor", gu5_free[ti % 2][1])
                    def gumm(t_, tokc=tokc, nw=nw, pg=pg, pu=pu, p5=p5):
                        for kc in range(8):
                            t_.matmul(pg[:, 0:nw],
                                      pool[kc][:, tokc * 128:(tokc + 1) * 128],
                                      wp[kc][:, p5 * 512:p5 * 512 + nw], start=(kc == 0), stop=(kc == 7))
                        for kc in range(8):
                            ins = t_.matmul(pu[:, 0:nw],
                                            pool[kc][:, tokc * 128:(tokc + 1) * 128],
                                            wp[8 + kc][:, p5 * 512:p5 * 512 + nw], start=(kc == 0), stop=(kc == 7))
                        return ins
                    TE(gumm)
                    gu_done = snap(cT)
                    wait_s("scalar", gu_done)
                    wait_s("scalar", sg_rd)
                    SE(lambda s, pg=pg, nw=nw: s.activation(sgS[:, 0:nw], pg[:, 0:nw], AF.Silu))
                    silu_done = snap(cC)
                    hrg = ti % 4
                    hqb = (hqstg[hrg] if hrg < 2 else stg[hrg - 2])
                    wait_s("vector", silu_done)
                    wait_s("vector", gu_done)
                    wait("vector", cHQ[hrg])
                    VE(lambda v, pu=pu, nw=nw: v.tensor_tensor(
                        tmpS[:, 0:nw], sgS[:, 0:nw], pu[:, 0:nw], ALU.mult), sp=False)
                    sg_rd = snap(cV)
                    gu5_free[ti % 2] = (silu_done, sg_rd)
                    qd_chain(tmpS[:, 0:nw], hqb[:, 0:nw], nw)
                    wait("sync", cV)
                    dma("sync", cHQ[hrg], hq_dr[tokc * 128:(tokc + 1) * 128, n0:n0 + nw],
                        hqb[:, 0:nw])
                slab_mm_done[p5] = snap(cT)
        barrier()

        def wdt_tile(kc):
            if kc < 16:
                return wp[kc][:, 0:1024]
            u = pool[(kc - 16) // 2]
            return u[:, ((kc - 16) % 2) * 1024:((kc - 16) % 2) * 1024 + 1024]

        for kc in range(22):
            dma("sync", cHW[4], wdt_tile(kc), wdq_dr[:, kc * 128:(kc + 1) * 128],
                transpose=True)
        wait("tensor", cHW[4])

        def hq_ring(st, kc):
            u = pool[8 + st * 6 + kc // 4]
            return u[:, (kc % 4) * 512:(kc % 4) * 512 + 512]

        dslots = [(PA[0][:, 0:512], PA[0][:, 512:1024]),
                  (PA[0][:, 1024:1536], PA[1][:, 0:512]),
                  (PA[1][:, 512:1024], PA[1][:, 1024:1536]),
                  (PD[:], PE[:])]
        dw_done = [None, None]
        out_ev = [None] * 4
        for tokg in range(4):
            st = tokg % 2
            wait_s("sync", dw_done[st])
            for kc in range(22):
                dma("sync", cHW[2 + st], hq_ring(st, kc),
                    hq_dr[tokg * 512:(tokg + 1) * 512, kc * 128:(kc + 1) * 128], transpose=True)
            wait("tensor", cHW[2 + st])
            for ti in range(4):
                wait_s("tensor", out_ev[ti])
                def dwmm(t_, st=st, ti=ti):
                    for kc in range(22):
                        for dh in range(2):
                            ins = t_.matmul(
                                dslots[ti][dh],
                                hq_ring(st, kc)[:, ti * 128:(ti + 1) * 128],
                                wdt_tile(kc)[:, dh * 512:dh * 512 + 512],
                                start=(kc == 0), stop=(kc == 21))
                    return ins
                TE(dwmm)
                dmm_done = snap(cT)
                gt = tokg * 4 + ti
                rg = ti % 2
                og = gt % 2
                ob = uS if og == 0 else tmpS
                wait("sync", cV)
                dma("sync", cH, xr[rg][:], x1_dr[gt * 128:(gt + 1) * 128, :])
                wait_s("vector", dmm_done)
                wait("vector", cH)
                wait("vector", cOS[og])
                VE(lambda v, ti=ti, rg=rg, ob=ob: v.tensor_tensor(
                    ob[:, 0:512], dslots[ti][0], xr[rg][:, 0:512], ALU.add))
                VE(lambda v, ti=ti, rg=rg, ob=ob: v.tensor_tensor(
                    ob[:, 512:1024], dslots[ti][1], xr[rg][:, 512:1024], ALU.add), sp=False)
                out_ev[ti] = snap(cV)
                wait("sync", cV)
                dma("sync", cOS[og], out_d[gt * 128:(gt + 1) * 128, :], ob[:])
            dw_done[st] = snap(cT)
        wait("sync", cSS)
        wait("sync", cOS[0])
        wait("sync", cOS[1])

        with nc.Block() as block:
            @block.sync
            def _(e):
                for eng, fn in PROG:
                    if eng == "sync":
                        fn(e)

            @block.gpsimd
            def _(e):
                for eng, fn in PROG:
                    if eng == "gpsimd":
                        fn(e)

            @block.tensor
            def _(e):
                for eng, fn in PROG:
                    if eng == "tensor":
                        fn(e)

            @block.vector
            def _(e):
                for eng, fn in PROG:
                    if eng == "vector":
                        fn(e)

            @block.scalar
            def _(e):
                for eng, fn in PROG:
                    if eng == "scalar":
                        fn(e)
    return nc


def _masks():
    r = np.arange(128)[:, None]
    q = np.arange(256)[None, :]
    m0 = (q <= r).astype(ml_dtypes.bfloat16)
    m1 = (q <= r + 128).astype(ml_dtypes.bfloat16)
    m4 = (q >= r).astype(ml_dtypes.bfloat16)
    m5 = (q >= r + 128).astype(ml_dtypes.bfloat16)
    return [m0, m1, m4, m5]


def _in_maps(inputs):
    masks = _masks()
    ones = np.ones((1, 128), dtype=ml_dtypes.bfloat16)
    base = {
        "Wq": inputs["L0_attn_Wq"], "bq": inputs["L0_attn_bq"],
        "Wk": inputs["L0_attn_Wk"], "bk": inputs["L0_attn_bk"],
        "Wv": inputs["L0_attn_Wv"], "bv": inputs["L0_attn_bv"],
        "Wo": inputs["L0_attn_Wo"], "bo": inputs["L0_attn_bo"],
        "g1": inputs["L0_attn_g1"], "b1": inputs["L0_attn_b1"],
        "W1": inputs["L0_attn_W1"], "c1": inputs["L0_attn_c1"],
        "W2": inputs["L0_attn_W2"], "c2": inputs["L0_attn_c2"],
        "g2": inputs["L0_attn_g2"], "b2": inputs["L0_attn_b2"],
        "ones_row": ones,
    }
    base = {k: np.ascontiguousarray(np.asarray(v, dtype=np.float32))
            if k not in ("ones_row",) else v for k, v in base.items()}
    for j in range(4):
        base[f"mask{j}"] = masks[j]
    x = np.ascontiguousarray(np.asarray(inputs["L0_attn_x"], dtype=np.float32))
    wggu = np.concatenate([np.asarray(inputs["L0_ffn_w_gate"], dtype=np.float32),
                           np.asarray(inputs["L0_ffn_w_up"], dtype=np.float32)], axis=0)
    wd = np.asarray(inputs["L0_ffn_w_down"], dtype=np.float32)
    in_maps = []
    for c in range(8):
        m = dict(base)
        m["x"] = np.ascontiguousarray(x[c])
        m["wggu_sl"] = np.ascontiguousarray(wggu[c * 704:(c + 1) * 704])
        m["wd_sl"] = np.ascontiguousarray(wd[c * 128:(c + 1) * 128])
        in_maps.append(m)
    return in_maps


def kernel(**inputs):
    nc = build()
    res = run_bass_kernel_spmd(nc, _in_maps(inputs), core_ids=list(range(8)))
    return np.stack([np.asarray(r["out"]) for r in res.results], axis=0)


# version 16
# speedup vs baseline: 1.2013x; 1.0189x over previous
import contextlib
import numpy as np
import ml_dtypes
import concourse.bass as bass
import concourse.mybir as mybir
from concourse.bass_utils import run_bass_kernel_spmd

F32 = mybir.dt.float32
BF16 = mybir.dt.bfloat16
AF = mybir.ActivationFunctionType
ALU = mybir.AluOpType
AX = mybir.AxisListType

S, D, H, HD, WB, NB, FFN, NI = 2048, 1024, 16, 64, 256, 8, 4096, 2816
NT = S // 128
MID = [0.25, 0.75, 1.25, 1.75, 2.5, 3.5, 5.0]
ENGS = ("sync", "gpsimd", "tensor", "vector", "scalar")
SAFE_DVE = False


def build(dump_scratch=False):
    nc = bass.Bass("TRN2", target_bir_lowering=False)

    def din(name, shape):
        return nc.dram_tensor(name, shape, F32, kind="ExternalInput").ap()

    x_in = din("x", [S, D])
    Wq_d, Wk_d, Wv_d, Wo_d = (din(n, [D, D]) for n in ("Wq", "Wk", "Wv", "Wo"))
    bq_d, bk_d, bv_d, bo_d = (din(n, [D]) for n in ("bq", "bk", "bv", "bo"))
    g1_d, b1_d, g2_d, b2_d = (din(n, [D]) for n in ("g1", "b1", "g2", "b2"))
    W1_d = din("W1", [D, FFN]); c1_d = din("c1", [FFN])
    W2_d = din("W2", [FFN, D]); c2_d = din("c2", [D])
    wggu_sl = din("wggu_sl", [704, D])
    wd_sl = din("wd_sl", [128, NI])
    masks_d = [nc.dram_tensor(f"mask{j}", [128, 256], BF16, kind="ExternalInput").ap()
               for j in range(4)]
    ones_d = nc.dram_tensor("ones_row", [1, 128], BF16, kind="ExternalInput").ap()
    out_d = nc.dram_tensor("out", [S, D], F32, kind="ExternalOutput").ap()

    sk = "ExternalOutput" if dump_scratch else "Internal"
    u_dr = nc.dram_tensor("u_dr", [S, D], BF16, kind=sk).ap()
    hf_dr = nc.dram_tensor("hf_dr", [S, D], F32, kind=sk).ap()
    hb_dr = nc.dram_tensor("hb_dr", [S, D], BF16, kind=sk).ap()
    x1_dr = nc.dram_tensor("x1_dr", [S, D], F32, kind=sk).ap()
    xq_dr = nc.dram_tensor("xq_dr", [S, D], BF16, kind=sk).ap()
    hq_dr = nc.dram_tensor("hq_dr", [S, NI], BF16, kind=sk).ap()
    ho_dbg = nc.dram_tensor("ho_dbg", [D, S], BF16, kind=sk).ap()
    q_dbg = nc.dram_tensor("q_dbg", [D, 1024], BF16, kind=sk).ap()
    k_dbg = nc.dram_tensor("k_dbg", [D, 1280], BF16, kind=sk).ap()
    v_dbg = nc.dram_tensor("v_dbg", [1280, 1040], BF16, kind=sk).ap()
    ex_dbg = nc.dram_tensor("ex_dbg", [128, 1536], BF16, kind=sk).ap()
    pv_dbg = nc.dram_tensor("pv_dbg", [64, 256], F32, kind=sk).ap()
    rr_dbg = nc.dram_tensor("rr_dbg", [1, 256], F32, kind=sk).ap()
    rep_dbg = nc.dram_tensor("rep_dbg", [64, 256], F32, kind=sk).ap()
    w2b_dr = nc.dram_tensor("w2b_dr", [FFN, D], BF16, kind="Internal").ap()
    cb1_dr = nc.dram_tensor("cb1_dr", [704, D // 2], F32, kind="Internal").ap()
    cb2_dr = nc.dram_tensor("cb2_dr", [128, NI // 2], F32, kind="Internal").ap()
    gg1_dr = nc.dram_tensor("gg1_dr", [2 * NI, D // 2], F32, kind="Internal").ap()
    gg2_dr = nc.dram_tensor("gg2_dr", [D, NI // 2], F32, kind="Internal").ap()
    gg1_dbg = nc.dram_tensor("gg1_dbg", [2 * NI, D // 2], F32, kind=sk).ap()
    gg2_dbg = nc.dram_tensor("gg2_dbg", [D, NI // 2], F32, kind=sk).ap()
    wgq_dr = gg1_dr.bitcast(BF16)[0:NI, :]
    wuq_dr = gg1_dr.bitcast(BF16)[NI:2 * NI, :]
    wdq_dr = gg2_dr.bitcast(BF16)

    es = contextlib.ExitStack()
    with es:
        T = lambda name, shape, dt=F32: es.enter_context(nc.sbuf_tensor(name, shape, dt))

        pool = [T(f"pool{i}", [128, 2048], BF16) for i in range(20)]
        wp = [T(f"wp{i}", [128, 1024], BF16) for i in range(16)]
        vt = [T(f"vt{i}", [128, 16 * 65], BF16) for i in range(10)]
        exb = [T(f"exb{i}", [128, 1536], BF16) for i in range(2)]
        xr = [T(f"xr{i}", [128, 1024], F32) for i in range(4)]
        uS = T("uS", [128, 1024], F32)
        tmpS = T("tmpS", [128, 1024], F32)
        xnS = T("xnS", [128, 512], F32)
        absS = T("absS", [128, 512], F32)
        sgS = T("sgS", [128, 512], F32)
        pvS = T("pvS", [64, 256], F32)
        Ib = [T(f"Ib{k}", [128, 512], BF16) for k in range(2)]
        sgnB = T("sgnB", [128, 512], BF16)
        bmS = T("bmS", [128, 64], F32)
        sclS = T("sclS", [128, 64], F32)
        rsS = T("rsS", [128, 64], F32)
        sclB = T("sclB", [128, 64], BF16)
        stg = [T(f"stg{i}", [128, 1024], BF16) for i in range(2)]
        hqstg = [T(f"hqstg{i}", [128, 512], BF16) for i in range(2)]
        repA = T("repA", [128, 1024], F32)
        repB = T("repB", [128, 1024], F32)
        lnS = T("lnS", [128, 8], F32)
        mu1 = T("mu1", [128, NT], F32)
        rstd1 = T("rstd1", [128, NT], F32)
        nmr1 = T("nmr1", [128, NT], F32)
        bq8 = T("bq8", [128, 8], F32)
        bkp = T("bkp", [128, 8], F32)
        c1p = T("c1p", [128, 32], F32)
        brow = T("brow", [1, 1024], BF16)
        onesb = T("onesb", [1, 128], BF16)
        rrow = T("rrow", [1, 256], F32)
        rrowb = T("rrowb", [1, 256], BF16)
        maskT = [T(f"maskT{j}", [128, 256], BF16) for j in range(4)]
        wfp = [T(f"wfp{i}", [128, 1024], F32) for i in range(2)]
        wqs = [T(f"wqs{i}", [128, 1024], BF16) for i in range(2)]

        PA = [es.enter_context(nc.psum_tensor(f"PA{i}", [128, 1536], F32)) for i in range(2)]
        PD = es.enter_context(nc.psum_tensor("PD", [128, 512], F32))
        PE = es.enter_context(nc.psum_tensor("PE", [128, 512], F32))

        class Ctr:
            def __init__(self, name):
                self.h = es.enter_context(nc.semaphore(name))
                self.n = 0
        cSL = Ctr("cSL"); cSS = Ctr("cSS"); cG = Ctr("cG")
        cT = Ctr("cT"); cV = Ctr("cV"); cC = Ctr("cC")
        cH = Ctr("cH")
        cCC = Ctr("cCC")
        cW2 = Ctr("cW2")

        PROG = []

        def em(eng, fn):
            PROG.append((eng, fn))

        def wait(eng, ctr, val=None):
            v = ctr.n if val is None else val
            if v > 0:
                em(eng, lambda e, h=ctr.h, v=v: e.wait_ge(h, v))

        def snap(ctr):
            return (ctr.h, ctr.n)

        def wait_s(eng, s):
            if s is None:
                return
            h, v = s
            if v > 0:
                em(eng, lambda e, h=h, v=v: e.wait_ge(h, v))

        def dma(qeng, ctr, out, in_, transpose=False, slow=False):
            ctr.n += 16
            def f(e, out=out, in_=in_, h=ctr.h, transpose=transpose, slow=slow):
                kw = {}
                if transpose:
                    kw["transpose"] = True
                if slow:
                    kw["allow_slow_non_contiguous"] = True
                e.dma_start(out=out, in_=in_, **kw).then_inc(h, 16)
            em(qeng, f)

        def op(eng, ctr, builder):
            ctr.n += 1
            def f(e, b=builder, h=ctr.h):
                b(e).then_inc(h, 1)
            em(eng, f)

        TE = lambda b: op("tensor", cT, b)
        SE = lambda b: op("scalar", cC, b)

        def VE(b, sp=True):
            if sp or SAFE_DVE:
                wait("vector", cV)
            op("vector", cV, b)

        def barrier():
            for eng in ENGS:
                for c in (cSL, cSS, cG, cT, cV, cC, cH, cW2, cWQ, cBR, *cWF, *cW1, *cGW, *cHW, *cPL, *cPS, *cHQ, *cOS):
                    wait(eng, c)

        def qd_chain(src_ap, dst_ap, width, p=128):
            nb = width // 16
            big = width >= 384
            src3 = src_ap.rearrange("p (c k) -> p c k", k=16)
            VE(lambda v, a=src3, w=nb: v.tensor_reduce(
                bmS[0:p, 0:w].unsqueeze(2), a, AX.X, ALU.max, apply_absolute_value=True))
            VE(lambda v, w=nb: v.tensor_scalar(
                sclS[0:p, 0:w], bmS[0:p, 0:w], 1.0 / 6, 1e-12, ALU.mult, ALU.max))
            wait("vector", cV)
            VE(lambda v, w=nb: v.reciprocal(rsS[0:p, 0:w], sclS[0:p, 0:w]), sp=False)
            wait("vector", cV)
            xn3 = xnS[0:p, 0:width].rearrange("p (c k) -> p c k", k=16)
            VE(lambda v, a=src3, x=xn3, w=nb: v.tensor_tensor(
                x, a, rsS[0:p, 0:w].unsqueeze(2).broadcast_to([p, w, 16]), ALU.mult), sp=False)
            A, B = Ib[0], Ib[1]
            VE(lambda v, w=width: v.scalar_tensor_tensor(
                absS[0:p, 0:w], xnS[0:p, 0:w], -1.0, xnS[0:p, 0:w], ALU.mult, ALU.max), sp=not big)
            VE(lambda v, w=width: v.tensor_scalar(B[0:p, 0:w], xnS[0:p, 0:w], 0.0, None, ALU.is_gt), sp=not big)
            VE(lambda v, w=width: v.tensor_scalar(sgnB[0:p, 0:w], B[0:p, 0:w], 2.0, -1.0, ALU.mult, ALU.add), sp=not big)
            VE(lambda v, w=width: v.tensor_scalar(A[0:p, 0:w], absS[0:p, 0:w], MID[0], None, ALU.is_gt), sp=not big)
            for k in (1, 2, 3):
                VE(lambda v, k=k, w=width: v.scalar_tensor_tensor(
                    A[0:p, 0:w], absS[0:p, 0:w], MID[k], A[0:p, 0:w], ALU.is_gt, ALU.add), sp=not big)
            VE(lambda v, w=width: v.tensor_scalar(B[0:p, 0:w], absS[0:p, 0:w], MID[4], None, ALU.is_gt), sp=False)
            VE(lambda v, w=width: v.scalar_tensor_tensor(
                B[0:p, 0:w], absS[0:p, 0:w], MID[5], B[0:p, 0:w], ALU.is_gt, ALU.add), sp=not big)
            VE(lambda v, w=width: v.scalar_tensor_tensor(
                A[0:p, 0:w], B[0:p, 0:w], 2.0, A[0:p, 0:w], ALU.mult, ALU.add), sp=not big)
            VE(lambda v, w=width: v.tensor_scalar(B[0:p, 0:w], absS[0:p, 0:w], MID[6], 4.0, ALU.is_gt, ALU.mult), sp=False)
            VE(lambda v, w=width: v.tensor_tensor(A[0:p, 0:w], A[0:p, 0:w], B[0:p, 0:w], ALU.add), sp=not big)
            VE(lambda v, w=width: v.tensor_tensor(A[0:p, 0:w], A[0:p, 0:w], sgnB[0:p, 0:w], ALU.mult), sp=not big)
            VE(lambda v, w=nb: v.tensor_scalar(sclB[0:p, 0:w], sclS[0:p, 0:w], 0.5, None, ALU.mult), sp=False)
            wait("vector", cV)
            q3 = Ib[0][0:p, 0:width].rearrange("p (c k) -> p c k", k=16)
            d3 = dst_ap.rearrange("p (c k) -> p c k", k=16)
            VE(lambda v, q=q3, d=d3, w=nb: v.tensor_tensor(
                d, q, sclB[0:p, 0:w].unsqueeze(2).broadcast_to([p, w, 16]), ALU.mult), sp=False)

        wqd_jobs = []
        for (r0, rr) in ((0, 128), (128, 128), (256, 128), (384, 128), (512, 128), (640, 64)):
            wqd_jobs.append((wggu_sl, cb1_dr, r0, rr, 0, 1024))
        for (c0, cw) in ((0, 1024), (1024, 1024), (2048, 768)):
            wqd_jobs.append((wd_sl, cb2_dr, 0, 128, c0, cw))
        wqd_state = {"i": 0, "ring": 0, "wfp_free": [None, None], "wqs_free": [None, None]}
        cWQ = Ctr("cWQ")
        cWF = [Ctr(f"cWF{i}") for i in range(2)]
        cW1 = [Ctr(f"cW1_{i}") for i in range(4)]
        cGW = [Ctr(f"cGW{i}") for i in range(4)]
        cHW = [Ctr(f"cHW{i}") for i in range(8)]
        cBR = Ctr("cBR")
        cPL = [Ctr(f"cPL{i}") for i in range(2)]
        cPS = [Ctr(f"cPS{i}") for i in range(2)]
        cHQ = [Ctr(f"cHQ{i}") for i in range(4)]
        cOS = [Ctr(f"cOS{i}") for i in range(2)]

        w2pc_state = {"i": 0}

        def drain_w2pc(k):
            for _ in range(k):
                i = w2pc_state["i"]
                if i >= 64:
                    return
                w2pc_state["i"] = i + 1
                rt_, ch_ = i // 2, i % 2
                s = i % 2
                hb = hqstg[s]
                wait("gpsimd", cPS[s])
                dma("gpsimd", cPL[s], hb[:],
                    W2_d[rt_ * 128:(rt_ + 1) * 128, ch_ * 512:ch_ * 512 + 512])
                wait("sync", cPL[s])
                dma("sync", cPS[s], w2b_dr[rt_ * 128:(rt_ + 1) * 128, ch_ * 512:ch_ * 512 + 512],
                    hb[:])

        def drain_wqd(k):
            for _ in range(k):
                i = wqd_state["i"]
                if i >= len(wqd_jobs):
                    return
                wqd_state["i"] = i + 1
                src_d, dst_d, r0, rr, c0, cw = wqd_jobs[i]
                rg = wqd_state["ring"]; wqd_state["ring"] ^= 1
                wait_s("gpsimd", wqd_state["wfp_free"][rg])
                dma("gpsimd", cWF[rg], wfp[rg][0:rr, 0:cw], src_d[r0:r0 + rr, c0:c0 + cw])
                wait("vector", cWF[rg])
                wait("vector", cWQ)
                for s0 in range(0, cw, 512):
                    sw = min(512, cw - s0)
                    qd_chain(wfp[rg][0:rr, s0:s0 + sw], wqs[rg][0:rr, s0:s0 + sw], sw, p=rr)
                wqd_state["wfp_free"][rg] = snap(cV)
                wait("sync", cV)
                dma("sync", cWQ, dst_d[r0:r0 + rr, c0 // 2:(c0 + cw) // 2],
                    wqs[rg][0:rr, 0:cw].bitcast(F32))


        for j in range(4):
            dma("sync", cSL, maskT[j][:], masks_d[j])
        dma("sync", cSL, onesb[:], ones_d)
        dma("sync", cSL, bq8[:], bq_d.rearrange("(c p) -> p c", p=128), slow=True)
        dma("sync", cSL, bkp[:], bk_d.rearrange("(c p) -> p c", p=128), slow=True)
        dma("sync", cSL, c1p[:], c1_d.rearrange("(c p) -> p c", p=128), slow=True)
        dma("sync", cSL, repA[:], g1_d.unsqueeze(0).broadcast_to([128, 1024]))
        dma("sync", cSL, repB[:], b1_d.unsqueeze(0).broadcast_to([128, 1024]))
        dma("gpsimd", cG, brow[:], bv_d.unsqueeze(0))
        wait("vector", cSL)
        VE(lambda v: v.tensor_scalar(bq8[:], bq8[:], 0.125, None, ALU.mult))

        def ln_stats(src, sq=None):
            if sq is None:
                sq = uS
            VE(lambda v, a=src: v.reduce_sum(lnS[:, 0:1], a, AX.X))
            wait("scalar", cV)
            SE(lambda s, a=src, sq=sq: s.activation(sq[:], a, AF.Square))
            wait("vector", cC)
            VE(lambda v, sq=sq: v.reduce_sum(lnS[:, 1:2], sq[:], AX.X))
            VE(lambda v: v.tensor_scalar(lnS[:, 2:3], lnS[:, 0:1], 1.0 / D, None, ALU.mult))
            wait("vector", cV)
            VE(lambda v: v.tensor_tensor(lnS[:, 3:4], lnS[:, 2:3], lnS[:, 2:3], ALU.mult))
            VE(lambda v: v.scalar_tensor_tensor(
                lnS[:, 4:5], lnS[:, 1:2], 1.0 / D, lnS[:, 3:4], ALU.mult, ALU.subtract))
            VE(lambda v: v.tensor_scalar(lnS[:, 4:5], lnS[:, 4:5], 1e-5, None, ALU.add))
            wait("scalar", cV)
            SE(lambda s: s.activation(lnS[:, 5:6], lnS[:, 4:5], AF.Sqrt))
            wait("vector", cC)
            VE(lambda v: v.reciprocal(lnS[:, 6:7], lnS[:, 5:6]))
            wait("vector", cV)

        for t in range(NT):
            rg = t % 2
            drain_w2pc(1)
            wait("sync", cC)
            wait("sync", cV)
            dma("sync", cSL, xr[rg][:], x_in[t * 128:(t + 1) * 128, :])
            wait("vector", cSL)
            wait("scalar", cSL)
            ln_stats(xr[rg][:])
            VE(lambda v, t=t: v.tensor_copy(mu1[:, t:t + 1], lnS[:, 2:3]))
            VE(lambda v, t=t: v.tensor_copy(rstd1[:, t:t + 1], lnS[:, 6:7]))
            wait("vector", cV)
            VE(lambda v, t=t: v.scalar_tensor_tensor(
                nmr1[:, t:t + 1], mu1[:, t:t + 1], -1.0, rstd1[:, t:t + 1], ALU.mult, ALU.mult))
            wait("scalar", cV)
            wait("scalar", cSS)
            SE(lambda s, t=t, rg=rg: s.activation(
                stg[rg][:], xr[rg][:], AF.Identity,
                bias=nmr1[:, t:t + 1], scale=rstd1[:, t:t + 1]))
            wait("sync", cC)
            dma("sync", cSS, u_dr[t * 128:(t + 1) * 128, :], stg[rg][:])
        drain_wqd(4)
        barrier()

        u_fm = lambda c: pool[c][:, 0:1280]
        k_fm = lambda c: pool[8 + c][:, 0:1280]
        q_fm = lambda c: pool[16 + c // 2][:, (c % 2) * 1024:(c % 2) * 1024 + 1024]
        ho_fm = lambda c: pool[c][:, 0:1024]

        for half in range(2):
            e0 = 0 if half == 0 else 6
            q0t = half * 8
            qex = 0 if half == 0 else 2
            for c in range(8):
                dma("sync", cSL, u_fm(c),
                    u_dr[e0 * 128:e0 * 128 + 1280, c * 128:(c + 1) * 128], transpose=True)
            wait("gpsimd", cT)
            for kc in range(8):
                dma("gpsimd", cGW[0], wp[kc][:], Wq_d[kc * 128:(kc + 1) * 128, :])
            wait("tensor", cSL)
            wait("tensor", cGW[0])

            pp = 0
            ps_free = {0: None, 1: None}
            for c in range(8):
                for tt in range(2):
                    pi = pp; ps = (PD if pp == 0 else PE); pp ^= 1
                    wait_s("tensor", ps_free[pi])
                    def qmm(t_, c=c, tt=tt, ps=ps, qex=qex):
                        for kc in range(8):
                            ins = t_.matmul(
                                ps[:], wp[kc][:, c * 128:(c + 1) * 128],
                                u_fm(kc)[:, qex * 128 + tt * 512: qex * 128 + tt * 512 + 512],
                                start=(kc == 0), stop=(kc == 7))
                        return ins
                    TE(qmm)
                    mm_done = snap(cT)
                    wait_s("scalar", mm_done)
                    wait("scalar", cV)
                    SE(lambda s, c=c, tt=tt, ps=ps: s.activation(
                        q_fm(c)[:, tt * 512:tt * 512 + 512], ps[:], AF.Identity,
                        bias=bq8[:, c:c + 1], scale=0.125))
                    ps_free[pi] = snap(cC)
            for kc in range(8):
                dma("gpsimd", cGW[1], wp[8 + kc][:], Wk_d[kc * 128:(kc + 1) * 128, :])
            wait("tensor", cGW[1])
            for c in range(8):
                for (t0, tw) in ((0, 512), (512, 512), (1024, 256)):
                    pi = pp; ps = (PD if pp == 0 else PE); pp ^= 1
                    wait_s("tensor", ps_free[pi])
                    def kmm(t_, c=c, t0=t0, tw=tw, ps=ps):
                        for kc in range(8):
                            ins = t_.matmul(
                                ps[:, 0:tw], wp[8 + kc][:, c * 128:(c + 1) * 128],
                                u_fm(kc)[:, t0:t0 + tw], start=(kc == 0), stop=(kc == 7))
                        return ins
                    TE(kmm)
                    mm_done = snap(cT)
                    wait_s("scalar", mm_done)
                    SE(lambda s, c=c, t0=t0, tw=tw, ps=ps: s.activation(
                        k_fm(c)[:, t0:t0 + tw], ps[:, 0:tw], AF.Identity,
                        bias=bkp[:, c:c + 1]))
                    ps_free[pi] = snap(cC)
            wait("gpsimd", cT)
            dma("gpsimd", cGW[2], brow[:], bv_d.unsqueeze(0))
            for kc in range(8):
                dma("gpsimd", cGW[2], wp[kc][:], Wv_d[kc * 128:(kc + 1) * 128, :])
            for bt in range(10):
                VE(lambda v, bt=bt: v.memset(
                    vt[bt][:].rearrange("p (g c) -> p g c", c=65)[:, :, 64:65], 1.0))
            wait("tensor", cGW[2])
            wait("tensor", cV)
            for bt in range(10):
                for dh in range(2):
                    pi = pp; ps = (PD if pp == 0 else PE); pp ^= 1
                    wait_s("tensor", ps_free[pi])
                    def vmm(t_, bt=bt, dh=dh, ps=ps):
                        for kc in range(8):
                            t_.matmul(ps[:], u_fm(kc)[:, bt * 128:(bt + 1) * 128],
                                      wp[kc][:, dh * 512:dh * 512 + 512],
                                      start=(kc == 0), stop=False)
                        return t_.matmul(ps[:], onesb[:],
                                         brow[0:1, dh * 512:dh * 512 + 512],
                                         start=False, stop=True)
                    TE(vmm)
                    mm_done = snap(cT)
                    wait_s("vector", mm_done)
                    VE(lambda v, bt=bt, dh=dh, ps=ps: v.tensor_copy(
                        vt[bt][:].rearrange("p (g c) -> p g c", c=65)[:, dh * 8:dh * 8 + 8, 0:64],
                        ps[:].rearrange("p (g c) -> p g c", c=64)), sp=False)
                    ps_free[pi] = snap(cV)
            drain_wqd(12)
            if half == 0:
                wait("gpsimd", cWQ)
                em("gpsimd", lambda g: g.collective_compute(
                    "AllGather", ALU.bypass, replica_groups=[list(range(8))],
                    ins=[cb1_dr.opt()], outs=[gg1_dr.opt()]).then_inc(cCC.h, 1))
                cCC.n += 1
                em("gpsimd", lambda g: g.collective_compute(
                    "AllGather", ALU.bypass, replica_groups=[list(range(8))],
                    ins=[cb2_dr.opt()], outs=[gg2_dr.opt()]).then_inc(cCC.h, 1))
                cCC.n += 1
            barrier()
            if dump_scratch and half == 1:
                for c in range(8):
                    dma("sync", cSS, q_dbg[c * 128:(c + 1) * 128, :], q_fm(c))
                    dma("sync", cSS, k_dbg[c * 128:(c + 1) * 128, :], k_fm(c))
                for bt in range(10):
                    dma("sync", cSS, v_dbg[bt * 128:(bt + 1) * 128, :], vt[bt][:])
                barrier()

            groups = [(nloc, h) for nloc in range(4) for h in range(H)]

            def g_meta(gi):
                nloc, h = groups[gi]
                n = half * 4 + nloc
                jpres = [j for j in range(6) if 0 <= (2 * (n - 1) + j) < 16]
                return nloc, h, n, jpres

            pa_rd = [None, None]
            ex_rd = [None, None]
            pd_rd_v = None
            pd_rd_c = None
            pe_rd = None
            rrowb_rd = None
            pvs_rd = None
            sc_done = {}; exp_done = {}; msk_done = {}; pv_done = {}
            rep_done = {}; evac_done = {}

            def emit_front(gi):
                nloc, h, n, jpres = g_meta(gi)
                jlo, jhi = jpres[0], jpres[-1] + 1
                pa = PA[gi % 2]
                ex = exb[gi % 2]
                c = h // 2
                rb = (h % 2) * 64
                wait_s("tensor", pa_rd[gi % 2])
                def smm(t_, n=n, c=c, rb=rb, pa=pa, jpres=jpres, e0=e0, nloc=nloc):
                    for j in jpres:
                        bt = 2 * (n - 1) + j - e0
                        ins = t_.matmul(
                            pa[:, j * 256:(j + 1) * 256],
                            k_fm(c)[rb:rb + 64, bt * 128:(bt + 1) * 128],
                            q_fm(c)[rb:rb + 64, nloc * 256:(nloc + 1) * 256],
                            start=True, stop=True)
                    return ins
                TE(smm)
                sc_done[gi] = snap(cT)
                wait_s("scalar", sc_done[gi])
                wait_s("scalar", ex_rd[gi % 2])
                SE(lambda s, pa=pa, ex=ex, jlo=jlo, jhi=jhi: s.activation(
                    ex[:, jlo * 256:jhi * 256], pa[:, jlo * 256:jhi * 256], AF.Exp))
                exp_done[gi] = snap(cC)
                pa_rd[gi % 2] = snap(cC)
                wait_s("vector", exp_done[gi])
                first = True
                for j in jpres:
                    if j in (2, 3):
                        continue
                    mi = {0: 0, 1: 1, 4: 2, 5: 3}[j]
                    VE(lambda v, ex=ex, j=j, mi=mi: v.tensor_tensor(
                        ex[:, j * 256:(j + 1) * 256], ex[:, j * 256:(j + 1) * 256],
                        maskT[mi][:], ALU.mult), sp=first)
                    first = False
                msk_done[gi] = snap(cV)

            def emit_back(gi):
                nonlocal pd_rd_v, pd_rd_c, pe_rd, rrowb_rd, pvs_rd
                nloc, h, n, jpres = g_meta(gi)
                ex = exb[gi % 2]
                c = h // 2
                rb = (h % 2) * 64
                wait_s("tensor", msk_done[gi])
                wait_s("tensor", exp_done[gi])
                wait_s("tensor", pd_rd_v)
                wait_s("tensor", pd_rd_c)
                def pvmm(t_, n=n, h=h, ex=ex, jpres=jpres, e0=e0):
                    for ji, j in enumerate(jpres):
                        bt = 2 * (n - 1) + j - e0
                        ins = t_.matmul(
                            PD[0:65, 0:256], vt[bt][:, h * 65:h * 65 + 65],
                            ex[:, j * 256:(j + 1) * 256],
                            start=(ji == 0), stop=(ji == len(jpres) - 1))
                    return ins
                TE(pvmm)
                pv_done[gi] = snap(cT)
                ex_rd[gi % 2] = snap(cT)
                wait_s("vector", pv_done[gi])
                VE(lambda v: v.reciprocal(rrow[:], PD[64:65, 0:256]))
                pd_rd_v = snap(cV)
                wait_s("vector", rrowb_rd)
                VE(lambda v: v.tensor_scalar(rrowb[:], rrow[:], 1.0, None, ALU.mult))
                rrowb_done = snap(cV)
                wait_s("tensor", rrowb_done)
                wait_s("tensor", pe_rd)
                TE(lambda t_: t_.matmul(PE[0:64, 0:256], onesb[0:1, 0:64], rrowb[:],
                                        start=True, stop=True))
                rep_done[gi] = snap(cT)
                rrowb_rd = snap(cT)
                wait_s("scalar", pv_done[gi])
                wait_s("scalar", pvs_rd)
                SE(lambda s: s.activation(pvS[:], PD[0:64, 0:256], AF.Identity))
                evac_done[gi] = snap(cC)
                pd_rd_c = snap(cC)
                wait_s("vector", evac_done[gi])
                wait_s("vector", rep_done[gi])
                VE(lambda v, c=c, rb=rb, nloc=nloc: v.tensor_tensor(
                    ho_fm(c)[rb:rb + 64, nloc * 256:(nloc + 1) * 256],
                    pvS[:], PE[0:64, 0:256], ALU.mult), sp=False)
                pe_rd = snap(cV)
                pvs_rd = snap(cV)

            emit_front(0)
            for gi in range(len(groups)):
                if gi + 1 < len(groups):
                    emit_front(gi + 1)
                emit_back(gi)
                drain_w2pc(1)
                if gi % 8 == 7:
                    drain_wqd(1)
            barrier()

            if dump_scratch:
                for c in range(8):
                    dma("sync", cSS, ho_dbg[c * 128:(c + 1) * 128, half * 1024:half * 1024 + 1024],
                        ho_fm(c))
            wait("gpsimd", cT)
            dma("gpsimd", cGW[3], brow[:], bo_d.unsqueeze(0))
            for kc in range(8):
                dma("gpsimd", cGW[3], wp[8 + kc][:], Wo_d[kc * 128:(kc + 1) * 128, :])
            wait("tensor", cGW[3])
            pa3_free = [None, None]
            for tl in range(8):
                gt = q0t + tl
                pa = PA[tl % 2]
                rg = tl % 2
                wait("sync", cC)
                dma("sync", cSL, xr[rg][:], x_in[gt * 128:(gt + 1) * 128, :])
                wait_s("tensor", pa3_free[tl % 2])
                def womm(t_, tl=tl, pa=pa):
                    for dh in range(2):
                        for kc in range(8):
                            t_.matmul(pa[:, dh * 512:dh * 512 + 512],
                                      ho_fm(kc)[:, tl * 128:(tl + 1) * 128],
                                      wp[8 + kc][:, dh * 512:dh * 512 + 512],
                                      start=(kc == 0), stop=False)
                        ins = t_.matmul(pa[:, dh * 512:dh * 512 + 512], onesb[:],
                                        brow[0:1, dh * 512:dh * 512 + 512],
                                        start=False, stop=True)
                    return ins
                TE(womm)
                womm_done = snap(cT)
                wait("scalar", cSL)
                wait("scalar", cV)
                SE(lambda s, gt=gt, rg=rg: s.activation(
                    uS[:], xr[rg][:], AF.Identity,
                    bias=nmr1[:, gt:gt + 1], scale=rstd1[:, gt:gt + 1]))
                wait_s("vector", womm_done)
                wait("vector", cC)
                wait("vector", cSS)
                VE(lambda v, pa=pa: v.tensor_tensor(tmpS[:, 0:512], pa[:, 0:512], uS[:, 0:512], ALU.add))
                VE(lambda v, pa=pa: v.tensor_tensor(tmpS[:, 512:1024], pa[:, 512:1024], uS[:, 512:1024], ALU.add), sp=False)
                pa3_free[tl % 2] = snap(cV)
                ln_stats(tmpS[:])
                VE(lambda v: v.tensor_scalar(tmpS[:], tmpS[:], lnS[:, 2:3], lnS[:, 6:7], ALU.subtract, ALU.mult))
                VE(lambda v: v.tensor_tensor(tmpS[:], tmpS[:], repA[:], ALU.mult), sp=False)
                VE(lambda v: v.tensor_tensor(tmpS[:], tmpS[:], repB[:], ALU.add), sp=False)
                wait("scalar", cV)
                wait("scalar", cSS)
                SE(lambda s, rg=rg: s.activation(stg[rg][:], tmpS[:], AF.Identity))
                wait("sync", cV)
                dma("sync", cSS, hf_dr[gt * 128:(gt + 1) * 128, :], tmpS[:])
                wait("sync", cC)
                dma("sync", cSS, hb_dr[gt * 128:(gt + 1) * 128, :], stg[rg][:])
            drain_wqd(6)
            barrier()

        wait("sync", cV)
        dma("sync", cSL, repA[:], g2_d.unsqueeze(0).broadcast_to([128, 1024]))
        dma("sync", cSL, repB[:], b2_d.unsqueeze(0).broadcast_to([128, 1024]))
        wait("gpsimd", cT)
        dma("gpsimd", cBR, brow[:], c2_d.unsqueeze(0))
        drain_w2pc(64)
        wait("sync", cPS[0])
        wait("sync", cPS[1])
        for c in range(8):
            dma("sync", cSL, pool[c][:, 0:2048],
                hb_dr[:, c * 128:(c + 1) * 128], transpose=True)
        wait("tensor", cSL)
        wait("tensor", cBR)

        g_strip = lambda m: pool[8 + m // 4][:, (m % 4) * 512:(m % 4) * 512 + 512]
        f2slots = [(PA[0][:, 0:512], PA[0][:, 512:1024]),
                   (PA[0][:, 1024:1536], PA[1][:, 0:512]),
                   (PA[1][:, 512:1024], PA[1][:, 1024:1536]),
                   (PD[:], PE[:])]

        f2_evac_done = None
        ev1_prev = None
        gu_free = [None, None]
        w1ring_free = [None, None, None, None]
        w2slot_free = [None] * 8
        slots4 = [PA[0][:, 0:512], PA[0][:, 512:1024], PA[0][:, 1024:1536], PA[1][:, 0:512]]
        for tokg in range(4):
            wait("sync", cV)
            wait("sync", cC)
            for ts4 in range(4):
                gt = tokg * 4 + ts4
                dma("sync", cH, xr[ts4][:], hf_dr[gt * 128:(gt + 1) * 128, :])
            h_loaded = cH.n
            for Mcg in range(16):
                w1t = pool[16 + Mcg % 4]
                wait_s("gpsimd", w1ring_free[Mcg % 4])
                for kc in range(8):
                    dma("gpsimd", cW1[Mcg % 4], w1t[:, kc * 256:(kc + 1) * 256],
                        W1_d[kc * 128:(kc + 1) * 128, Mcg * 256:(Mcg + 1) * 256])
                for Mh in range(2):
                    Mc = Mcg * 2 + Mh
                    ps = (PD if Mc % 2 == 0 else PE)
                    if Mh == 0:
                        wait("tensor", cW1[Mcg % 4])
                    wait_s("tensor", gu_free[Mc % 2])
                    def w1mm(t_, Mh=Mh, tokg=tokg, ps=ps, w1t=w1t):
                        for kc in range(8):
                            ins = t_.matmul(
                                ps[:], w1t[:, kc * 256 + Mh * 128: kc * 256 + Mh * 128 + 128],
                                pool[kc][:, tokg * 512:tokg * 512 + 512],
                                start=(kc == 0), stop=(kc == 7))
                        return ins
                    TE(w1mm)
                    w1_done = snap(cT)
                    wait_s("scalar", w1_done)
                    SE(lambda s, Mc=Mc, ps=ps: s.activation(
                        g_strip(Mc), ps[:], AF.Gelu_apprx_tanh, bias=c1p[:, Mc:Mc + 1]))
                    gu_free[Mc % 2] = snap(cC)
                w1ring_free[Mcg % 4] = snap(cT)
            for dh in range(2):
                for Mc in range(32):
                    w2t = wp[8 + Mc % 8]
                    wait_s("sync", w2slot_free[Mc % 8])
                    dma("sync", cHW[Mc % 8], w2t[:, 0:512],
                        w2b_dr[Mc * 128:(Mc + 1) * 128, dh * 512:dh * 512 + 512])
                    wait("tensor", cHW[Mc % 8])
                    if Mc == 0:
                        wait_s("tensor", gu_free[0])
                        wait_s("tensor", gu_free[1])
                        if dh == 0:
                            wait_s("tensor", ev1_prev)
                        else:
                            wait_s("tensor", ev0)
                    def w2mm(t_, Mc=Mc, w2t=w2t):
                        for ts4 in range(4):
                            ins = t_.matmul(slots4[ts4],
                                            g_strip(Mc)[:, ts4 * 128:(ts4 + 1) * 128],
                                            w2t[:, 0:512],
                                            start=(Mc == 0), stop=False)
                        return ins
                    TE(w2mm)
                    w2slot_free[Mc % 8] = snap(cT)
                def c2mm(t_, dh=dh):
                    for ts4 in range(4):
                        ins = t_.matmul(slots4[ts4], onesb[:],
                                        brow[0:1, dh * 512:dh * 512 + 512],
                                        start=False, stop=True)
                    return ins
                TE(c2mm)
                te_dh = snap(cT)
                wait_s("vector", te_dh)
                wait("vector", cH, h_loaded)
                for ts4 in range(4):
                    VE(lambda v, ts4=ts4, dh=dh: v.tensor_tensor(
                        xr[ts4][:, dh * 512:dh * 512 + 512], slots4[ts4],
                        xr[ts4][:, dh * 512:dh * 512 + 512], ALU.add), sp=(ts4 == 0))
                if dh == 0:
                    ev0 = snap(cV)
                else:
                    ev1_prev = snap(cV)
            for ts4 in range(4):
                gt = tokg * 4 + ts4
                rg = ts4 % 2
                wait("scalar", cSS)
                wait("vector", cSS)
                ln_stats(xr[ts4][:], sq=uS)
                wait("vector", cSS)
                VE(lambda v, ts4=ts4: v.tensor_scalar(
                    tmpS[:], xr[ts4][:], lnS[:, 2:3], lnS[:, 6:7], ALU.subtract, ALU.mult))
                VE(lambda v: v.tensor_tensor(tmpS[:], tmpS[:], repA[:], ALU.mult), sp=False)
                VE(lambda v: v.tensor_tensor(tmpS[:], tmpS[:], repB[:], ALU.add), sp=False)
                t_done = snap(cV)
                wait("sync", cV)
                dma("sync", cH, xr[ts4][:], x_in[gt * 128:(gt + 1) * 128, :])
                wait("vector", cH)
                VE(lambda v, ts4=ts4: v.tensor_tensor(uS[:], tmpS[:], xr[ts4][:], ALU.add), sp=False)
                wait("sync", cV)
                dma("sync", cSS, x1_dr[gt * 128:(gt + 1) * 128, :], uS[:])
                ln_stats(uS[:], sq=xr[ts4])
                wait("vector", cSS)
                VE(lambda v: v.tensor_scalar(
                    tmpS[:], uS[:], lnS[:, 2:3], lnS[:, 6:7], ALU.subtract, ALU.mult))
                for ch in range(2):
                    qd_chain(tmpS[:, ch * 512:ch * 512 + 512], stg[rg][:, ch * 512:ch * 512 + 512], 512)
                wait("sync", cV)
                dma("sync", cSS, xq_dr[gt * 128:(gt + 1) * 128, :], stg[rg][:])
            drain_wqd(6)
        barrier()

        drain_wqd(len(wqd_jobs))
        barrier()
        wait("sync", cCC)
        if dump_scratch:
            dma("sync", cSS, gg1_dbg, gg1_dr)
            dma("sync", cSS, gg2_dbg, gg2_dr)
        for c in range(8):
            dma("sync", cSL, pool[c][:, 0:2048],
                xq_dr[:, c * 128:(c + 1) * 128], transpose=True)
        wait("tensor", cSL)

        NCHW = [(0, 512), (512, 512), (1024, 512), (1536, 512), (2048, 512), (2560, 256)]
        gu_slots = [(PD[:], PE[:]), (PA[0][:, 0:512], PA[0][:, 512:1024])]
        gu5_free = [(None, None), (None, None)]
        sg_rd = None
        slab_mm_done = [None, None]
        slab_i = 0
        for tokg in range(4):
            for (n0, nw) in NCHW:
                p5 = slab_i % 2
                slab_i += 1
                wait_s("sync", slab_mm_done[p5])
                for kc in range(8):
                    dma("sync", cHW[p5], wp[kc][:, p5 * 512:p5 * 512 + nw],
                        wgq_dr[n0:n0 + nw, kc * 128:(kc + 1) * 128], transpose=True)
                    dma("sync", cHW[p5], wp[8 + kc][:, p5 * 512:p5 * 512 + nw],
                        wuq_dr[n0:n0 + nw, kc * 128:(kc + 1) * 128], transpose=True)
                wait("tensor", cHW[p5])
                for ti in range(4):
                    tokc = tokg * 4 + ti
                    pg, pu = gu_slots[ti % 2]
                    wait_s("tensor", gu5_free[ti % 2][0])
                    wait_s("tensor", gu5_free[ti % 2][1])
                    def gumm(t_, tokc=tokc, nw=nw, pg=pg, pu=pu, p5=p5):
                        for kc in range(8):
                            t_.matmul(pg[:, 0:nw],
                                      pool[kc][:, tokc * 128:(tokc + 1) * 128],
                                      wp[kc][:, p5 * 512:p5 * 512 + nw], start=(kc == 0), stop=(kc == 7))
                        for kc in range(8):
                            ins = t_.matmul(pu[:, 0:nw],
                                            pool[kc][:, tokc * 128:(tokc + 1) * 128],
                                            wp[8 + kc][:, p5 * 512:p5 * 512 + nw], start=(kc == 0), stop=(kc == 7))
                        return ins
                    TE(gumm)
                    gu_done = snap(cT)
                    wait_s("scalar", gu_done)
                    wait_s("scalar", sg_rd)
                    SE(lambda s, pg=pg, nw=nw: s.activation(sgS[:, 0:nw], pg[:, 0:nw], AF.Silu))
                    silu_done = snap(cC)
                    hrg = ti % 4
                    hqb = (hqstg[hrg] if hrg < 2 else stg[hrg - 2])
                    wait_s("vector", silu_done)
                    wait_s("vector", gu_done)
                    wait("vector", cHQ[hrg])
                    VE(lambda v, pu=pu, nw=nw: v.tensor_tensor(
                        tmpS[:, 0:nw], sgS[:, 0:nw], pu[:, 0:nw], ALU.mult), sp=False)
                    sg_rd = snap(cV)
                    gu5_free[ti % 2] = (silu_done, sg_rd)
                    qd_chain(tmpS[:, 0:nw], hqb[:, 0:nw], nw)
                    wait("sync", cV)
                    dma("sync", cHQ[hrg], hq_dr[tokc * 128:(tokc + 1) * 128, n0:n0 + nw],
                        hqb[:, 0:nw])
                slab_mm_done[p5] = snap(cT)
        barrier()

        def wdt_tile(kc):
            if kc < 16:
                return wp[kc][:, 0:1024]
            u = pool[(kc - 16) // 2]
            return u[:, ((kc - 16) % 2) * 1024:((kc - 16) % 2) * 1024 + 1024]

        for kc in range(22):
            dma("sync", cHW[4], wdt_tile(kc), wdq_dr[:, kc * 128:(kc + 1) * 128],
                transpose=True)
        wait("tensor", cHW[4])

        def hq_ring(st, kc):
            u = pool[8 + st * 6 + kc // 4]
            return u[:, (kc % 4) * 512:(kc % 4) * 512 + 512]

        dslots = [(PA[0][:, 0:512], PA[0][:, 512:1024]),
                  (PA[0][:, 1024:1536], PA[1][:, 0:512]),
                  (PA[1][:, 512:1024], PA[1][:, 1024:1536]),
                  (PD[:], PE[:])]
        dw_done = [None, None]
        out_ev = [None] * 4
        for tokg in range(4):
            st = tokg % 2
            wait_s("sync", dw_done[st])
            for kc in range(22):
                dma("sync", cHW[2 + st], hq_ring(st, kc),
                    hq_dr[tokg * 512:(tokg + 1) * 512, kc * 128:(kc + 1) * 128], transpose=True)
            wait("tensor", cHW[2 + st])
            for ti in range(4):
                wait_s("tensor", out_ev[ti])
                def dwmm(t_, st=st, ti=ti):
                    for kc in range(22):
                        for dh in range(2):
                            ins = t_.matmul(
                                dslots[ti][dh],
                                hq_ring(st, kc)[:, ti * 128:(ti + 1) * 128],
                                wdt_tile(kc)[:, dh * 512:dh * 512 + 512],
                                start=(kc == 0), stop=(kc == 21))
                    return ins
                TE(dwmm)
                dmm_done = snap(cT)
                gt = tokg * 4 + ti
                rg = ti % 2
                og = gt % 2
                ob = uS if og == 0 else tmpS
                wait("sync", cV)
                dma("sync", cH, xr[rg][:], x1_dr[gt * 128:(gt + 1) * 128, :])
                wait_s("vector", dmm_done)
                wait("vector", cH)
                wait("vector", cOS[og])
                VE(lambda v, ti=ti, rg=rg, ob=ob: v.tensor_tensor(
                    ob[:, 0:512], dslots[ti][0], xr[rg][:, 0:512], ALU.add))
                VE(lambda v, ti=ti, rg=rg, ob=ob: v.tensor_tensor(
                    ob[:, 512:1024], dslots[ti][1], xr[rg][:, 512:1024], ALU.add), sp=False)
                out_ev[ti] = snap(cV)
                wait("sync", cV)
                dma("sync", cOS[og], out_d[gt * 128:(gt + 1) * 128, :], ob[:])
            dw_done[st] = snap(cT)
        wait("sync", cSS)
        wait("sync", cOS[0])
        wait("sync", cOS[1])

        with nc.Block() as block:
            @block.sync
            def _(e):
                for eng, fn in PROG:
                    if eng == "sync":
                        fn(e)

            @block.gpsimd
            def _(e):
                for eng, fn in PROG:
                    if eng == "gpsimd":
                        fn(e)

            @block.tensor
            def _(e):
                for eng, fn in PROG:
                    if eng == "tensor":
                        fn(e)

            @block.vector
            def _(e):
                for eng, fn in PROG:
                    if eng == "vector":
                        fn(e)

            @block.scalar
            def _(e):
                for eng, fn in PROG:
                    if eng == "scalar":
                        fn(e)
    return nc


def _masks():
    r = np.arange(128)[:, None]
    q = np.arange(256)[None, :]
    m0 = (q <= r).astype(ml_dtypes.bfloat16)
    m1 = (q <= r + 128).astype(ml_dtypes.bfloat16)
    m4 = (q >= r).astype(ml_dtypes.bfloat16)
    m5 = (q >= r + 128).astype(ml_dtypes.bfloat16)
    return [m0, m1, m4, m5]


def _in_maps(inputs):
    masks = _masks()
    ones = np.ones((1, 128), dtype=ml_dtypes.bfloat16)
    base = {
        "Wq": inputs["L0_attn_Wq"], "bq": inputs["L0_attn_bq"],
        "Wk": inputs["L0_attn_Wk"], "bk": inputs["L0_attn_bk"],
        "Wv": inputs["L0_attn_Wv"], "bv": inputs["L0_attn_bv"],
        "Wo": inputs["L0_attn_Wo"], "bo": inputs["L0_attn_bo"],
        "g1": inputs["L0_attn_g1"], "b1": inputs["L0_attn_b1"],
        "W1": inputs["L0_attn_W1"], "c1": inputs["L0_attn_c1"],
        "W2": inputs["L0_attn_W2"], "c2": inputs["L0_attn_c2"],
        "g2": inputs["L0_attn_g2"], "b2": inputs["L0_attn_b2"],
        "ones_row": ones,
    }
    base = {k: np.ascontiguousarray(np.asarray(v, dtype=np.float32))
            if k not in ("ones_row",) else v for k, v in base.items()}
    for j in range(4):
        base[f"mask{j}"] = masks[j]
    x = np.ascontiguousarray(np.asarray(inputs["L0_attn_x"], dtype=np.float32))
    wggu = np.concatenate([np.asarray(inputs["L0_ffn_w_gate"], dtype=np.float32),
                           np.asarray(inputs["L0_ffn_w_up"], dtype=np.float32)], axis=0)
    wd = np.asarray(inputs["L0_ffn_w_down"], dtype=np.float32)
    in_maps = []
    for c in range(8):
        m = dict(base)
        m["x"] = np.ascontiguousarray(x[c])
        m["wggu_sl"] = np.ascontiguousarray(wggu[c * 704:(c + 1) * 704])
        m["wd_sl"] = np.ascontiguousarray(wd[c * 128:(c + 1) * 128])
        in_maps.append(m)
    return in_maps


def kernel(**inputs):
    nc = build()
    res = run_bass_kernel_spmd(nc, _in_maps(inputs), core_ids=list(range(8)))
    return np.stack([np.asarray(r["out"]) for r in res.results], axis=0)


# version 17 (final)
# speedup vs baseline: 1.2211x; 1.0165x over previous
import contextlib
import numpy as np
import ml_dtypes
import concourse.bass as bass
import concourse.mybir as mybir
from concourse.bass_utils import run_bass_kernel_spmd

F32 = mybir.dt.float32
BF16 = mybir.dt.bfloat16
AF = mybir.ActivationFunctionType
ALU = mybir.AluOpType
AX = mybir.AxisListType

S, D, H, HD, WB, NB, FFN, NI = 2048, 1024, 16, 64, 256, 8, 4096, 2816
NT = S // 128
MID = [0.25, 0.75, 1.25, 1.75, 2.5, 3.5, 5.0]
ENGS = ("sync", "gpsimd", "tensor", "vector", "scalar")
SAFE_DVE = False


def build(dump_scratch=False):
    nc = bass.Bass("TRN2", target_bir_lowering=False)

    def din(name, shape):
        return nc.dram_tensor(name, shape, F32, kind="ExternalInput").ap()

    x_in = din("x", [S, D])
    Wq_d, Wk_d, Wv_d, Wo_d = (din(n, [D, D]) for n in ("Wq", "Wk", "Wv", "Wo"))
    bq_d, bk_d, bv_d, bo_d = (din(n, [D]) for n in ("bq", "bk", "bv", "bo"))
    g1_d, b1_d, g2_d, b2_d = (din(n, [D]) for n in ("g1", "b1", "g2", "b2"))
    W1_d = din("W1", [D, FFN]); c1_d = din("c1", [FFN])
    W2_d = din("W2", [FFN, D]); c2_d = din("c2", [D])
    wggu_sl = din("wggu_sl", [704, D])
    wd_sl = din("wd_sl", [128, NI])
    masks_d = [nc.dram_tensor(f"mask{j}", [128, 256], BF16, kind="ExternalInput").ap()
               for j in range(4)]
    ones_d = nc.dram_tensor("ones_row", [1, 128], BF16, kind="ExternalInput").ap()
    out_d = nc.dram_tensor("out", [S, D], F32, kind="ExternalOutput").ap()

    sk = "ExternalOutput" if dump_scratch else "Internal"
    u_dr = nc.dram_tensor("u_dr", [S, D], BF16, kind=sk).ap()
    hf_dr = nc.dram_tensor("hf_dr", [S, D], F32, kind=sk).ap()
    hb_dr = nc.dram_tensor("hb_dr", [S, D], BF16, kind=sk).ap()
    x1_dr = nc.dram_tensor("x1_dr", [S, D], F32, kind=sk).ap()
    xq_dr = nc.dram_tensor("xq_dr", [S, D], BF16, kind=sk).ap()
    hq_dr = nc.dram_tensor("hq_dr", [S, NI], BF16, kind=sk).ap()
    ho_dbg = nc.dram_tensor("ho_dbg", [D, S], BF16, kind=sk).ap()
    q_dbg = nc.dram_tensor("q_dbg", [D, 1024], BF16, kind=sk).ap()
    k_dbg = nc.dram_tensor("k_dbg", [D, 1280], BF16, kind=sk).ap()
    v_dbg = nc.dram_tensor("v_dbg", [1280, 1040], BF16, kind=sk).ap()
    ex_dbg = nc.dram_tensor("ex_dbg", [128, 1536], BF16, kind=sk).ap()
    pv_dbg = nc.dram_tensor("pv_dbg", [64, 256], F32, kind=sk).ap()
    rr_dbg = nc.dram_tensor("rr_dbg", [1, 256], F32, kind=sk).ap()
    rep_dbg = nc.dram_tensor("rep_dbg", [64, 256], F32, kind=sk).ap()
    w2b_dr = nc.dram_tensor("w2b_dr", [FFN, D], BF16, kind="Internal").ap()
    cb1_dr = nc.dram_tensor("cb1_dr", [704, D // 2], F32, kind="Internal").ap()
    cb2_dr = nc.dram_tensor("cb2_dr", [128, NI // 2], F32, kind="Internal").ap()
    gg1_dr = nc.dram_tensor("gg1_dr", [2 * NI, D // 2], F32, kind="Internal").ap()
    gg2_dr = nc.dram_tensor("gg2_dr", [D, NI // 2], F32, kind="Internal").ap()
    gg1_dbg = nc.dram_tensor("gg1_dbg", [2 * NI, D // 2], F32, kind=sk).ap()
    gg2_dbg = nc.dram_tensor("gg2_dbg", [D, NI // 2], F32, kind=sk).ap()
    wgq_dr = gg1_dr.bitcast(BF16)[0:NI, :]
    wuq_dr = gg1_dr.bitcast(BF16)[NI:2 * NI, :]
    wdq_dr = gg2_dr.bitcast(BF16)

    es = contextlib.ExitStack()
    with es:
        T = lambda name, shape, dt=F32: es.enter_context(nc.sbuf_tensor(name, shape, dt))

        pool = [T(f"pool{i}", [128, 2048], BF16) for i in range(20)]
        wp = [T(f"wp{i}", [128, 1024], BF16) for i in range(16)]
        vt = [T(f"vt{i}", [128, 16 * 65], BF16) for i in range(10)]
        exb = [T(f"exb{i}", [128, 1536], BF16) for i in range(2)]
        xr = [T(f"xr{i}", [128, 1024], F32) for i in range(4)]
        uS = T("uS", [128, 1024], F32)
        tmpS = T("tmpS", [128, 1024], F32)
        xnS = T("xnS", [128, 512], F32)
        absS = T("absS", [128, 512], F32)
        sgS = T("sgS", [128, 512], F32)
        pvS = T("pvS", [64, 256], F32)
        Ib = [T(f"Ib{k}", [128, 512], BF16) for k in range(2)]
        sgnB = T("sgnB", [128, 512], BF16)
        bmS = T("bmS", [128, 64], F32)
        sclS = T("sclS", [128, 64], F32)
        rsS = T("rsS", [128, 64], F32)
        sclB = T("sclB", [128, 64], BF16)
        stg = [T(f"stg{i}", [128, 1024], BF16) for i in range(2)]
        hqstg = [T(f"hqstg{i}", [128, 512], BF16) for i in range(2)]
        repA = T("repA", [128, 1024], F32)
        repB = T("repB", [128, 1024], F32)
        lnS = T("lnS", [128, 8], F32)
        mu1 = T("mu1", [128, NT], F32)
        rstd1 = T("rstd1", [128, NT], F32)
        nmr1 = T("nmr1", [128, NT], F32)
        bq8 = T("bq8", [128, 8], F32)
        bkp = T("bkp", [128, 8], F32)
        c1p = T("c1p", [128, 32], F32)
        brow = T("brow", [1, 1024], BF16)
        onesb = T("onesb", [1, 128], BF16)
        rrow = T("rrow", [1, 256], F32)
        rrowb = T("rrowb", [1, 256], BF16)
        maskT = [T(f"maskT{j}", [128, 256], BF16) for j in range(4)]
        wfp = [T(f"wfp{i}", [128, 1024], F32) for i in range(2)]
        wqs = [T(f"wqs{i}", [128, 1024], BF16) for i in range(2)]

        PA = [es.enter_context(nc.psum_tensor(f"PA{i}", [128, 1536], F32)) for i in range(2)]
        PD = es.enter_context(nc.psum_tensor("PD", [128, 512], F32))
        PE = es.enter_context(nc.psum_tensor("PE", [128, 512], F32))

        class Ctr:
            def __init__(self, name):
                self.h = es.enter_context(nc.semaphore(name))
                self.n = 0
        cSL = Ctr("cSL"); cSS = Ctr("cSS"); cG = Ctr("cG")
        cT = Ctr("cT"); cV = Ctr("cV"); cC = Ctr("cC")
        cH = Ctr("cH")
        cCC = Ctr("cCC")
        cW2 = Ctr("cW2")

        PROG = []

        def em(eng, fn):
            PROG.append((eng, fn))

        def wait(eng, ctr, val=None):
            v = ctr.n if val is None else val
            if v > 0:
                em(eng, lambda e, h=ctr.h, v=v: e.wait_ge(h, v))

        def snap(ctr):
            return (ctr.h, ctr.n)

        def wait_s(eng, s):
            if s is None:
                return
            h, v = s
            if v > 0:
                em(eng, lambda e, h=h, v=v: e.wait_ge(h, v))

        def dma(qeng, ctr, out, in_, transpose=False, slow=False):
            ctr.n += 16
            def f(e, out=out, in_=in_, h=ctr.h, transpose=transpose, slow=slow):
                kw = {}
                if transpose:
                    kw["transpose"] = True
                if slow:
                    kw["allow_slow_non_contiguous"] = True
                e.dma_start(out=out, in_=in_, **kw).then_inc(h, 16)
            em(qeng, f)

        def op(eng, ctr, builder):
            ctr.n += 1
            def f(e, b=builder, h=ctr.h):
                b(e).then_inc(h, 1)
            em(eng, f)

        TE = lambda b: op("tensor", cT, b)
        SE = lambda b: op("scalar", cC, b)

        def VE(b, sp=True):
            if sp or SAFE_DVE:
                wait("vector", cV)
            op("vector", cV, b)

        def barrier():
            for eng in ENGS:
                for c in (cSL, cSS, cG, cT, cV, cC, cH, cW2, cWQ, cBR, *cWF, *cW1, *cGW, *cHW, *cPL, *cPS, *cHQ, *cOS):
                    wait(eng, c)

        def qd_chain(src_ap, dst_ap, width, p=128):
            nb = width // 16
            big = width >= 384
            src3 = src_ap.rearrange("p (c k) -> p c k", k=16)
            VE(lambda v, a=src3, w=nb: v.tensor_reduce(
                bmS[0:p, 0:w].unsqueeze(2), a, AX.X, ALU.max, apply_absolute_value=True), sp=not big)
            VE(lambda v, w=nb: v.tensor_scalar(
                sclS[0:p, 0:w], bmS[0:p, 0:w], 1.0 / 6, 1e-12, ALU.mult, ALU.max))
            wait("vector", cV)
            VE(lambda v, w=nb: v.reciprocal(rsS[0:p, 0:w], sclS[0:p, 0:w]), sp=False)
            wait("vector", cV)
            xn3 = xnS[0:p, 0:width].rearrange("p (c k) -> p c k", k=16)
            VE(lambda v, a=src3, x=xn3, w=nb: v.tensor_tensor(
                x, a, rsS[0:p, 0:w].unsqueeze(2).broadcast_to([p, w, 16]), ALU.mult), sp=False)
            A, B = Ib[0], Ib[1]
            VE(lambda v, w=width: v.scalar_tensor_tensor(
                absS[0:p, 0:w], xnS[0:p, 0:w], -1.0, xnS[0:p, 0:w], ALU.mult, ALU.max), sp=not big)
            VE(lambda v, w=width: v.tensor_scalar(B[0:p, 0:w], xnS[0:p, 0:w], 0.0, None, ALU.is_gt), sp=not big)
            VE(lambda v, w=width: v.tensor_scalar(sgnB[0:p, 0:w], B[0:p, 0:w], 2.0, -1.0, ALU.mult, ALU.add), sp=not big)
            VE(lambda v, w=width: v.tensor_scalar(A[0:p, 0:w], absS[0:p, 0:w], MID[0], None, ALU.is_gt), sp=not big)
            for k in (1, 2, 3):
                VE(lambda v, k=k, w=width: v.scalar_tensor_tensor(
                    A[0:p, 0:w], absS[0:p, 0:w], MID[k], A[0:p, 0:w], ALU.is_gt, ALU.add), sp=not big)
            VE(lambda v, w=width: v.tensor_scalar(B[0:p, 0:w], absS[0:p, 0:w], MID[4], None, ALU.is_gt), sp=False)
            VE(lambda v, w=width: v.scalar_tensor_tensor(
                B[0:p, 0:w], absS[0:p, 0:w], MID[5], B[0:p, 0:w], ALU.is_gt, ALU.add), sp=not big)
            VE(lambda v, w=width: v.scalar_tensor_tensor(
                A[0:p, 0:w], B[0:p, 0:w], 2.0, A[0:p, 0:w], ALU.mult, ALU.add), sp=not big)
            VE(lambda v, w=width: v.tensor_scalar(B[0:p, 0:w], absS[0:p, 0:w], MID[6], 4.0, ALU.is_gt, ALU.mult), sp=False)
            VE(lambda v, w=width: v.tensor_tensor(A[0:p, 0:w], A[0:p, 0:w], B[0:p, 0:w], ALU.add), sp=not big)
            VE(lambda v, w=width: v.tensor_tensor(A[0:p, 0:w], A[0:p, 0:w], sgnB[0:p, 0:w], ALU.mult), sp=not big)
            VE(lambda v, w=nb: v.tensor_scalar(sclB[0:p, 0:w], sclS[0:p, 0:w], 0.5, None, ALU.mult), sp=False)
            wait("vector", cV)
            q3 = Ib[0][0:p, 0:width].rearrange("p (c k) -> p c k", k=16)
            d3 = dst_ap.rearrange("p (c k) -> p c k", k=16)
            VE(lambda v, q=q3, d=d3, w=nb: v.tensor_tensor(
                d, q, sclB[0:p, 0:w].unsqueeze(2).broadcast_to([p, w, 16]), ALU.mult), sp=False)

        wqd_jobs = []
        for (r0, rr) in ((0, 128), (128, 128), (256, 128), (384, 128), (512, 128), (640, 64)):
            wqd_jobs.append((wggu_sl, cb1_dr, r0, rr, 0, 1024))
        for (c0, cw) in ((0, 1024), (1024, 1024), (2048, 768)):
            wqd_jobs.append((wd_sl, cb2_dr, 0, 128, c0, cw))
        wqd_state = {"i": 0, "ring": 0, "wfp_free": [None, None], "wqs_free": [None, None]}
        cWQ = Ctr("cWQ")
        cWF = [Ctr(f"cWF{i}") for i in range(2)]
        cW1 = [Ctr(f"cW1_{i}") for i in range(4)]
        cGW = [Ctr(f"cGW{i}") for i in range(4)]
        cHW = [Ctr(f"cHW{i}") for i in range(8)]
        cBR = Ctr("cBR")
        cPL = [Ctr(f"cPL{i}") for i in range(2)]
        cPS = [Ctr(f"cPS{i}") for i in range(2)]
        cHQ = [Ctr(f"cHQ{i}") for i in range(4)]
        cOS = [Ctr(f"cOS{i}") for i in range(2)]

        w2pc_state = {"i": 0}

        def drain_w2pc(k):
            for _ in range(k):
                i = w2pc_state["i"]
                if i >= 64:
                    return
                w2pc_state["i"] = i + 1
                rt_, ch_ = i // 2, i % 2
                s = i % 2
                hb = hqstg[s]
                wait("gpsimd", cPS[s])
                dma("gpsimd", cPL[s], hb[:],
                    W2_d[rt_ * 128:(rt_ + 1) * 128, ch_ * 512:ch_ * 512 + 512])
                wait("sync", cPL[s])
                dma("sync", cPS[s], w2b_dr[rt_ * 128:(rt_ + 1) * 128, ch_ * 512:ch_ * 512 + 512],
                    hb[:])

        def drain_wqd(k):
            for _ in range(k):
                i = wqd_state["i"]
                if i >= len(wqd_jobs):
                    return
                wqd_state["i"] = i + 1
                src_d, dst_d, r0, rr, c0, cw = wqd_jobs[i]
                rg = wqd_state["ring"]; wqd_state["ring"] ^= 1
                wait_s("gpsimd", wqd_state["wfp_free"][rg])
                dma("gpsimd", cWF[rg], wfp[rg][0:rr, 0:cw], src_d[r0:r0 + rr, c0:c0 + cw])
                wait("vector", cWF[rg])
                wait("vector", cWQ)
                for s0 in range(0, cw, 512):
                    sw = min(512, cw - s0)
                    qd_chain(wfp[rg][0:rr, s0:s0 + sw], wqs[rg][0:rr, s0:s0 + sw], sw, p=rr)
                wqd_state["wfp_free"][rg] = snap(cV)
                wait("sync", cV)
                dma("sync", cWQ, dst_d[r0:r0 + rr, c0 // 2:(c0 + cw) // 2],
                    wqs[rg][0:rr, 0:cw].bitcast(F32))


        for j in range(4):
            dma("sync", cSL, maskT[j][:], masks_d[j])
        dma("sync", cSL, onesb[:], ones_d)
        dma("sync", cSL, bq8[:], bq_d.rearrange("(c p) -> p c", p=128), slow=True)
        dma("sync", cSL, bkp[:], bk_d.rearrange("(c p) -> p c", p=128), slow=True)
        dma("sync", cSL, c1p[:], c1_d.rearrange("(c p) -> p c", p=128), slow=True)
        dma("sync", cSL, repA[:], g1_d.unsqueeze(0).broadcast_to([128, 1024]))
        dma("sync", cSL, repB[:], b1_d.unsqueeze(0).broadcast_to([128, 1024]))
        dma("gpsimd", cG, brow[:], bv_d.unsqueeze(0))
        wait("vector", cSL)
        VE(lambda v: v.tensor_scalar(bq8[:], bq8[:], 0.125, None, ALU.mult))

        def ln_stats(src, sq=None):
            if sq is None:
                sq = uS
            VE(lambda v, a=src: v.reduce_sum(lnS[:, 0:1], a, AX.X), sp=False)
            wait("scalar", cV)
            SE(lambda s, a=src, sq=sq: s.activation(sq[:], a, AF.Square))
            wait("vector", cC)
            VE(lambda v, sq=sq: v.reduce_sum(lnS[:, 1:2], sq[:], AX.X))
            VE(lambda v: v.tensor_scalar(lnS[:, 2:3], lnS[:, 0:1], 1.0 / D, None, ALU.mult))
            wait("vector", cV)
            VE(lambda v: v.tensor_tensor(lnS[:, 3:4], lnS[:, 2:3], lnS[:, 2:3], ALU.mult))
            VE(lambda v: v.scalar_tensor_tensor(
                lnS[:, 4:5], lnS[:, 1:2], 1.0 / D, lnS[:, 3:4], ALU.mult, ALU.subtract))
            VE(lambda v: v.tensor_scalar(lnS[:, 4:5], lnS[:, 4:5], 1e-5, None, ALU.add))
            wait("scalar", cV)
            SE(lambda s: s.activation(lnS[:, 5:6], lnS[:, 4:5], AF.Sqrt))
            wait("vector", cC)
            VE(lambda v: v.reciprocal(lnS[:, 6:7], lnS[:, 5:6]))
            wait("vector", cV)

        for t in range(NT):
            rg = t % 2
            drain_w2pc(1)
            wait("sync", cC)
            wait("sync", cV)
            dma("sync", cSL, xr[rg][:], x_in[t * 128:(t + 1) * 128, :])
            wait("vector", cSL)
            wait("scalar", cSL)
            ln_stats(xr[rg][:])
            VE(lambda v, t=t: v.tensor_copy(mu1[:, t:t + 1], lnS[:, 2:3]))
            VE(lambda v, t=t: v.tensor_copy(rstd1[:, t:t + 1], lnS[:, 6:7]))
            wait("vector", cV)
            VE(lambda v, t=t: v.scalar_tensor_tensor(
                nmr1[:, t:t + 1], mu1[:, t:t + 1], -1.0, rstd1[:, t:t + 1], ALU.mult, ALU.mult))
            wait("scalar", cV)
            wait("scalar", cSS)
            SE(lambda s, t=t, rg=rg: s.activation(
                stg[rg][:], xr[rg][:], AF.Identity,
                bias=nmr1[:, t:t + 1], scale=rstd1[:, t:t + 1]))
            wait("sync", cC)
            dma("sync", cSS, u_dr[t * 128:(t + 1) * 128, :], stg[rg][:])
        drain_wqd(4)
        barrier()

        u_fm = lambda c: pool[c][:, 0:1280]
        k_fm = lambda c: pool[8 + c][:, 0:1280]
        q_fm = lambda c: pool[16 + c // 2][:, (c % 2) * 1024:(c % 2) * 1024 + 1024]
        ho_fm = lambda c: pool[c][:, 0:1024]

        for half in range(2):
            e0 = 0 if half == 0 else 6
            q0t = half * 8
            qex = 0 if half == 0 else 2
            for c in range(8):
                dma("sync", cSL, u_fm(c),
                    u_dr[e0 * 128:e0 * 128 + 1280, c * 128:(c + 1) * 128], transpose=True)
            wait("gpsimd", cT)
            for kc in range(8):
                dma("gpsimd", cGW[0], wp[kc][:], Wq_d[kc * 128:(kc + 1) * 128, :])
            wait("tensor", cSL)
            wait("tensor", cGW[0])

            pp = 0
            ps_free = {0: None, 1: None}
            for c in range(8):
                for tt in range(2):
                    pi = pp; ps = (PD if pp == 0 else PE); pp ^= 1
                    wait_s("tensor", ps_free[pi])
                    def qmm(t_, c=c, tt=tt, ps=ps, qex=qex):
                        for kc in range(8):
                            ins = t_.matmul(
                                ps[:], wp[kc][:, c * 128:(c + 1) * 128],
                                u_fm(kc)[:, qex * 128 + tt * 512: qex * 128 + tt * 512 + 512],
                                start=(kc == 0), stop=(kc == 7))
                        return ins
                    TE(qmm)
                    mm_done = snap(cT)
                    wait_s("scalar", mm_done)
                    wait("scalar", cV)
                    SE(lambda s, c=c, tt=tt, ps=ps: s.activation(
                        q_fm(c)[:, tt * 512:tt * 512 + 512], ps[:], AF.Identity,
                        bias=bq8[:, c:c + 1], scale=0.125))
                    ps_free[pi] = snap(cC)
            for kc in range(8):
                dma("gpsimd", cGW[1], wp[8 + kc][:], Wk_d[kc * 128:(kc + 1) * 128, :])
            wait("tensor", cGW[1])
            for c in range(8):
                for (t0, tw) in ((0, 512), (512, 512), (1024, 256)):
                    pi = pp; ps = (PD if pp == 0 else PE); pp ^= 1
                    wait_s("tensor", ps_free[pi])
                    def kmm(t_, c=c, t0=t0, tw=tw, ps=ps):
                        for kc in range(8):
                            ins = t_.matmul(
                                ps[:, 0:tw], wp[8 + kc][:, c * 128:(c + 1) * 128],
                                u_fm(kc)[:, t0:t0 + tw], start=(kc == 0), stop=(kc == 7))
                        return ins
                    TE(kmm)
                    mm_done = snap(cT)
                    wait_s("scalar", mm_done)
                    SE(lambda s, c=c, t0=t0, tw=tw, ps=ps: s.activation(
                        k_fm(c)[:, t0:t0 + tw], ps[:, 0:tw], AF.Identity,
                        bias=bkp[:, c:c + 1]))
                    ps_free[pi] = snap(cC)
            wait("gpsimd", cT)
            dma("gpsimd", cGW[2], brow[:], bv_d.unsqueeze(0))
            for kc in range(8):
                dma("gpsimd", cGW[2], wp[kc][:], Wv_d[kc * 128:(kc + 1) * 128, :])
            for bt in range(10):
                VE(lambda v, bt=bt: v.memset(
                    vt[bt][:].rearrange("p (g c) -> p g c", c=65)[:, :, 64:65], 1.0))
            wait("tensor", cGW[2])
            wait("tensor", cV)
            for bt in range(10):
                for dh in range(2):
                    pi = pp; ps = (PD if pp == 0 else PE); pp ^= 1
                    wait_s("tensor", ps_free[pi])
                    def vmm(t_, bt=bt, dh=dh, ps=ps):
                        for kc in range(8):
                            t_.matmul(ps[:], u_fm(kc)[:, bt * 128:(bt + 1) * 128],
                                      wp[kc][:, dh * 512:dh * 512 + 512],
                                      start=(kc == 0), stop=False)
                        return t_.matmul(ps[:], onesb[:],
                                         brow[0:1, dh * 512:dh * 512 + 512],
                                         start=False, stop=True)
                    TE(vmm)
                    mm_done = snap(cT)
                    wait_s("vector", mm_done)
                    VE(lambda v, bt=bt, dh=dh, ps=ps: v.tensor_copy(
                        vt[bt][:].rearrange("p (g c) -> p g c", c=65)[:, dh * 8:dh * 8 + 8, 0:64],
                        ps[:].rearrange("p (g c) -> p g c", c=64)), sp=False)
                    ps_free[pi] = snap(cV)
            drain_wqd(12)
            if half == 0:
                wait("gpsimd", cWQ)
                em("gpsimd", lambda g: g.collective_compute(
                    "AllGather", ALU.bypass, replica_groups=[list(range(8))],
                    ins=[cb1_dr.opt()], outs=[gg1_dr.opt()]).then_inc(cCC.h, 1))
                cCC.n += 1
                em("gpsimd", lambda g: g.collective_compute(
                    "AllGather", ALU.bypass, replica_groups=[list(range(8))],
                    ins=[cb2_dr.opt()], outs=[gg2_dr.opt()]).then_inc(cCC.h, 1))
                cCC.n += 1
            barrier()
            if dump_scratch and half == 1:
                for c in range(8):
                    dma("sync", cSS, q_dbg[c * 128:(c + 1) * 128, :], q_fm(c))
                    dma("sync", cSS, k_dbg[c * 128:(c + 1) * 128, :], k_fm(c))
                for bt in range(10):
                    dma("sync", cSS, v_dbg[bt * 128:(bt + 1) * 128, :], vt[bt][:])
                barrier()

            groups = [(nloc, h) for nloc in range(4) for h in range(H)]

            def g_meta(gi):
                nloc, h = groups[gi]
                n = half * 4 + nloc
                jpres = [j for j in range(6) if 0 <= (2 * (n - 1) + j) < 16]
                return nloc, h, n, jpres

            pa_rd = [None, None]
            ex_rd = [None, None]
            pd_rd_v = None
            pd_rd_c = None
            pe_rd = None
            rrowb_rd = None
            pvs_rd = None
            sc_done = {}; exp_done = {}; msk_done = {}; pv_done = {}
            rep_done = {}; evac_done = {}

            def emit_front(gi):
                nloc, h, n, jpres = g_meta(gi)
                jlo, jhi = jpres[0], jpres[-1] + 1
                pa = PA[gi % 2]
                ex = exb[gi % 2]
                c = h // 2
                rb = (h % 2) * 64
                wait_s("tensor", pa_rd[gi % 2])
                def smm(t_, n=n, c=c, rb=rb, pa=pa, jpres=jpres, e0=e0, nloc=nloc):
                    for j in jpres:
                        bt = 2 * (n - 1) + j - e0
                        ins = t_.matmul(
                            pa[:, j * 256:(j + 1) * 256],
                            k_fm(c)[rb:rb + 64, bt * 128:(bt + 1) * 128],
                            q_fm(c)[rb:rb + 64, nloc * 256:(nloc + 1) * 256],
                            start=True, stop=True)
                    return ins
                TE(smm)
                sc_done[gi] = snap(cT)
                wait_s("scalar", sc_done[gi])
                wait_s("scalar", ex_rd[gi % 2])
                SE(lambda s, pa=pa, ex=ex, jlo=jlo, jhi=jhi: s.activation(
                    ex[:, jlo * 256:jhi * 256], pa[:, jlo * 256:jhi * 256], AF.Exp))
                exp_done[gi] = snap(cC)
                pa_rd[gi % 2] = snap(cC)
                wait_s("vector", exp_done[gi])
                first = True
                for j in jpres:
                    if j in (2, 3):
                        continue
                    mi = {0: 0, 1: 1, 4: 2, 5: 3}[j]
                    VE(lambda v, ex=ex, j=j, mi=mi: v.tensor_tensor(
                        ex[:, j * 256:(j + 1) * 256], ex[:, j * 256:(j + 1) * 256],
                        maskT[mi][:], ALU.mult), sp=first)
                    first = False
                msk_done[gi] = snap(cV)

            def emit_back(gi):
                nonlocal pd_rd_v, pd_rd_c, pe_rd, rrowb_rd, pvs_rd
                nloc, h, n, jpres = g_meta(gi)
                ex = exb[gi % 2]
                c = h // 2
                rb = (h % 2) * 64
                wait_s("tensor", msk_done[gi])
                wait_s("tensor", exp_done[gi])
                wait_s("tensor", pd_rd_v)
                wait_s("tensor", pd_rd_c)
                def pvmm(t_, n=n, h=h, ex=ex, jpres=jpres, e0=e0):
                    for ji, j in enumerate(jpres):
                        bt = 2 * (n - 1) + j - e0
                        ins = t_.matmul(
                            PD[0:65, 0:256], vt[bt][:, h * 65:h * 65 + 65],
                            ex[:, j * 256:(j + 1) * 256],
                            start=(ji == 0), stop=(ji == len(jpres) - 1))
                    return ins
                TE(pvmm)
                pv_done[gi] = snap(cT)
                ex_rd[gi % 2] = snap(cT)
                wait_s("vector", pv_done[gi])
                VE(lambda v: v.reciprocal(rrow[:], PD[64:65, 0:256]))
                pd_rd_v = snap(cV)
                wait_s("vector", rrowb_rd)
                VE(lambda v: v.tensor_scalar(rrowb[:], rrow[:], 1.0, None, ALU.mult))
                rrowb_done = snap(cV)
                wait_s("tensor", rrowb_done)
                wait_s("tensor", pe_rd)
                TE(lambda t_: t_.matmul(PE[0:64, 0:256], onesb[0:1, 0:64], rrowb[:],
                                        start=True, stop=True))
                rep_done[gi] = snap(cT)
                rrowb_rd = snap(cT)
                wait_s("scalar", pv_done[gi])
                wait_s("scalar", pvs_rd)
                SE(lambda s: s.activation(pvS[:], PD[0:64, 0:256], AF.Identity))
                evac_done[gi] = snap(cC)
                pd_rd_c = snap(cC)
                wait_s("vector", evac_done[gi])
                wait_s("vector", rep_done[gi])
                VE(lambda v, c=c, rb=rb, nloc=nloc: v.tensor_tensor(
                    ho_fm(c)[rb:rb + 64, nloc * 256:(nloc + 1) * 256],
                    pvS[:], PE[0:64, 0:256], ALU.mult), sp=False)
                pe_rd = snap(cV)
                pvs_rd = snap(cV)

            emit_front(0)
            for gi in range(len(groups)):
                if gi + 1 < len(groups):
                    emit_front(gi + 1)
                emit_back(gi)
                drain_w2pc(1)
                if gi % 8 == 7:
                    drain_wqd(1)
            barrier()

            if dump_scratch:
                for c in range(8):
                    dma("sync", cSS, ho_dbg[c * 128:(c + 1) * 128, half * 1024:half * 1024 + 1024],
                        ho_fm(c))
            wait("gpsimd", cT)
            dma("gpsimd", cGW[3], brow[:], bo_d.unsqueeze(0))
            for kc in range(8):
                dma("gpsimd", cGW[3], wp[8 + kc][:], Wo_d[kc * 128:(kc + 1) * 128, :])
            wait("tensor", cGW[3])
            pa3_free = [None, None]
            for tl in range(8):
                gt = q0t + tl
                pa = PA[tl % 2]
                rg = tl % 2
                wait("sync", cC)
                dma("sync", cSL, xr[rg][:], x_in[gt * 128:(gt + 1) * 128, :])
                wait_s("tensor", pa3_free[tl % 2])
                def womm(t_, tl=tl, pa=pa):
                    for dh in range(2):
                        for kc in range(8):
                            t_.matmul(pa[:, dh * 512:dh * 512 + 512],
                                      ho_fm(kc)[:, tl * 128:(tl + 1) * 128],
                                      wp[8 + kc][:, dh * 512:dh * 512 + 512],
                                      start=(kc == 0), stop=False)
                        ins = t_.matmul(pa[:, dh * 512:dh * 512 + 512], onesb[:],
                                        brow[0:1, dh * 512:dh * 512 + 512],
                                        start=False, stop=True)
                    return ins
                TE(womm)
                womm_done = snap(cT)
                wait("scalar", cSL)
                wait("scalar", cV)
                SE(lambda s, gt=gt, rg=rg: s.activation(
                    uS[:], xr[rg][:], AF.Identity,
                    bias=nmr1[:, gt:gt + 1], scale=rstd1[:, gt:gt + 1]))
                wait_s("vector", womm_done)
                wait("vector", cC)
                wait("vector", cSS)
                VE(lambda v, pa=pa: v.tensor_tensor(tmpS[:, 0:512], pa[:, 0:512], uS[:, 0:512], ALU.add))
                VE(lambda v, pa=pa: v.tensor_tensor(tmpS[:, 512:1024], pa[:, 512:1024], uS[:, 512:1024], ALU.add), sp=False)
                pa3_free[tl % 2] = snap(cV)
                ln_stats(tmpS[:])
                VE(lambda v: v.tensor_scalar(tmpS[:], tmpS[:], lnS[:, 2:3], lnS[:, 6:7], ALU.subtract, ALU.mult))
                VE(lambda v: v.tensor_tensor(tmpS[:], tmpS[:], repA[:], ALU.mult), sp=False)
                VE(lambda v: v.tensor_tensor(tmpS[:], tmpS[:], repB[:], ALU.add), sp=False)
                wait("scalar", cV)
                wait("scalar", cSS)
                SE(lambda s, rg=rg: s.activation(stg[rg][:], tmpS[:], AF.Identity))
                wait("sync", cV)
                dma("sync", cSS, hf_dr[gt * 128:(gt + 1) * 128, :], tmpS[:])
                wait("sync", cC)
                dma("sync", cSS, hb_dr[gt * 128:(gt + 1) * 128, :], stg[rg][:])
            drain_wqd(6)
            barrier()

        wait("sync", cV)
        dma("sync", cSL, repA[:], g2_d.unsqueeze(0).broadcast_to([128, 1024]))
        dma("sync", cSL, repB[:], b2_d.unsqueeze(0).broadcast_to([128, 1024]))
        wait("gpsimd", cT)
        dma("gpsimd", cBR, brow[:], c2_d.unsqueeze(0))
        drain_w2pc(64)
        wait("sync", cPS[0])
        wait("sync", cPS[1])
        for c in range(8):
            dma("sync", cSL, pool[c][:, 0:2048],
                hb_dr[:, c * 128:(c + 1) * 128], transpose=True)
        wait("tensor", cSL)
        wait("tensor", cBR)

        g_strip = lambda m: pool[8 + m // 4][:, (m % 4) * 512:(m % 4) * 512 + 512]
        f2slots = [(PA[0][:, 0:512], PA[0][:, 512:1024]),
                   (PA[0][:, 1024:1536], PA[1][:, 0:512]),
                   (PA[1][:, 512:1024], PA[1][:, 1024:1536]),
                   (PD[:], PE[:])]

        f2_evac_done = None
        ev1_prev = None
        gu_free = [None, None]
        w1ring_free = [None, None, None, None]
        w2slot_free = [None] * 8
        slots4 = [PA[0][:, 0:512], PA[0][:, 512:1024], PA[0][:, 1024:1536], PA[1][:, 0:512]]
        for tokg in range(4):
            wait("sync", cV)
            wait("sync", cC)
            for ts4 in range(4):
                gt = tokg * 4 + ts4
                dma("sync", cH, xr[ts4][:], hf_dr[gt * 128:(gt + 1) * 128, :])
            h_loaded = cH.n
            for Mcg in range(16):
                w1t = pool[16 + Mcg % 4]
                wait_s("gpsimd", w1ring_free[Mcg % 4])
                for kc in range(8):
                    dma("gpsimd", cW1[Mcg % 4], w1t[:, kc * 256:(kc + 1) * 256],
                        W1_d[kc * 128:(kc + 1) * 128, Mcg * 256:(Mcg + 1) * 256])
                for Mh in range(2):
                    Mc = Mcg * 2 + Mh
                    ps = (PD if Mc % 2 == 0 else PE)
                    if Mh == 0:
                        wait("tensor", cW1[Mcg % 4])
                    wait_s("tensor", gu_free[Mc % 2])
                    def w1mm(t_, Mh=Mh, tokg=tokg, ps=ps, w1t=w1t):
                        for kc in range(8):
                            ins = t_.matmul(
                                ps[:], w1t[:, kc * 256 + Mh * 128: kc * 256 + Mh * 128 + 128],
                                pool[kc][:, tokg * 512:tokg * 512 + 512],
                                start=(kc == 0), stop=(kc == 7))
                        return ins
                    TE(w1mm)
                    w1_done = snap(cT)
                    wait_s("scalar", w1_done)
                    SE(lambda s, Mc=Mc, ps=ps: s.activation(
                        g_strip(Mc), ps[:], AF.Gelu_apprx_tanh, bias=c1p[:, Mc:Mc + 1]))
                    gu_free[Mc % 2] = snap(cC)
                w1ring_free[Mcg % 4] = snap(cT)
            for dh in range(2):
                for Mc in range(32):
                    w2t = wp[8 + Mc % 8]
                    wait_s("sync", w2slot_free[Mc % 8])
                    dma("sync", cHW[Mc % 8], w2t[:, 0:512],
                        w2b_dr[Mc * 128:(Mc + 1) * 128, dh * 512:dh * 512 + 512])
                    wait("tensor", cHW[Mc % 8])
                    if Mc == 0:
                        wait_s("tensor", gu_free[0])
                        wait_s("tensor", gu_free[1])
                        if dh == 0:
                            wait_s("tensor", ev1_prev)
                        else:
                            wait_s("tensor", ev0)
                    def w2mm(t_, Mc=Mc, w2t=w2t):
                        for ts4 in range(4):
                            ins = t_.matmul(slots4[ts4],
                                            g_strip(Mc)[:, ts4 * 128:(ts4 + 1) * 128],
                                            w2t[:, 0:512],
                                            start=(Mc == 0), stop=False)
                        return ins
                    TE(w2mm)
                    w2slot_free[Mc % 8] = snap(cT)
                def c2mm(t_, dh=dh):
                    for ts4 in range(4):
                        ins = t_.matmul(slots4[ts4], onesb[:],
                                        brow[0:1, dh * 512:dh * 512 + 512],
                                        start=False, stop=True)
                    return ins
                TE(c2mm)
                te_dh = snap(cT)
                wait_s("vector", te_dh)
                wait("vector", cH, h_loaded)
                for ts4 in range(4):
                    VE(lambda v, ts4=ts4, dh=dh: v.tensor_tensor(
                        xr[ts4][:, dh * 512:dh * 512 + 512], slots4[ts4],
                        xr[ts4][:, dh * 512:dh * 512 + 512], ALU.add), sp=(ts4 == 0))
                if dh == 0:
                    ev0 = snap(cV)
                else:
                    ev1_prev = snap(cV)
            for ts4 in range(4):
                gt = tokg * 4 + ts4
                rg = ts4 % 2
                wait("scalar", cSS)
                wait("vector", cSS)
                ln_stats(xr[ts4][:], sq=uS)
                wait("vector", cSS)
                VE(lambda v, ts4=ts4: v.tensor_scalar(
                    tmpS[:], xr[ts4][:], lnS[:, 2:3], lnS[:, 6:7], ALU.subtract, ALU.mult))
                VE(lambda v: v.tensor_tensor(tmpS[:], tmpS[:], repA[:], ALU.mult), sp=False)
                VE(lambda v: v.tensor_tensor(tmpS[:], tmpS[:], repB[:], ALU.add), sp=False)
                t_done = snap(cV)
                wait("sync", cV)
                dma("sync", cH, xr[ts4][:], x_in[gt * 128:(gt + 1) * 128, :])
                wait("vector", cH)
                VE(lambda v, ts4=ts4: v.tensor_tensor(uS[:], tmpS[:], xr[ts4][:], ALU.add), sp=False)
                wait("sync", cV)
                dma("sync", cSS, x1_dr[gt * 128:(gt + 1) * 128, :], uS[:])
                ln_stats(uS[:], sq=xr[ts4])
                wait("vector", cSS)
                VE(lambda v: v.tensor_scalar(
                    tmpS[:], uS[:], lnS[:, 2:3], lnS[:, 6:7], ALU.subtract, ALU.mult))
                for ch in range(2):
                    qd_chain(tmpS[:, ch * 512:ch * 512 + 512], stg[rg][:, ch * 512:ch * 512 + 512], 512)
                wait("sync", cV)
                dma("sync", cSS, xq_dr[gt * 128:(gt + 1) * 128, :], stg[rg][:])
            drain_wqd(6)
        barrier()

        drain_wqd(len(wqd_jobs))
        barrier()
        wait("sync", cCC)
        if dump_scratch:
            dma("sync", cSS, gg1_dbg, gg1_dr)
            dma("sync", cSS, gg2_dbg, gg2_dr)
        for c in range(8):
            dma("sync", cSL, pool[c][:, 0:2048],
                xq_dr[:, c * 128:(c + 1) * 128], transpose=True)
        wait("tensor", cSL)

        NCHW = [(0, 512), (512, 512), (1024, 512), (1536, 512), (2048, 512), (2560, 256)]
        gu_slots = [(PD[:], PE[:]), (PA[0][:, 0:512], PA[0][:, 512:1024])]
        gu5_free = [(None, None), (None, None)]
        sg_rd = None
        slab_mm_done = [None, None]
        slab_i = 0
        for tokg in range(4):
            for (n0, nw) in NCHW:
                p5 = slab_i % 2
                slab_i += 1
                wait_s("sync", slab_mm_done[p5])
                for kc in range(8):
                    dma("sync", cHW[p5], wp[kc][:, p5 * 512:p5 * 512 + nw],
                        wgq_dr[n0:n0 + nw, kc * 128:(kc + 1) * 128], transpose=True)
                    dma("sync", cHW[p5], wp[8 + kc][:, p5 * 512:p5 * 512 + nw],
                        wuq_dr[n0:n0 + nw, kc * 128:(kc + 1) * 128], transpose=True)
                wait("tensor", cHW[p5])
                for ti in range(4):
                    tokc = tokg * 4 + ti
                    pg, pu = gu_slots[ti % 2]
                    wait_s("tensor", gu5_free[ti % 2][0])
                    wait_s("tensor", gu5_free[ti % 2][1])
                    def gumm(t_, tokc=tokc, nw=nw, pg=pg, pu=pu, p5=p5):
                        for kc in range(8):
                            t_.matmul(pg[:, 0:nw],
                                      pool[kc][:, tokc * 128:(tokc + 1) * 128],
                                      wp[kc][:, p5 * 512:p5 * 512 + nw], start=(kc == 0), stop=(kc == 7))
                        for kc in range(8):
                            ins = t_.matmul(pu[:, 0:nw],
                                            pool[kc][:, tokc * 128:(tokc + 1) * 128],
                                            wp[8 + kc][:, p5 * 512:p5 * 512 + nw], start=(kc == 0), stop=(kc == 7))
                        return ins
                    TE(gumm)
                    gu_done = snap(cT)
                    wait_s("scalar", gu_done)
                    wait_s("scalar", sg_rd)
                    SE(lambda s, pg=pg, nw=nw: s.activation(sgS[:, 0:nw], pg[:, 0:nw], AF.Silu))
                    silu_done = snap(cC)
                    hrg = ti % 4
                    hqb = (hqstg[hrg] if hrg < 2 else stg[hrg - 2])
                    wait_s("vector", silu_done)
                    wait_s("vector", gu_done)
                    wait("vector", cHQ[hrg])
                    VE(lambda v, pu=pu, nw=nw: v.tensor_tensor(
                        tmpS[:, 0:nw], sgS[:, 0:nw], pu[:, 0:nw], ALU.mult), sp=False)
                    sg_rd = snap(cV)
                    gu5_free[ti % 2] = (silu_done, sg_rd)
                    qd_chain(tmpS[:, 0:nw], hqb[:, 0:nw], nw)
                    wait("sync", cV)
                    dma("sync", cHQ[hrg], hq_dr[tokc * 128:(tokc + 1) * 128, n0:n0 + nw],
                        hqb[:, 0:nw])
                slab_mm_done[p5] = snap(cT)
        barrier()

        def wdt_tile(kc):
            if kc < 16:
                return wp[kc][:, 0:1024]
            u = pool[(kc - 16) // 2]
            return u[:, ((kc - 16) % 2) * 1024:((kc - 16) % 2) * 1024 + 1024]

        for kc in range(22):
            dma("sync", cHW[4], wdt_tile(kc), wdq_dr[:, kc * 128:(kc + 1) * 128],
                transpose=True)
        wait("tensor", cHW[4])

        def hq_ring(st, kc):
            u = pool[8 + st * 6 + kc // 4]
            return u[:, (kc % 4) * 512:(kc % 4) * 512 + 512]

        dslots = [(PA[0][:, 0:512], PA[0][:, 512:1024]),
                  (PA[0][:, 1024:1536], PA[1][:, 0:512]),
                  (PA[1][:, 512:1024], PA[1][:, 1024:1536]),
                  (PD[:], PE[:])]
        dw_done = [None, None]
        out_ev = [None] * 4
        for tokg in range(4):
            st = tokg % 2
            wait_s("sync", dw_done[st])
            for kc in range(22):
                dma("sync", cHW[2 + st], hq_ring(st, kc),
                    hq_dr[tokg * 512:(tokg + 1) * 512, kc * 128:(kc + 1) * 128], transpose=True)
            wait("tensor", cHW[2 + st])
            for ti in range(4):
                wait_s("tensor", out_ev[ti])
                def dwmm(t_, st=st, ti=ti):
                    for kc in range(22):
                        for dh in range(2):
                            ins = t_.matmul(
                                dslots[ti][dh],
                                hq_ring(st, kc)[:, ti * 128:(ti + 1) * 128],
                                wdt_tile(kc)[:, dh * 512:dh * 512 + 512],
                                start=(kc == 0), stop=(kc == 21))
                    return ins
                TE(dwmm)
                dmm_done = snap(cT)
                gt = tokg * 4 + ti
                rg = ti % 2
                og = gt % 2
                ob = uS if og == 0 else tmpS
                wait("sync", cV)
                dma("sync", cH, xr[rg][:], x1_dr[gt * 128:(gt + 1) * 128, :])
                wait_s("vector", dmm_done)
                wait("vector", cH)
                wait("vector", cOS[og])
                VE(lambda v, ti=ti, rg=rg, ob=ob: v.tensor_tensor(
                    ob[:, 0:512], dslots[ti][0], xr[rg][:, 0:512], ALU.add))
                VE(lambda v, ti=ti, rg=rg, ob=ob: v.tensor_tensor(
                    ob[:, 512:1024], dslots[ti][1], xr[rg][:, 512:1024], ALU.add), sp=False)
                out_ev[ti] = snap(cV)
                wait("sync", cV)
                dma("sync", cOS[og], out_d[gt * 128:(gt + 1) * 128, :], ob[:])
            dw_done[st] = snap(cT)
        wait("sync", cSS)
        wait("sync", cOS[0])
        wait("sync", cOS[1])

        with nc.Block() as block:
            @block.sync
            def _(e):
                for eng, fn in PROG:
                    if eng == "sync":
                        fn(e)

            @block.gpsimd
            def _(e):
                for eng, fn in PROG:
                    if eng == "gpsimd":
                        fn(e)

            @block.tensor
            def _(e):
                for eng, fn in PROG:
                    if eng == "tensor":
                        fn(e)

            @block.vector
            def _(e):
                for eng, fn in PROG:
                    if eng == "vector":
                        fn(e)

            @block.scalar
            def _(e):
                for eng, fn in PROG:
                    if eng == "scalar":
                        fn(e)
    return nc


def _masks():
    r = np.arange(128)[:, None]
    q = np.arange(256)[None, :]
    m0 = (q <= r).astype(ml_dtypes.bfloat16)
    m1 = (q <= r + 128).astype(ml_dtypes.bfloat16)
    m4 = (q >= r).astype(ml_dtypes.bfloat16)
    m5 = (q >= r + 128).astype(ml_dtypes.bfloat16)
    return [m0, m1, m4, m5]


def _in_maps(inputs):
    masks = _masks()
    ones = np.ones((1, 128), dtype=ml_dtypes.bfloat16)
    base = {
        "Wq": inputs["L0_attn_Wq"], "bq": inputs["L0_attn_bq"],
        "Wk": inputs["L0_attn_Wk"], "bk": inputs["L0_attn_bk"],
        "Wv": inputs["L0_attn_Wv"], "bv": inputs["L0_attn_bv"],
        "Wo": inputs["L0_attn_Wo"], "bo": inputs["L0_attn_bo"],
        "g1": inputs["L0_attn_g1"], "b1": inputs["L0_attn_b1"],
        "W1": inputs["L0_attn_W1"], "c1": inputs["L0_attn_c1"],
        "W2": inputs["L0_attn_W2"], "c2": inputs["L0_attn_c2"],
        "g2": inputs["L0_attn_g2"], "b2": inputs["L0_attn_b2"],
        "ones_row": ones,
    }
    base = {k: np.ascontiguousarray(np.asarray(v, dtype=np.float32))
            if k not in ("ones_row",) else v for k, v in base.items()}
    for j in range(4):
        base[f"mask{j}"] = masks[j]
    x = np.ascontiguousarray(np.asarray(inputs["L0_attn_x"], dtype=np.float32))
    wggu = np.concatenate([np.asarray(inputs["L0_ffn_w_gate"], dtype=np.float32),
                           np.asarray(inputs["L0_ffn_w_up"], dtype=np.float32)], axis=0)
    wd = np.asarray(inputs["L0_ffn_w_down"], dtype=np.float32)
    in_maps = []
    for c in range(8):
        m = dict(base)
        m["x"] = np.ascontiguousarray(x[c])
        m["wggu_sl"] = np.ascontiguousarray(wggu[c * 704:(c + 1) * 704])
        m["wd_sl"] = np.ascontiguousarray(wd[c * 128:(c + 1) * 128])
        in_maps.append(m)
    return in_maps


def kernel(**inputs):
    nc = build()
    res = run_bass_kernel_spmd(nc, _in_maps(inputs), core_ids=list(range(8)))
    return np.stack([np.asarray(r["out"]) for r in res.results], axis=0)
